# version 03
# baseline (speedup 1.0000x reference)
import time
import numpy as np
import ml_dtypes
import concourse.bass as bass
import concourse.mybir as mybir
from concourse.bass_utils import run_bass_kernel_spmd

F32 = mybir.dt.float32
BF16 = mybir.dt.bfloat16
U32 = mybir.dt.uint32
AF = mybir.ActivationFunctionType
ALU = mybir.AluOpType
AX = mybir.AxisListType
SP = mybir.EngineType.SP

DM = 2048
TOK = 2048
SLAB = 1024
KT = 16
NEXP = 6
LN_EPS = 1e-5
L2_EPS = 1e-12
BIG = 1e30
DEBUG = False
STAGES = 3


class Ctx:
    ENG = ["tensor", "scalar", "vector", "gpsimd", "sync"]

    def __init__(self, nc):
        self.nc = nc
        self.sem = {}
        self.cnt = {}
        self.waited = {}
        for name in ["w0", "w1", "v0", "v1", "xb0", "xb1", "oh0", "oh1",
                     "sd", "xr", "xi", "kv0", "kv1", "kvw", "mm", "sc", "vv", "p1", "p2"]:
            self.sem[name] = nc.alloc_semaphore(name="s_" + name)
            self.cnt[name] = 0
        self.stream = {e: [] for e in self.ENG}

    def wait(self, ename, sname, target):
        if target is None or target <= 0:
            return
        key = (ename, sname)
        if self.waited.get(key, 0) >= target:
            return
        self.waited[key] = target
        sem = self.sem[sname]
        self.stream[ename].append(lambda eng, s=sem, t=target: eng.wait_ge(s, t))

    def wait_pair(self, ename, pair):
        if pair is not None:
            self.wait(ename, pair[0], pair[1])

    def op(self, ename, fn, inc=None):
        if inc is None:
            self.stream[ename].append(fn)
            return None
        sem = self.sem[inc]
        self.cnt[inc] += 1
        tgt = self.cnt[inc]
        def wrapped(eng, fn=fn, sem=sem):
            fn(eng).then_inc(sem, 1)
        self.stream[ename].append(wrapped)
        return tgt

    def dma(self, ename, sname, out, in_):
        sem = self.sem[sname]
        self.cnt[sname] += 16
        tgt = self.cnt[sname]
        def fn(eng, out=out, in_=in_, sem=sem):
            eng.dma_start(out=out, in_=in_).then_inc(sem, 16)
        self.stream[ename].append(fn)
        return tgt

    def barrier_waits(self, ename):
        for s in ["mm", "sc", "vv"]:
            self.wait(ename, s, self.cnt[s])

    def run(self, block):
        decos = {"tensor": block.tensor, "scalar": block.scalar,
                 "vector": block.vector, "gpsimd": block.gpsimd, "sync": block.sync}
        for ename in self.ENG:
            ops = self.stream[ename]
            if not ops:
                continue
            def body(eng, ops=ops):
                for f in ops:
                    f(eng)
            decos[ename](body)
        self.stream = {e: [] for e in self.ENG}


class PsumRing:
    def __init__(self, ctx, banks):
        self.ctx = ctx
        self.banks = list(banks)
        self.i = 0
        self.free_at = {b: None for b in self.banks}

    def take(self, engine="tensor"):
        b = self.banks[self.i % len(self.banks)]
        self.i += 1
        fa = self.free_at[b]
        if fa is not None:
            self.ctx.wait(engine, fa[0], fa[1])
        return b

    def done(self, b, sname, target):
        self.free_at[b] = (sname, target)


def build_nc(stages=STAGES, debug=DEBUG, self_pair=False):
    nc = bass.Bass("TRN2", target_bir_lowering=False)
    c = Ctx(nc)

    def I(name, shape, dt=F32):
        return nc.dram_tensor(name, shape, dt, kind="ExternalInput").ap()

    x0 = I("x0", [DM, TOK])
    wq = I("wq", [DM, 1024]); wk = I("wk", [DM, 1024])
    wv = I("wv", [DM, DM]); wgt = I("wgt", [DM, DM]); wo = I("wo", [DM, DM])
    ls_iaw = I("ls_iaw", [1, 16]); ls_hws = I("ls_hws", [1, 16])
    wgp = I("wgp", [128, KT, 2, 8]); bgr = I("bgr", [1, 16])
    wup = I("wup", [NEXP, DM, 1024]); wsw = I("wsw", [NEXP, 1024, 1024])
    wdn = I("wdn", [NEXP, 1024, DM])
    wsu = I("wsu", [DM, DM]); wsa = I("wsa", [1024, 1024]); wsd = I("wsd", [DM, DM])
    bupp = I("bupp", [128, NEXP, 8]); bswp = I("bswp", [128, NEXP, 8])
    bdn2 = I("bdn2", [NEXP, DM]); bsup = I("bsup", [128, 16]); bsap = I("bsap", [128, 8])
    bsdp = I("bsdp", [128, 16])
    csel64 = I("csel64", [128, 2]); csel64T = I("csel64T", [2, 128])
    cones1 = I("cones1", [1, 128]); cones1f = I("cones1f", [1, 128])
    conescol = I("conescol", [128, 1]); cident = I("cident", [128, 128])
    nonce_in = I("nonce", [1, 1], U32)
    ceps = I("ceps", [1, 1])

    out_d = nc.dram_tensor("out", [DM, TOK], F32, kind="ExternalOutput").ap()
    dbg1 = dbg2 = None
    if debug:
        dbg1 = nc.dram_tensor("dbg1", [DM, TOK], F32, kind="ExternalOutput").ap()
        dbg2 = nc.dram_tensor("dbg2", [DM, TOK], F32, kind="ExternalOutput").ap()

    xres1 = nc.dram_tensor("xres1", [DM, TOK], F32, kind="Internal").ap()
    xres2 = nc.dram_tensor("xres2", [DM, TOK], F32, kind="Internal").ap()
    g2d = nc.dram_tensor("g2d", [16, 128, TOK], BF16, kind="Internal").ap()
    rdram = nc.dram_tensor("rdram", [1, 16], F32, kind="Internal").ap()
    statd = nc.dram_tensor("statd", [3, 2, 4, 512], F32, kind="Internal").ap()
    KSH = nc.dram_tensor("KSH", [2, 8, 128, TOK], BF16, kind="Internal", addr_space="Shared").ap()
    VSH = nc.dram_tensor("VSH", [2, 16, 128, DM], BF16, kind="Internal", addr_space="Shared").ap()
    RNSH = nc.dram_tensor("RNSH", [2, 16, 128, 16], F32, kind="Internal", addr_space="Shared").ap()
    KLOC = nc.dram_tensor("KLOC", [8, 128, TOK], BF16, kind="Internal").ap()
    VLOC = nc.dram_tensor("VLOC", [16, 128, DM], BF16, kind="Internal").ap()
    RLOC = nc.dram_tensor("RLOC", [16, 128, 16], F32, kind="Internal").ap()
    FKV = nc.dram_tensor("FKV", [2, 1, 8], U32, kind="Internal", addr_space="Shared").ap()
    FDN = nc.dram_tensor("FDN", [2, 1, 8], U32, kind="Internal", addr_space="Shared").ap()

    sb = lambda n, s, d: nc.alloc_sbuf_tensor(n, s, d)
    arena = sb("arena", [128, 128 * 1024], mybir.dt.uint8)
    abase = nc.lookup_mloc(arena).addr

    def ar(name, shape, dtype, off_kb):
        return nc.alloc_sbuf_tensor_at(name, shape, dtype, offset=int(abase + off_kb * 1024))

    a_b = ar("a_b", [128, KT, SLAB], BF16, 0)
    acc = ar("acc", [128, KT, SLAB], F32, 32)
    B0 = ar("B0", [128, 8, SLAB], BF16, 96)
    B1 = ar("B1", [128, 8, SLAB], BF16, 112)
    Bd = [B0, B1]
    sh_h = ar("sh_h", [128, KT, 512], BF16, 96)
    h2sh = ar("h2sh", [128, 8, 512], BF16, 112)
    a2 = ar("a2", [128, KT, TOK], BF16, 0)
    wvp = ar("wvp", [128, 2, KT, 512], BF16, 64)
    q_sb = ar("q_sb", [128, 8, TOK], BF16, 64)
    kbuf = ar("kbuf", [128, 2, 2, TOK], BF16, 0)
    vbuf = ar("vbuf", [128, 2, 2, 32, 128], BF16, 16)
    osc = ar("osc", [128, 16, 512], BF16, 48)
    rt = ar("rt", [128, 2, 2, 512], F32, 96)
    at = ar("at", [128, 2, 4, 512], BF16, 104)
    af = ar("af", [128, 2, 512], F32, 112)
    g2b = ar("g2b", [128, 2, 2, 512], BF16, 116)
    srn = ar("srn", [128, 32, 16], F32, 120)
    srn2 = srn
    sqe = ar("sqe", [128, 512], BF16, 122)
    aft = ar("aft", [128, 512], F32, 123)
    vsacc = ar("vsacc", [128, 16, 16], F32, 105)
    vps = ar("vps", [128, 512], F32, 102)
    vnb = ar("vnb", [128, 512], BF16, 106)
    prA = ar("prA", [128, 512], F32, 96)
    prB = ar("prB", [128, 512], BF16, 98)
    prC = ar("prC", [128, 512], F32, 99)
    prD = ar("prD", [128, 512], BF16, 101)

    wpool = sb("wpool", [128, 2, KT * 128], BF16)
    xbuf = sb("xbuf", [128, 2, 512], F32)
    t1b = sb("t1b", [128, 512], F32)
    outh = sb("outh", [128, 2, 512], F32)
    xb16 = sb("xb16", [128, 2, 512], BF16)
    sqx = sb("sqx", [128, 2, 512], BF16)
    gate_f = sb("gate_f", [128, 64], F32)
    tk = {n: sb("tk_" + n, [128, 64], F32) for n in ["t8", "g1", "g2t", "k8", "d8", "e8", "em8"]}
    w_tm = sb("w_tm", [128, 64], F32)
    tkr = {n: sb("tkr_" + n, [128, 8], F32) for n in ["m1", "m2", "m3", "nmx", "gz", "gr"]}
    w_fm = sb("w_fm", [8, 8, 128], BF16)
    w_bc = sb("w_bc", [128, SLAB], BF16)
    wrow = sb("wrow", [1, SLAB], BF16)
    bgb = sb("bgb", [128, 16], F32)
    wg_sb = sb("wg_sb", [128, KT, 2, 8], BF16)
    bup_sb = sb("bup_sb", [128, NEXP, 8], F32)
    bsw_sb = sb("bsw_sb", [128, NEXP, 8], F32)
    bdn_sb = sb("bdn_sb", [NEXP, DM], BF16)
    bsu_sb = sb("bsu_sb", [128, 16], F32)
    bsa_sb = sb("bsa_sb", [128, 8], F32)
    bsd_sb = sb("bsd_sb", [128, 16], F32)
    sel64 = sb("sel64", [128, 2], BF16)
    sel64T = sb("sel64T", [2, 128], BF16)
    ones1 = sb("ones1", [1, 128], BF16)
    ones1f = sb("ones1f", [1, 128], F32)
    onescol = sb("onescol", [128, 1], BF16)
    ident_f = sb("ident_f", [128, 128], F32)
    rra = sb("rra", [1, 16], F32)
    rrb = sb("rrb", [1, 16], F32)
    rrc = sb("rrc", [1, 16], F32)
    r_cols = sb("r_cols", [2, 8], F32)
    ehws = sb("ehws", [1, 16], F32)
    b_all = sb("b_all", [128, 16], F32)
    rowS = sb("rowS", [1, 512], F32)
    rowS2 = sb("rowS2", [1, 512], F32)
    rowU = sb("rowU", [1, 512], F32)
    rowB = sb("rowB", [1, 512], F32)
    rowT = sb("rowT", [1, 512], F32)
    rowT2 = sb("rowT2", [1, 512], F32)
    ubc_sb = sb("ubc_sb", [128, 2, 512], F32)
    bbc_sb = sb("bbc_sb", [128, 2, 512], F32)
    arwA = sb("arwA", [1, 512], F32)
    arwB = sb("arwB", [1, 512], F32)
    arwC = sb("arwC", [1, 512], F32)
    rnk = sb("rnk", [2, 512], F32)
    rnk2 = sb("rnk2", [2, 512], F32)
    rnkb = sb("rnkb", [2, 512], BF16)
    epst = sb("epst", [1, 1], F32)
    nonb = sb("nonb", [1, 8], U32)
    fsb = sb("fsb", [1, 8], U32)

    ps = nc.alloc_psum_tensor("ps", [128, 8, 512], F32)

    wslot_free = {}
    wload_counter = {}

    def wload(src_ap, slot_group, dst_views, nkt=None):
        idx = wload_counter.setdefault(slot_group, 0)
        wload_counter[slot_group] = idx + 1
        slot = idx % 2
        key = (slot_group, slot)
        wm = wslot_free.get(key)
        if wm is not None:
            c.wait("gpsimd", wm[0], wm[1])
        view = dst_views[slot]
        if nkt is not None and view.shape[1] != nkt:
            view = view[:, 0:nkt, :]
        sname = ("w" if slot_group == "w" else "v") + str(slot)
        tgt = c.dma("gpsimd", sname, view, src_ap)
        return view, (sname, tgt), key

    def wfree(key, mm_target):
        wslot_free[key] = ("mm", mm_target)

    wpv = [wpool[:, s, :].rearrange("p (kt m) -> p kt m", m=128) for s in range(2)]
    wvpv = [wvp[:, s, :, :] for s in range(2)]

    def vop(fn, sync=True):
        t = c.op("vector", fn, inc="vv")
        if sync:
            c.wait("vector", "vv", t)
        return t

    def sop(fn, sync=True):
        t = c.op("scalar", fn, inc="sc")
        if sync:
            c.wait("scalar", "sc", t)
        return t

    def mmop(fn):
        return c.op("tensor", fn, inc="mm")

    up_ring = PsumRing(c, [0, 1])
    sw_ring = PsumRing(c, [2, 3])
    dn_ring = PsumRing(c, [4, 5])
    misc_ring = PsumRing(c, [6, 7])

    def group(ring, mms, extra_wait=(), out_cols=None, out_rows=128):
        b = ring.take()
        for sname, t in extra_wait:
            c.wait("tensor", sname, t)
        n = len(mms)
        last = None
        for i, (lh, rh) in enumerate(mms):
            o = ps[0:out_rows, b, :] if out_cols is None else ps[0:out_rows, b, out_cols[0]:out_cols[0] + out_cols[1]]
            def f(eng, lh=lh, rh=rh, i=i, n=n, o=o):
                return eng.matmul(o, lh, rh, start=(i == 0), stop=(i == n - 1))
            if i == n - 1:
                last = mmop(f)
            else:
                c.op("tensor", f)
        return b, last

    def seg8(tns):
        return tns[:, :].rearrange("p (a b) -> p a b", b=8)

    def colb8(tns):
        v = tns[:, :].rearrange("p (a b) -> p a b", b=1)
        return bass.AP(v.tensor, v.offset, [list(v.ap[0]), list(v.ap[1]), [0, 8]])

    nc.const_aps.register(F32, float(LN_EPS), epst[0:1, 0:1])
    pid = nc.partition_id(engines=[SP])
    par = pid & 1
    opar = par if self_pair else ((pid + 1) & 1)

    war = {}

    def col_block(w2d, col0):
        return w2d[:, col0:col0 + 128].rearrange("(kt p) m -> p kt m", p=128)

    def emit_init():
        for dst, src in [(bup_sb[:, :, :], bupp), (bsw_sb[:, :, :], bswp),
                         (bsu_sb[:, :], bsup), (bsa_sb[:, :], bsap), (bsd_sb[:, :], bsdp),
                         (rra[:, :], ls_iaw), (rrb[:, :], ls_hws),
                         (nonb[0:1, 0:1], nonce_in), (ones1f[:, :], cones1f),
                         (epst[:, :], ceps),
                         (ident_f[:, :], cident), (rowT[0:1, 0:16], bgr)]:
            c.dma("sync", "xi", dst, src)
        tload = c.cnt["xi"]
        c.wait("sync", "xi", tload)
        for dst, src in [(sel64[:, :], csel64), (sel64T[:, :], csel64T),
                         (ones1[:, :], cones1), (onescol[:, :], conescol),
                         (wg_sb[:, :, :, :], wgp), (bdn_sb[:, :], bdn2)]:
            c.dma("gpsimd", "w0", dst, src)
        tcast = c.cnt["w0"]
        c.wait("gpsimd", "w0", tcast)
        for e in ["scalar", "vector", "tensor"]:
            c.wait(e, "xi", tload)
        c.wait("tensor", "w0", tcast)
        t = sop(lambda e: e.activation(rrc[:, :], rra[:, :], AF.Exp))
        c.wait("vector", "sc", t)
        vop(lambda e: e.tensor_scalar(rrc[:, :], rrc[:, :], 1.0, None, ALU.add))
        vop(lambda e: e.reciprocal(rra[:, :], rrc[:, :]))
        tv = vop(lambda e: e.tensor_scalar(rrc[:, :], rra[:, :], -1.0, 1.0, ALU.mult, ALU.add))
        c.wait("scalar", "vv", tv)
        sop(lambda e: e.activation(ehws[:, :], rrb[:, :], AF.Exp))
        c.wait("sync", "vv", tv)
        c.dma("sync", "xi", rdram, rra[:, :])
        c.wait("sync", "xi", c.cnt["xi"])
        c.cnt["xi"] += 16; tr = c.cnt["xi"]
        def _rl(eng, sem=c.sem["xi"]):
            eng.dma_start(out=r_cols[:, :], in_=rdram.rearrange("a (t j) -> (a j) t", j=2),
                          allow_slow_non_contiguous=True).then_inc(sem, 16)
        c.stream["sync"].append(_rl)
        c.wait("sync", "xi", tr)
        c.wait("tensor", "xi", tr)
        b, last = group(misc_ring, [(ones1f[:, :], rrc[:, :])], extra_wait=[("vv", tv)], out_cols=(0, 16))
        c.wait("vector", "mm", last)
        t = vop(lambda e, b=b: e.tensor_copy(b_all[:, :], ps[:, b, 0:16]))
        misc_ring.done(b, "vv", t)
        b, last = group(misc_ring, [(ones1f[:, :], rowT[0:1, 0:16])], out_cols=(0, 16))
        c.wait("vector", "mm", last)
        t = vop(lambda e, b=b: e.tensor_copy(bgb[:, :], ps[:, b, 0:16]))
        misc_ring.done(b, "vv", t)

    def ln_rows(ln_idx, tcg):
        c.wait("sync", "vv", c.cnt["vv"])
        c.wait("sync", "mm", c.cnt["mm"])
        c.wait("sync", "sd", c.cnt["sd"])
        c.dma("sync", "xr", rowS[:, :], statd[ln_idx, 0, tcg, :][None, :])
        t2 = c.dma("sync", "xr", rowS2[:, :], statd[ln_idx, 1, tcg, :][None, :])
        c.wait("sync", "xr", t2)
        c.wait("vector", "xr", t2)
        vop(lambda e: e.tensor_scalar_mul(rowS[:, :], rowS[:, :], 1.0 / DM))
        vop(lambda e: e.tensor_scalar_mul(rowS2[:, :], rowS2[:, :], 1.0 / DM))
        vop(lambda e: e.tensor_mul(rowT2[:, :], rowS[:, :], rowS[:, :]))
        t = vop(lambda e: e.tensor_sub(rowS2[:, :], rowS2[:, :], rowT2[:, :]))
        c.wait("scalar", "vv", t)
        t = sop(lambda e: e.activation(rowT2[:, :], rowS2[:, :], AF.Sqrt, bias=float(LN_EPS)))
        c.wait("vector", "sc", t)
        vop(lambda e: e.reciprocal(rowU[:, :], rowT2[:, :]))
        t = vop(lambda e: e.scalar_tensor_tensor(rowB[:, :], rowS[:, :], -1.0, rowU[:, :], ALU.mult, ALU.mult))
        bu, _ = group(misc_ring, [(ones1f[:, :], rowU[:, :])], extra_wait=[("vv", t)])
        bb, lastb = group(misc_ring, [(ones1f[:, :], rowB[:, :])])
        return bu, bb, lastb

    def ln_rows_sbuf(ln_idx, tcg, tc2):
        bu, bb, lastb = ln_rows(ln_idx, tcg)
        c.wait("vector", "mm", lastb)
        c.wait_pair("vector", war.get(("ubc_sb", tc2)))
        vop(lambda e, bu=bu, tc2=tc2: e.tensor_copy(ubc_sb[:, tc2, :], ps[:, bu, :]))
        t = vop(lambda e, bb=bb, tc2=tc2: e.tensor_copy(bbc_sb[:, tc2, :], ps[:, bb, :]))
        misc_ring.done(bu, "vv", t)
        misc_ring.done(bb, "vv", t)
        return t

    def xload(src_ap, slot):
        c.wait_pair("sync", war.get(("xbuf", slot)))
        return ("xb%d" % slot, c.dma("sync", "xb%d" % slot, xbuf[:, slot, :], src_ap))

    def cast_pair(slot, src_tile, mm_guard_key):
        c.wait_pair("scalar", war.get((mm_guard_key, slot)))
        sop(lambda e, slot=slot, src_tile=src_tile: e.copy(xb16[:, slot, :], src_tile))
        return sop(lambda e, slot=slot, src_tile=src_tile: e.activation(sqx[:, slot, :], src_tile, AF.Square))

    def emit_stats(src_d, ln_idx):
        for tcg in range(4):
            bs = dn_ring.take()
            bs2 = dn_ring.take()
            l2 = None
            for mt in range(KT):
                slot = mt % 2
                tl = xload(src_d[mt * 128:(mt + 1) * 128, tcg * 512:(tcg + 1) * 512], slot)
                c.wait_pair("scalar", tl)
                ts2 = cast_pair(slot, xbuf[:, slot, :], "statmm")
                war[("xbuf", slot)] = ("sc", ts2)
                c.wait("tensor", "sc", ts2)
                def f1(eng, slot=slot, mt=mt, bs=bs):
                    return eng.matmul(ps[0:1, bs, :], onescol[:, :], xb16[:, slot, :],
                                      start=(mt == 0), stop=(mt == KT - 1))
                def f2(eng, slot=slot, mt=mt, bs2=bs2):
                    return eng.matmul(ps[0:1, bs2, :], onescol[:, :], sqx[:, slot, :],
                                      start=(mt == 0), stop=(mt == KT - 1))
                mmop(f1)
                tf2 = mmop(f2)
                if mt == KT - 1:
                    l2 = tf2
                war[("statmm", slot)] = ("mm", tf2)
            c.wait("vector", "mm", l2)
            c.wait_pair("vector", war.get(("rowS",)))
            v1 = vop(lambda e, bs=bs: e.tensor_copy(rowS[:, :], ps[0:1, bs, :]))
            v2 = vop(lambda e, bs2=bs2: e.tensor_copy(rowS2[:, :], ps[0:1, bs2, :]))
            dn_ring.done(bs, "vv", v1)
            dn_ring.done(bs2, "vv", v2)
            c.wait("sync", "vv", v2)
            c.dma("sync", "sd", statd[ln_idx, 0, tcg, :][None, :], rowS[:, :])
            war[("rowS",)] = ("sd", c.dma("sync", "sd", statd[ln_idx, 1, tcg, :][None, :], rowS2[:, :]))

    def emit_ln_apply(src_d, ln_idx, base_tcg, ntc, dest, dest_guard=True):
        if dest_guard:
            c.barrier_waits("vector")
        for tc in range(ntc):
            tcg = base_tcg + tc
            bu, bb, lastb = ln_rows(ln_idx, tcg)
            free_tgt = None
            for mt in range(KT):
                slot = mt % 2
                tl = xload(src_d[mt * 128:(mt + 1) * 128, tcg * 512:(tcg + 1) * 512], slot)
                c.wait_pair("vector", tl)
                c.wait("vector", "mm", lastb)
                vop(lambda e, slot=slot, bu=bu: e.tensor_mul(t1b[:, :], xbuf[:, slot, :], ps[:, bu, :]))
                free_tgt = vop(lambda e, bb=bb, mt=mt, tc=tc, dest=dest: e.tensor_add(
                    dest[:, mt, tc * 512:(tc + 1) * 512], t1b[:, :], ps[:, bb, :]))
                war[("xbuf", slot)] = ("vv", free_tgt)
            misc_ring.done(bu, "vv", free_tgt)
            misc_ring.done(bb, "vv", free_tgt)
            war[("a_ready",)] = ("vv", free_tgt)

    def emit_gate(j):
        c.wait("vector", "mm", c.cnt["mm"])
        c.wait_pair("tensor", war.get(("a_ready",)))
        bg_ = misc_ring.take()
        last = None
        for tm in range(8):
            for k in range(KT):
                def f(eng, tm=tm, k=k, j=j, bg_=bg_):
                    return eng.matmul(ps[:, bg_, tm * 8:(tm + 1) * 8],
                                      a_b[:, k, tm * 128:(tm + 1) * 128],
                                      wg_sb[:, k, j, :], start=(k == 0), stop=(k == KT - 1))
                if tm == 7 and k == KT - 1:
                    last = mmop(f)
                else:
                    c.op("tensor", f)
        c.wait("vector", "mm", last)
        ev = vop(lambda e, bg_=bg_: e.tensor_copy(gate_f[:, :], ps[:, bg_, 0:64]))
        misc_ring.done(bg_, "vv", ev)
        bgv0 = bgb[:, j * 8:(j + 1) * 8].rearrange("p (a b) -> p a b", a=1)
        bgv = bass.AP(bgv0.tensor, bgv0.offset, [list(bgv0.ap[0]), [0, 8], list(bgv0.ap[2])])
        vop(lambda e: e.tensor_tensor(seg8(gate_f), seg8(gate_f), bgv, ALU.add))
        vop(lambda e: e.tensor_reduce(tkr["m1"][:, :].rearrange("p (a b) -> p a b", b=1), seg8(gate_f), AX.X, ALU.min))
        vop(lambda e: e.tensor_tensor(seg8(tk["t8"]), seg8(gate_f), colb8(tkr["m1"]), ALU.is_equal))
        vop(lambda e: e.scalar_tensor_tensor(tk["g1"][:, :], tk["t8"][:, :], BIG, gate_f[:, :], ALU.mult, ALU.add))
        vop(lambda e: e.tensor_reduce(tkr["m2"][:, :].rearrange("p (a b) -> p a b", b=1), seg8(tk["g1"]), AX.X, ALU.min))
        vop(lambda e: e.tensor_tensor(seg8(tk["t8"]), seg8(tk["g1"]), colb8(tkr["m2"]), ALU.is_equal))
        vop(lambda e: e.scalar_tensor_tensor(tk["g2t"][:, :], tk["t8"][:, :], BIG, tk["g1"][:, :], ALU.mult, ALU.add))
        vop(lambda e: e.tensor_reduce(tkr["m3"][:, :].rearrange("p (a b) -> p a b", b=1), seg8(tk["g2t"]), AX.X, ALU.min))
        vop(lambda e: e.tensor_tensor(seg8(tk["k8"]), seg8(gate_f), colb8(tkr["m3"]), ALU.is_ge))
        vop(lambda e: e.tensor_reduce(tkr["nmx"][:, :].rearrange("p (a b) -> p a b", b=1), seg8(gate_f), AX.X, ALU.max, negate=True))
        tdv = vop(lambda e: e.tensor_tensor(seg8(tk["d8"]), seg8(gate_f), colb8(tkr["nmx"]), ALU.add))
        c.wait("scalar", "vv", tdv)
        te = sop(lambda e: e.activation(tk["e8"][:, :], tk["d8"][:, :], AF.Exp))
        c.wait("vector", "sc", te)
        vop(lambda e: e.tensor_mul(tk["em8"][:, :], tk["e8"][:, :], tk["k8"][:, :]))
        vop(lambda e: e.tensor_reduce(tkr["gz"][:, :].rearrange("p (a b) -> p a b", b=1), seg8(tk["em8"]), AX.X, ALU.add))
        vop(lambda e: e.reciprocal(tkr["gr"][:, :], tkr["gz"][:, :]))
        wt = vop(lambda e: e.tensor_tensor(seg8(w_tm), seg8(tk["em8"]), colb8(tkr["gr"]), ALU.mult))
        c.wait("tensor", "vv", wt)
        ev = None
        for tm in range(8):
            bt = misc_ring.take()
            def ft(eng, tm=tm, bt=bt):
                return eng.transpose(ps[0:8, bt, 0:128], w_tm[:, tm * 8:(tm + 1) * 8], ident_f[:, :])
            lastt = mmop(ft)
            c.wait("vector", "mm", lastt)
            ev = vop(lambda e, tm=tm, bt=bt: e.tensor_copy(w_fm[:, tm, :], ps[0:8, bt, 0:128]))
            misc_ring.done(bt, "vv", ev)
        war[("w_fm",)] = ("vv", ev)

    down_par = {0: None, 1: None}

    def emit_shared(slab):
        for half in range(2):
            tof = half * 512
            sh_ready = None
            for ms in range(KT):
                v, wt, key = wload(col_block(wsu, ms * 128), "w", wpv)
                c.wait_pair("tensor", wt)
                b, last = group(up_ring, [(v[:, k, :], a_b[:, k, tof:tof + 512]) for k in range(KT)])
                wfree(key, last)
                c.wait("scalar", "mm", last)
                c.wait_pair("scalar", ("mm", down_par[0][1]) if down_par[0] else None)
                t = sop(lambda e, ms=ms, b=b: e.activation(
                    sh_h[:, ms, :], ps[:, b, :], AF.Identity, bias=bsu_sb[:, ms:ms + 1]))
                up_ring.done(b, "sc", t)
                sh_ready = t
            sa_last = None
            sd_last = None
            for s2 in range(2):
                h2_ready = None
                for ma in range(8):
                    v, wt, key = wload(col_block(wsa, ma * 128), "w", wpv, nkt=8)
                    c.wait_pair("tensor", wt)
                    b, last = group(sw_ring, [(v[:, k, :], sh_h[:, s2 * 8 + k, :]) for k in range(8)],
                                    extra_wait=[("sc", sh_ready)])
                    wfree(key, last)
                    sa_last = last
                    bsv = misc_ring.take(engine="scalar")
                    c.wait("scalar", "mm", last)
                    t = sop(lambda e, b=b, bsv=bsv, ma=ma: e.activation(
                        ps[:, bsv, :], ps[:, b, :], AF.Silu, bias=bsa_sb[:, ma:ma + 1]))
                    sw_ring.done(b, "sc", t)
                    c.wait("vector", "sc", t)
                    c.wait_pair("vector", ("mm", down_par[1][1]) if down_par[1] else None)
                    t = vop(lambda e, bsv=bsv, ma=ma, s2=s2: e.tensor_mul(
                        h2sh[:, ma, :], ps[:, bsv, :], sh_h[:, s2 * 8 + ma, :]))
                    misc_ring.done(bsv, "vv", t)
                    h2_ready = t
                for md in range(KT):
                    v, wt, key = wload(col_block(wsd[s2 * 1024:(s2 + 1) * 1024, :], md * 128), "w", wpv, nkt=8)
                    c.wait_pair("tensor", wt)
                    b, last = group(dn_ring, [(v[:, k, :], h2sh[:, k, :]) for k in range(8)],
                                    extra_wait=[("vv", h2_ready)])
                    wfree(key, last)
                    sd_last = last
                    c.wait("vector", "mm", last)
                    if s2 == 0:
                        t = vop(lambda e, md=md, b=b, tof=tof: e.tensor_scalar_add(
                            acc[:, md, tof:tof + 512], ps[:, b, :], bsd_sb[:, md:md + 1]))
                    else:
                        t = vop(lambda e, md=md, b=b, tof=tof: e.tensor_add(
                            acc[:, md, tof:tof + 512], acc[:, md, tof:tof + 512], ps[:, b, :]))
                    dn_ring.done(b, "vv", t)
            down_par[0] = ("mm", sa_last)
            down_par[1] = ("mm", sd_last)

    def emit_expert(n, tail_cb=None):
        src = Bd[n % 2]
        dst = Bd[(n + 1) % 2]
        c.wait_pair("sync", war.get(("w_fm",)))
        c.wait_pair("sync", war.get(("wrow",)))
        trw = c.dma("sync", "xr", wrow[0:1, :],
                    w_fm[n:n + 1, :, :].rearrange("a tm m -> a (tm m)"))
        c.wait("sync", "xr", trw)
        for half in range(2):
            bwc = misc_ring.take()
            c.wait("tensor", "xr", trw)
            def fbc(eng, bwc=bwc, half=half):
                return eng.matmul(ps[:, bwc, :], ones1[:, :],
                                  wrow[0:1, half * 512:(half + 1) * 512], start=True, stop=True)
            last = mmop(fbc)
            war[("wrow",)] = ("mm", last)
            c.wait("vector", "mm", last)
            c.wait_pair("vector", war.get(("w_bc", half)))
            t = vop(lambda e, bwc=bwc, half=half: e.tensor_copy(
                w_bc[:, half * 512:(half + 1) * 512], ps[:, bwc, :]))
            misc_ring.done(bwc, "vv", t)
        h_up_ready = None
        for mu in range(8):
            v, wt, key = wload(col_block(wup[n, :, :], mu * 128), "w", wpv)
            c.wait_pair("tensor", wt)
            for tc2 in range(2):
                tof = tc2 * 512
                b, last = group(up_ring, [(v[:, k, :], a_b[:, k, tof:tof + 512]) for k in range(KT)])
                if tc2 == 1:
                    wfree(key, last)
                c.wait("scalar", "mm", last)
                c.wait_pair("scalar", ("mm", down_par[n % 2][1]) if down_par[n % 2] else None)
                t = sop(lambda e, mu=mu, b=b, n=n, tof=tof, src=src: e.activation(
                    src[:, mu, tof:tof + 512], ps[:, b, :], AF.Identity, bias=bup_sb[:, n, mu:mu + 1]))
                up_ring.done(b, "sc", t)
                h_up_ready = t
        h2_ready = {0: None, 1: None}
        for mw in range(8):
            v, wt, key = wload(col_block(wsw[n, :, :], mw * 128), "w", wpv, nkt=8)
            c.wait_pair("tensor", wt)
            for tc2 in range(2):
                tof = tc2 * 512
                b, last = group(sw_ring, [(v[:, k, :], src[:, k, tof:tof + 512]) for k in range(8)],
                                extra_wait=[("sc", h_up_ready)])
                if tc2 == 1:
                    wfree(key, last)
                bsv = misc_ring.take(engine="scalar")
                c.wait("scalar", "mm", last)
                t = sop(lambda e, b=b, bsv=bsv, n=n, mw=mw: e.activation(
                    ps[:, bsv, :], ps[:, b, :], AF.Silu, bias=bsw_sb[:, n, mw:mw + 1]))
                sw_ring.done(b, "sc", t)
                c.wait("vector", "sc", t)
                c.wait_pair("vector", ("mm", down_par[(n + 1) % 2][1]) if down_par[(n + 1) % 2] else None)
                t = vop(lambda e, bsv=bsv, mw=mw, tof=tof, src=src, dst=dst: e.tensor_mul(
                    dst[:, mw, tof:tof + 512], ps[:, bsv, :], src[:, mw, tof:tof + 512]))
                misc_ring.done(bsv, "vv", t)
                h2_ready[tc2] = vop(lambda e, mw=mw, tof=tof, dst=dst: e.tensor_mul(
                    dst[:, mw, tof:tof + 512], dst[:, mw, tof:tof + 512], w_bc[:, tof:tof + 512]))
        last_all = None
        for md in range(KT):
            v, wt, key = wload(col_block(wdn[n, :, :], md * 128), "w", wpv, nkt=8)
            c.wait_pair("tensor", wt)
            for tc2 in range(2):
                tof = tc2 * 512
                mms = [(v[:, k, :], dst[:, k, tof:tof + 512]) for k in range(8)]
                ew = [("vv", h2_ready[tc2])]
                if n == 0:
                    wfmv = w_fm[0:6, :, :].rearrange("n tm m -> n (tm m)")[:, tof:tof + 512]
                    mms.append((bdn_sb[0:6, md * 128:(md + 1) * 128], wfmv))
                    ew.append(war.get(("w_fm",)))
                b, last = group(dn_ring, mms, extra_wait=[w for w in ew if w])
                if tc2 == 1:
                    wfree(key, last)
                last_all = last
                c.wait("vector", "mm", last)
                t = vop(lambda e, md=md, b=b, tof=tof: e.tensor_add(
                    acc[:, md, tof:tof + 512], acc[:, md, tof:tof + 512], ps[:, b, :]))
                dn_ring.done(b, "vv", t)
                if tail_cb is not None:
                    tail_cb(md, tc2)
        down_par[(n + 1) % 2] = ("mm", last_all)
        war[("w_bc", 0)] = war[("w_bc", 1)] = ("mm", last_all)

    def make_xt1_tail(src_d, slab):
        def cb(mt, tc2):
            tcg = slab * 2 + tc2
            slot = mt % 2
            tl = xload(src_d[mt * 128:(mt + 1) * 128, tcg * 512:(tcg + 1) * 512], slot)
            c.wait_pair("vector", tl)
            vop(lambda e, slot=slot, tc2=tc2: e.tensor_mul(t1b[:, :], xbuf[:, slot, :], ubc_sb[:, tc2, :]))
            vop(lambda e, tc2=tc2: e.tensor_add(t1b[:, :], t1b[:, :], bbc_sb[:, tc2, :]))
            free_tgt = vop(lambda e, mt=mt, tc2=tc2: e.tensor_add(
                a_b[:, mt, tc2 * 512:(tc2 + 1) * 512], t1b[:, :], acc[:, mt, tc2 * 512:(tc2 + 1) * 512]))
            war[("xbuf", slot)] = ("vv", free_tgt)
            war[("a_ready",)] = ("vv", free_tgt)
        return cb

    def make_asm_tail(src_d, slab, dest_d, next_ln_idx, stat_banks):
        def cb(mt, tc2):
            tcg = slab * 2 + tc2
            slot = mt % 2
            tl = xload(src_d[mt * 128:(mt + 1) * 128, tcg * 512:(tcg + 1) * 512], slot)
            c.wait_pair("vector", tl)
            vop(lambda e, slot=slot, tc2=tc2: e.tensor_mul(t1b[:, :], xbuf[:, slot, :], ubc_sb[:, tc2, :]))
            vop(lambda e, tc2=tc2: e.tensor_add(t1b[:, :], t1b[:, :], bbc_sb[:, tc2, :]))
            vop(lambda e, mt=mt, tc2=tc2: e.tensor_add(
                t1b[:, :], t1b[:, :], acc[:, mt, tc2 * 512:(tc2 + 1) * 512]))
            c.wait_pair("vector", war.get(("outh", slot)))
            c.wait_pair("vector", war.get(("outhc", slot)))
            free_tgt = vop(lambda e, slot=slot: e.scalar_tensor_tensor(
                outh[:, slot, :], t1b[:, :], 0.5, xbuf[:, slot, :], ALU.mult, ALU.add))
            war[("xbuf", slot)] = ("vv", free_tgt)
            if next_ln_idx is not None:
                if tc2 not in stat_banks:
                    ring = sw_ring if tc2 == 0 else misc_ring
                    stat_banks[tc2] = (ring.take(), ring.take())
                    stat_banks[("ring", tc2)] = ring
                bs, bs2 = stat_banks[tc2]
                c.wait("scalar", "vv", free_tgt)
                ts2 = cast_pair(slot, outh[:, slot, :], "statmm")
                war[("outhc", slot)] = ("sc", ts2)
                c.wait("tensor", "sc", ts2)
                def f1(eng, slot=slot, mt=mt, bs=bs):
                    return eng.matmul(ps[0:1, bs, :], onescol[:, :], xb16[:, slot, :],
                                      start=(mt == 0), stop=(mt == KT - 1))
                def f2(eng, slot=slot, mt=mt, bs2=bs2):
                    return eng.matmul(ps[0:1, bs2, :], onescol[:, :], sqx[:, slot, :],
                                      start=(mt == 0), stop=(mt == KT - 1))
                mmop(f1)
                tf2 = mmop(f2)
                war[("statmm", slot)] = ("mm", tf2)
                stat_banks[("last", tc2)] = tf2
            c.wait("sync", "vv", free_tgt)
            st = c.dma("sync", "oh%d" % slot, dest_d[mt * 128:(mt + 1) * 128, tcg * 512:(tcg + 1) * 512],
                       outh[:, slot, :])
            war[("outh", slot)] = ("oh%d" % slot, st)
        return cb

    def finish_asm_stats(slab, next_ln_idx, stat_banks):
        if next_ln_idx is None:
            return
        for tc2 in range(2):
            tcg = slab * 2 + tc2
            bs, bs2 = stat_banks[tc2]
            ring = stat_banks[("ring", tc2)]
            c.wait("vector", "mm", stat_banks[("last", tc2)])
            c.wait_pair("vector", war.get(("rowS",)))
            v1 = vop(lambda e, bs=bs: e.tensor_copy(rowS[:, :], ps[0:1, bs, :]))
            v2 = vop(lambda e, bs2=bs2: e.tensor_copy(rowS2[:, :], ps[0:1, bs2, :]))
            ring.done(bs, "vv", v1)
            ring.done(bs2, "vv", v2)
            c.wait("sync", "vv", v2)
            c.dma("sync", "sd", statd[next_ln_idx, 0, tcg, :][None, :], rowS[:, :])
            war[("rowS",)] = ("sd", c.dma("sync", "sd", statd[next_ln_idx, 1, tcg, :][None, :], rowS2[:, :]))

    def emit_xt1(src_d, ln_idx, slab):
        c.wait("vector", "mm", c.cnt["mm"])
        for tc in range(2):
            tcg = slab * 2 + tc
            bu, bb, lastb = ln_rows(ln_idx, tcg)
            free_tgt = None
            for mt in range(KT):
                slot = mt % 2
                tl = xload(src_d[mt * 128:(mt + 1) * 128, tcg * 512:(tcg + 1) * 512], slot)
                c.wait_pair("vector", tl)
                c.wait("vector", "mm", lastb)
                vop(lambda e, slot=slot, bu=bu: e.tensor_mul(t1b[:, :], xbuf[:, slot, :], ps[:, bu, :]))
                vop(lambda e, bb=bb: e.tensor_add(t1b[:, :], t1b[:, :], ps[:, bb, :]))
                free_tgt = vop(lambda e, mt=mt, tc=tc: e.tensor_add(
                    a_b[:, mt, tc * 512:(tc + 1) * 512], t1b[:, :], acc[:, mt, tc * 512:(tc + 1) * 512]))
                war[("xbuf", slot)] = ("vv", free_tgt)
            misc_ring.done(bu, "vv", free_tgt)
            misc_ring.done(bb, "vv", free_tgt)
            war[("a_ready",)] = ("vv", free_tgt)

    def emit_assembly(src_d, ln_idx, slab, dest_d, next_ln_idx):
        for tc in range(2):
            tcg = slab * 2 + tc
            bu, bb, lastb = ln_rows(ln_idx, tcg)
            bs = bs2 = None
            if next_ln_idx is not None:
                bs = dn_ring.take()
                bs2 = dn_ring.take()
            l2 = None
            free_tgt = None
            for mt in range(KT):
                slot = mt % 2
                tl = xload(src_d[mt * 128:(mt + 1) * 128, tcg * 512:(tcg + 1) * 512], slot)
                c.wait_pair("vector", tl)
                c.wait("vector", "mm", lastb)
                vop(lambda e, slot=slot, bu=bu: e.tensor_mul(t1b[:, :], xbuf[:, slot, :], ps[:, bu, :]))
                vop(lambda e, bb=bb: e.tensor_add(t1b[:, :], t1b[:, :], ps[:, bb, :]))
                vop(lambda e, mt=mt, tc=tc: e.tensor_add(
                    t1b[:, :], t1b[:, :], acc[:, mt, tc * 512:(tc + 1) * 512]))
                c.wait_pair("vector", war.get(("outh", slot)))
                c.wait_pair("vector", war.get(("outhc", slot)))
                free_tgt = vop(lambda e, slot=slot: e.scalar_tensor_tensor(
                    outh[:, slot, :], t1b[:, :], 0.5, xbuf[:, slot, :], ALU.mult, ALU.add))
                war[("xbuf", slot)] = ("vv", free_tgt)
                if next_ln_idx is not None:
                    c.wait("scalar", "vv", free_tgt)
                    ts2 = cast_pair(slot, outh[:, slot, :], "statmm")
                    war[("outhc", slot)] = ("sc", ts2)
                    c.wait("tensor", "sc", ts2)
                    def f1(eng, slot=slot, mt=mt, bs=bs):
                        return eng.matmul(ps[0:1, bs, :], onescol[:, :], xb16[:, slot, :],
                                          start=(mt == 0), stop=(mt == KT - 1))
                    def f2(eng, slot=slot, mt=mt, bs2=bs2):
                        return eng.matmul(ps[0:1, bs2, :], onescol[:, :], sqx[:, slot, :],
                                          start=(mt == 0), stop=(mt == KT - 1))
                    mmop(f1)
                    tf2 = mmop(f2)
                    if mt == KT - 1:
                        l2 = tf2
                    war[("statmm", slot)] = ("mm", tf2)
                c.wait("sync", "vv", free_tgt)
                st = c.dma("sync", "oh%d" % slot, dest_d[mt * 128:(mt + 1) * 128, tcg * 512:(tcg + 1) * 512],
                           outh[:, slot, :])
                war[("outh", slot)] = ("oh%d" % slot, st)
            misc_ring.done(bu, "vv", free_tgt)
            misc_ring.done(bb, "vv", free_tgt)
            if next_ln_idx is not None:
                c.wait("vector", "mm", l2)
                v1 = vop(lambda e, bs=bs: e.tensor_copy(rowS[:, :], ps[0:1, bs, :]))
                v2 = vop(lambda e, bs2=bs2: e.tensor_copy(rowS2[:, :], ps[0:1, bs2, :]))
                dn_ring.done(bs, "vv", v1)
                dn_ring.done(bs2, "vv", v2)
                c.wait("sync", "vv", v2)
                c.dma("sync", "sd", statd[next_ln_idx, 0, tcg, :][None, :], rowS[:, :])
                war[("rowS",)] = ("sd", c.dma("sync", "sd", statd[next_ln_idx, 1, tcg, :][None, :], rowS2[:, :]))

    def emit_ffn_call(src_d, ln_idx, dest_d, next_ln_idx):
        for e in ["vector", "scalar", "gpsimd", "sync"]:
            c.wait(e, "mm", c.cnt["mm"])
        c.wait("tensor", "vv", c.cnt["vv"])
        c.wait("tensor", "sc", c.cnt["sc"])
        c.wait("sync", "oh0", c.cnt["oh0"])
        c.wait("sync", "oh1", c.cnt["oh1"])
        for slab in range(2):
            emit_ln_apply(src_d, ln_idx, slab * 2, 2, a_b)
            for j in range(2):
                emit_gate(j)
                emit_shared(slab)
                for n in range(NEXP - 1):
                    emit_expert(n)
                if j == 0:
                    for tc2 in range(2):
                        ln_rows_sbuf(ln_idx, slab * 2 + tc2, tc2)
                        war[("ubc_sb", tc2)] = None
                    emit_expert(NEXP - 1, tail_cb=make_xt1_tail(src_d, slab))
                else:
                    for tc2 in range(2):
                        ln_rows_sbuf(ln_idx, slab * 2 + tc2, tc2)
                        war[("ubc_sb", tc2)] = None
                    stat_banks = {}
                    emit_expert(NEXP - 1, tail_cb=make_asm_tail(src_d, slab, dest_d, next_ln_idx, stat_banks))
                    finish_asm_stats(slab, next_ln_idx, stat_banks)

    def emit_attn_proj():
        for e in ["vector", "scalar", "gpsimd", "sync"]:
            c.wait(e, "mm", c.cnt["mm"])
        c.wait("tensor", "vv", c.cnt["vv"])
        c.wait("tensor", "sc", c.cnt["sc"])
        c.wait("sync", "oh0", c.cnt["oh0"])
        c.wait("sync", "oh1", c.cnt["oh1"])
        emit_ln_apply(xres1, 1, 0, 4, a2, dest_guard=False)
        c.wait_pair("tensor", war.get(("a_ready",)))
        for mq in range(8):
            v, wt, key = wload(col_block(wk, mq * 128), "w", wpv)
            c.wait_pair("tensor", wt)
            for tc in range(4):
                tof = tc * 512
                b, last = group(up_ring, [(v[:, k, :], a2[:, k, tof:tof + 512]) for k in range(KT)])
                if tc == 3:
                    wfree(key, last)
                c.wait("scalar", "mm", last)
                c.wait_pair("scalar", war.get(("prB", 0)))
                tsq = sop(lambda e, b=b: e.activation(prB[:, :], ps[:, b, :], AF.Square))
                bsq, lasts = group(misc_ring, [(sel64[:, :], prB[:, :])], out_rows=2,
                                   extra_wait=[("sc", tsq)])
                war[("prB", 0)] = ("mm", lasts)
                c.wait_pair("scalar", war.get(("prA", 0)))
                t = sop(lambda e, b=b: e.copy(prA[:, :], ps[:, b, :]))
                up_ring.done(b, "sc", t)
                c.wait("scalar", "mm", lasts)
                t = sop(lambda e, bsq=bsq: e.activation(rnk[:, :], ps[0:2, bsq, :], AF.Sqrt))
                misc_ring.done(bsq, "sc", t)
                c.wait("vector", "sc", t)
                vop(lambda e: e.tensor_scalar_max(rnk[:, :], rnk[:, :], float(L2_EPS)))
                vop(lambda e: e.reciprocal(rnk2[:, :], rnk[:, :]))
                vop(lambda e, mq=mq: e.tensor_scalar_mul(rnk2[:, :], rnk2[:, :], r_cols[:, mq:mq + 1]))
                c.wait_pair("vector", war.get(("rnkb",)))
                tv = vop(lambda e: e.tensor_copy(rnkb[:, :], rnk2[:, :]))
                bb2, lastb2 = group(misc_ring, [(sel64T[:, :], rnkb[:, :])], extra_wait=[("vv", tv)])
                war[("rnkb",)] = ("mm", lastb2)
                c.wait("vector", "mm", lastb2)
                c.wait_pair("vector", war.get(("vnb", 0)))
                tv = vop(lambda e, bb2=bb2: e.tensor_mul(vnb[:, :], prA[:, :], ps[:, bb2, :]))
                war[("prA", 0)] = ("vv", tv)
                misc_ring.done(bb2, "vv", tv)
                c.wait("sync", "vv", tv)
                st = c.dma("sync", "kvw", KLOC[mq, :, tof:tof + 512], vnb[:, :])
                war[("vnb", 0)] = ("kvw", st)
        for dvc in range(4):
            v, wt, key = wload(wv[:, dvc * 512:(dvc + 1) * 512].rearrange("(kt p) m -> p kt m", p=128),
                               "v", wvpv)
            c.wait_pair("tensor", wt)
            for tm in range(KT):
                b, last = group(up_ring, [(a2[:, k, tm * 128:(tm + 1) * 128], v[:, k, :]) for k in range(KT)])
                if tm == KT - 1:
                    wfree(key, last)
                c.wait("scalar", "mm", last)
                c.wait_pair("scalar", war.get(("vps", 0)))
                ts = sop(lambda e, b=b: e.activation(vps[:, :], ps[:, b, :], AF.Square))
                c.wait("vector", "sc", ts)
                tv = vop(lambda e, tm=tm, dvc=dvc: e.tensor_reduce(
                    vsacc[:, tm:tm + 1, dvc * 4:(dvc + 1) * 4].rearrange("p a (x b) -> p (a x) b", b=1),
                    vps[:, :].rearrange("p (a b) -> p a b", b=128), AX.X, ALU.add))
                war[("vps", 0)] = ("vv", tv)
                c.wait_pair("scalar", war.get(("vnb", 0)))
                t = sop(lambda e, b=b: e.copy(vnb[:, :], ps[:, b, :]))
                up_ring.done(b, "sc", t)
                c.wait("sync", "sc", t)
                st = c.dma("sync", "kvw", VLOC[tm, :, dvc * 512:(dvc + 1) * 512], vnb[:, :])
                war[("vnb", 0)] = ("kvw", st)
        c.wait("scalar", "vv", c.cnt["vv"])
        for tm in range(KT):
            c.wait_pair("scalar", war.get(("vps", 0)))
            c.wait_pair("scalar", war.get(("vpsrn", 0)))
            t = sop(lambda e, tm=tm: e.activation(vps[:, 0:16], vsacc[:, tm, :], AF.Sqrt))
            c.wait("vector", "sc", t)
            vop(lambda e: e.tensor_scalar_max(vps[:, 0:16], vps[:, 0:16], float(L2_EPS)))
            t = vop(lambda e: e.tensor_copy(vps[:, 16:32], vps[:, 0:16]))
            c.wait("scalar", "vv", t)
            t = sop(lambda e: e.activation(vps[:, 32:48], vps[:, 16:32], AF.Sqrt))
            c.wait("vector", "sc", t)
            tv = vop(lambda e: e.reciprocal(vps[:, 48:64], vps[:, 32:48]))
            c.wait("sync", "vv", tv)
            st = c.dma("sync", "kvw", RLOC[tm, :, :], vps[:, 48:64])
            war[("vpsrn", 0)] = ("kvw", st)
            war[("vps", 0)] = ("kvw", st)
        c.wait("sync", "kvw", c.cnt["kvw"])
        c.dma("sync", "kvw", KSH[bass.ds(par, 1), :, :, :], KLOC[None, :, :, :])
        c.dma("sync", "kvw", VSH[bass.ds(par, 1), :, :, :], VLOC[None, :, :, :])
        c.dma("sync", "kvw", RNSH[bass.ds(par, 1), :, :, :], RLOC[None, :, :, :])
        c.wait("sync", "kvw", c.cnt["kvw"])
        c.dma("sync", "kvw", FKV[bass.ds(par, 1), 0:1, 0:1], nonb[0:1, 0:1][None, :, :])
        c.wait("sync", "kvw", c.cnt["kvw"])
        for mq in range(8):
            v, wt, key = wload(col_block(wq, mq * 128), "w", wpv)
            c.wait_pair("tensor", wt)
            for tc in range(4):
                tof = tc * 512
                b, last = group(up_ring, [(v[:, k, :], a2[:, k, tof:tof + 512]) for k in range(KT)])
                if tc == 3:
                    wfree(key, last)
                c.wait("scalar", "mm", last)
                c.wait_pair("scalar", war.get(("prB", 0)))
                tsq = sop(lambda e, b=b: e.activation(prB[:, :], ps[:, b, :], AF.Square))
                bsq, lasts = group(misc_ring, [(sel64[:, :], prB[:, :])], out_rows=2,
                                   extra_wait=[("sc", tsq)])
                war[("prB", 0)] = ("mm", lasts)
                c.wait_pair("scalar", war.get(("prA", 0)))
                t = sop(lambda e, b=b: e.copy(prA[:, :], ps[:, b, :]))
                up_ring.done(b, "sc", t)
                c.wait("scalar", "mm", lasts)
                t = sop(lambda e, bsq=bsq: e.activation(rnk[:, :], ps[0:2, bsq, :], AF.Sqrt))
                misc_ring.done(bsq, "sc", t)
                c.wait("vector", "sc", t)
                vop(lambda e: e.tensor_scalar_max(rnk[:, :], rnk[:, :], float(L2_EPS)))
                vop(lambda e: e.reciprocal(rnk2[:, :], rnk[:, :]))
                c.wait_pair("vector", war.get(("rnkb",)))
                tv = vop(lambda e: e.tensor_copy(rnkb[:, :], rnk2[:, :]))
                bb2, lastb2 = group(misc_ring, [(sel64T[:, :], rnkb[:, :])], extra_wait=[("vv", tv)])
                war[("rnkb",)] = ("mm", lastb2)
                c.wait("vector", "mm", lastb2)
                tv = vop(lambda e, bb2=bb2, mq=mq, tof=tof: e.tensor_mul(
                    q_sb[:, mq, tof:tof + 512], prA[:, :], ps[:, bb2, :]))
                war[("prA", 0)] = ("vv", tv)
                misc_ring.done(bb2, "vv", tv)
        for gm in range(KT):
            v, wt, key = wload(col_block(wgt, gm * 128), "w", wpv)
            c.wait_pair("tensor", wt)
            for tc in range(4):
                tof = tc * 512
                b, last = group(up_ring, [(v[:, k, :], a2[:, k, tof:tof + 512]) for k in range(KT)])
                if tc == 3:
                    wfree(key, last)
                c.wait("scalar", "mm", last)
                c.wait_pair("scalar", war.get(("prD", 0)))
                t = sop(lambda e, b=b: e.activation(prC[:, :], ps[:, b, :], AF.Silu))
                up_ring.done(b, "sc", t)
                t = sop(lambda e: e.activation(prD[:, :], prC[:, :], AF.Tanh))
                c.wait("sync", "sc", t)
                st = c.dma("sync", "kvw", g2d[gm, :, tof:tof + 512], prD[:, :])
                war[("prD", 0)] = ("kvw", st)

    sim_free = {}
    attn_ring = PsumRing(c, [0, 3])

    def emit_attn_core():
        for e in ["vector", "scalar", "gpsimd", "sync"]:
            c.wait(e, "mm", c.cnt["mm"])
        c.wait("tensor", "vv", c.cnt["vv"])
        c.wait("tensor", "sc", c.cnt["sc"])
        c.wait("sync", "oh0", c.cnt["oh0"])
        c.wait("sync", "oh1", c.cnt["oh1"])
        for p2 in range(2):
            rn_src = (RLOC[:, :, :] if self_pair else RNSH[p2, :, :, :])
            c.dma("sync", "xr", srn[:, p2 * 16:(p2 + 1) * 16, :],
                  rn_src.rearrange("tm p h -> p tm h"))
        tsrn = c.cnt["xr"]
        c.wait("sync", "xr", tsrn)
        c.wait("scalar", "xr", tsrn)
        kv_free = {}
        wo_last = None
        for qb in range(4):
            qof = qb * 512
            if wo_last is not None:
                c.wait("vector", "mm", wo_last)
            for hp in range(8):
                slot = hp % 2
                c.wait_pair("sync", kv_free.get(slot))
                ksv = "kv%d" % slot
                for p2 in range(2):
                    k_src = (KLOC[hp, :, :] if self_pair else KSH[p2, hp, :, :])
                    c.dma("sync", ksv, kbuf[:, slot, p2, :], k_src)
                    for j in range(2):
                        h = 2 * hp + j
                        v_src = (VLOC[:, :, h * 128:(h + 1) * 128]
                                 if self_pair else VSH[p2, :, :, h * 128:(h + 1) * 128])
                        c.dma("sync", ksv, vbuf[:, slot, j, p2 * 16:(p2 + 1) * 16, :],
                              v_src.rearrange("tm p dv -> p tm dv"))
                for j in range(2):
                    c.dma("sync", ksv, g2b[:, slot, j, :], g2d[2 * hp + j, :, qof:qof + 512])
                tkv = c.cnt[ksv]
                c.wait("tensor", ksv, tkv)
                c.wait("vector", ksv, tkv)
                c.wait("scalar", ksv, tkv)
                agg_last = [None, None]
                sq_t = {}
                PIPE = 2
                for it in range(32 + PIPE):
                    if it < 32:
                        kb = it
                        for j in range(2):
                            h = 2 * hp + j
                            bsim = j * 3 + kb % 3
                            c.wait_pair("tensor", sim_free.get(bsim))
                            if bsim in (0, 3):
                                fa_ = attn_ring.free_at.get(bsim)
                                if fa_ is not None:
                                    c.wait_pair("tensor", fa_)
                            def fsim(eng, j=j, kb=kb, slot=slot, hp=hp, qof=qof, bsim=bsim):
                                return eng.matmul(
                                    ps[:, bsim, :],
                                    kbuf[j * 64:(j + 1) * 64, slot, kb // 16, (kb % 16) * 128:(kb % 16) * 128 + 128],
                                    q_sb[j * 64:(j + 1) * 64, hp, qof:qof + 512],
                                    start=True, stop=True, tile_position=(j * 64, 0))
                            lsim = mmop(fsim)
                            if j == 0:
                                c.wait("scalar", "mm", lsim)
                                c.wait_pair("scalar", war.get(("rt", j, kb % 2)))
                                trl = sop(lambda e, j=j, kb=kb, h=h, bsim=bsim: e.activation(
                                    rt[:, j, kb % 2, :].bitcast(BF16)[:, 0:512], ps[:, bsim, :],
                                    AF.Relu, bias=b_all[:, h:h + 1]))
                                sim_free[bsim] = ("sc", trl)
                                c.wait("vector", "sc", trl)
                                c.wait_pair("vector", war.get(("at", j, kb % 4)))
                                tq = vop(lambda e, j=j, kb=kb, h=h: e.scalar_tensor_tensor(
                                    at[:, j, kb % 4, :], rt[:, j, kb % 2, :].bitcast(BF16)[:, 0:512],
                                    srn[:, kb, h:h + 1],
                                    rt[:, j, kb % 2, :].bitcast(BF16)[:, 0:512], ALU.mult, ALU.mult), sync=False)
                                war[("rt", j, kb % 2)] = ("vv", tq)
                                sq_t[(kb, j)] = ("vv", tq)
                            else:
                                c.wait("vector", "mm", lsim)
                                c.wait_pair("vector", war.get(("rt", j, kb % 2)))
                                trl = vop(lambda e, j=j, kb=kb, h=h, bsim=bsim: e.tensor_scalar(
                                    rt[:, j, kb % 2, :].bitcast(BF16)[:, 0:512], ps[:, bsim, :],
                                    b_all[:, h:h + 1], 0.0, ALU.add, ALU.max), sync=False)
                                sim_free[bsim] = ("vv", trl)
                                c.wait("scalar", "vv", trl)
                                c.wait_pair("scalar", war.get(("at", j, kb % 4)))
                                tq = sop(lambda e, j=j, kb=kb, h=h: e.activation(
                                    at[:, j, kb % 4, :], rt[:, j, kb % 2, :].bitcast(BF16)[:, 0:512],
                                    AF.Square, scale=srn2[:, kb, h:h + 1]))
                                war[("rt", j, kb % 2)] = ("sc", tq)
                                sq_t[(kb, j)] = ("sc", tq)
                    if it >= PIPE:
                        kb = it - PIPE
                        for j in range(2):
                            c.wait_pair("tensor", sq_t[(kb, j)])
                            if kb == 0:
                                c.wait_pair("tensor", sim_free.get(100 + j))
                            def fagg(eng, j=j, kb=kb, slot=slot):
                                return eng.matmul(ps[:, 6 + j, :], vbuf[:, slot, j, kb, :],
                                                  at[:, j, kb % 4, :], start=(kb == 0), stop=(kb == 31))
                            tagg = mmop(fagg)
                            if kb == 31:
                                agg_last[j] = tagg
                            war[("at", j, kb % 4)] = ("mm", tagg)
                ep_last = None
                c.wait_pair("tensor", sim_free.get(0))
                c.wait_pair("tensor", sim_free.get(3))
                for j in range(2):
                    h = 2 * hp + j
                    c.wait("scalar", "mm", agg_last[j])
                    c.wait("scalar", "vv", c.cnt["vv"])
                    c.wait_pair("scalar", war.get(("sqe", 0)))
                    c.wait_pair("scalar", war.get(("arwC",)))
                    tsq = sop(lambda e, j=j: e.activation(sqe[:, :], ps[:, 6 + j, :], AF.Square))
                    bsq, lasts = group(attn_ring, [(onescol[:, :], sqe[:, :])], out_rows=1,
                                       extra_wait=[("sc", tsq)])
                    war[("sqe", 0)] = ("mm", lasts)
                    c.wait_pair("scalar", war.get(("af", j)))
                    t = sop(lambda e, j=j: e.copy(af[:, j, :], ps[:, 6 + j, :]))
                    sim_free[100 + j] = ("sc", t)
                    c.wait("scalar", "mm", lasts)
                    t = sop(lambda e, bsq=bsq: e.activation(arwA[:, :], ps[0:1, bsq, :], AF.Sqrt))
                    attn_ring.done(bsq, "sc", t)
                    c.wait("vector", "sc", t)
                    vop(lambda e: e.tensor_scalar_max(arwB[:, :], arwA[:, :], float(L2_EPS)))
                    c.wait_pair("vector", war.get(("arwC",)))
                    tv = vop(lambda e: e.reciprocal(arwC[:, :], arwB[:, :]))
                    c.wait("scalar", "vv", tv)
                    t = sop(lambda e: e.activation(arwB[:, :], arwA[:, :], AF.Tanh))
                    c.wait("vector", "sc", t)
                    vop(lambda e: e.tensor_mul(arwC[:, :], arwC[:, :], arwB[:, :]))
                    tv = vop(lambda e, h=h: e.tensor_scalar_mul(arwC[:, :], arwC[:, :], ehws[0:1, h:h + 1]))
                    bb2, lastb2 = group(attn_ring, [(ones1f[:, :], arwC[:, :])], extra_wait=[("vv", tv)])
                    war[("arwC",)] = ("mm", lastb2)
                    c.wait("vector", "mm", lastb2)
                    vop(lambda e, j=j, bb2=bb2: e.tensor_mul(aft[:, :], af[:, j, :], ps[:, bb2, :]))
                    war[("af", j)] = ("vv", c.cnt["vv"])
                    ep_last = vop(lambda e, j=j, h=h, slot=slot: e.tensor_mul(
                        osc[:, h, :], aft[:, :], g2b[:, slot, j, :]))
                    kv_free[slot] = ("vv", ep_last)
                    attn_ring.done(bb2, "vv", ep_last)
                kv_free[slot] = ("vv", ep_last)
                war[("osc",)] = ("vv", ep_last)
            for bb_ in range(6):
                c.wait_pair("tensor", sim_free.get(bb_))
            c.wait_pair("tensor", sim_free.get(100))
            c.wait_pair("tensor", sim_free.get(101))
            bs = dn_ring.take()
            bs2 = dn_ring.take()
            l2 = None
            for mt in range(KT):
                v, wt, key = wload(col_block(wo, mt * 128), "w", wpv)
                c.wait_pair("tensor", wt)
                b, last = group(up_ring, [(v[:, k, :], osc[:, k, :]) for k in range(KT)],
                                extra_wait=[war.get(("osc",))] if war.get(("osc",)) else [])
                wfree(key, last)
                wo_last = last
                slot = mt % 2
                tl = xload(xres1[mt * 128:(mt + 1) * 128, qof:qof + 512], slot)
                c.wait_pair("vector", tl)
                c.wait("vector", "mm", last)
                c.wait_pair("vector", war.get(("outh", slot)))
                c.wait_pair("vector", war.get(("outhc", slot)))
                free_tgt = vop(lambda e, b=b, slot=slot: e.tensor_add(
                    outh[:, slot, :], xbuf[:, slot, :], ps[:, b, :]))
                war[("xbuf", slot)] = ("vv", free_tgt)
                up_ring.done(b, "vv", free_tgt)
                c.wait("scalar", "vv", free_tgt)
                ts2 = cast_pair(slot, outh[:, slot, :], "statmm")
                war[("outhc", slot)] = ("sc", ts2)
                c.wait("tensor", "sc", ts2)
                def f1(eng, slot=slot, mt=mt, bs=bs):
                    return eng.matmul(ps[0:1, bs, :], onescol[:, :], xb16[:, slot, :],
                                      start=(mt == 0), stop=(mt == KT - 1))
                def f2(eng, slot=slot, mt=mt, bs2=bs2):
                    return eng.matmul(ps[0:1, bs2, :], onescol[:, :], sqx[:, slot, :],
                                      start=(mt == 0), stop=(mt == KT - 1))
                mmop(f1)
                tf2 = mmop(f2)
                if mt == KT - 1:
                    l2 = tf2
                war[("statmm", slot)] = ("mm", tf2)
                c.wait("sync", "vv", free_tgt)
                st = c.dma("sync", "oh%d" % slot, xres2[mt * 128:(mt + 1) * 128, qof:qof + 512], outh[:, slot, :])
                war[("outh", slot)] = ("oh%d" % slot, st)
            c.wait("vector", "mm", l2)
            c.wait_pair("vector", war.get(("rowS",)))
            v1 = vop(lambda e, bs=bs: e.tensor_copy(rowS[:, :], ps[0:1, bs, :]))
            v2 = vop(lambda e, bs2=bs2: e.tensor_copy(rowS2[:, :], ps[0:1, bs2, :]))
            dn_ring.done(bs, "vv", v1)
            dn_ring.done(bs2, "vv", v2)
            c.wait("sync", "vv", v2)
            c.dma("sync", "sd", statd[2, 0, qb, :][None, :], rowS[:, :])
            war[("rowS",)] = ("sd", c.dma("sync", "sd", statd[2, 1, qb, :][None, :], rowS2[:, :]))

    def emit_poll(flag_t, psem_name, tag):
        loop = f"poll_{tag}_loop"
        done = f"poll_{tag}_done"
        sy = nc.sync
        (f_r,) = list(nc.alloc_registers(f"pf_{tag}", engines=[SP]))
        (w_r,) = list(nc.alloc_registers(f"pw_{tag}", engines=[SP]))
        (n_r,) = list(nc.alloc_registers(f"pn_{tag}", engines=[SP]))
        sy.reg_mov(w_r, 0)
        sy.reg_load(n_r, nonb[0:1, 0:1])
        nc.br(loop, engines=[SP])
        with nc.body(loop, valid_engines=[SP]):
            sy.dma_start(out=fsb[0:1, 0:1], in_=flag_t[bass.ds(opar, 1), 0:1, 0:1]).then_inc(c.sem[psem_name], 16)
            sy.reg_alu(w_r, w_r, 16, ALU.add)
            sy.wait_ge(c.sem[psem_name], w_r)
            sy.reg_load(f_r, fsb[0:1, 0:1])
            sy.reg_alu(f_r, f_r, n_r, ALU.bitwise_xor)
            sy.reg_alu(f_r, f_r, 0x7FFFFFFF, ALU.bitwise_and)
            nc.br_lt(f_r, 1, on_true=done, on_false=loop, engines=[SP])
        nc.switch_bb(done)

    with nc.Block() as block:
        emit_init()
        emit_stats(x0, 0)
        emit_ffn_call(x0, 0, xres1, 1)
        if stages >= 2:
            emit_attn_proj()
        else:
            for s_ in ["xb0", "xb1", "oh0", "oh1", "sd"]:
                c.wait("sync", s_, c.cnt[s_])
            c.dma("sync", "kvw", out_d, xres1)
            c.wait("sync", "kvw", c.cnt["kvw"])
        for s_ in ["xb0", "xb1", "oh0", "oh1", "sd", "kvw", "xr"]:
            c.wait("sync", s_, c.cnt[s_])
        c.run(block)

    if stages >= 2:
        emit_poll(FKV, "p1", "kv")
        with nc.Block() as block:
            emit_attn_core()
            if stages >= 3:
                for s_ in ["kv0", "kv1", "xr"]:
                    c.wait("sync", s_, c.cnt[s_])
                c.dma("sync", "kvw", FDN[bass.ds(par, 1), 0:1, 0:1], nonb[0:1, 0:1][None, :, :])
                emit_ffn_call(xres2, 2, out_d, None)
            else:
                c.wait("sync", "vv", c.cnt["vv"])
                c.dma("sync", "kvw", out_d, xres2)
            if debug:
                c.dma("sync", "kvw", dbg1, xres1)
                c.dma("sync", "kvw", dbg2, xres2)
            if stages < 3:
                c.dma("sync", "kvw", FDN[bass.ds(par, 1), 0:1, 0:1], nonb[0:1, 0:1][None, :, :])
            for s_ in ["xb0", "xb1", "oh0", "oh1", "sd", "kvw", "xr", "kv0", "kv1"]:
                c.wait("sync", s_, c.cnt[s_])
            c.run(block)
        emit_poll(FDN, "p2", "dn")
    return nc



def _prep(inputs):
    p = {}
    x = np.asarray(inputs["L0_attn_tokens"], np.float32).reshape(-1, DM)
    qg = np.asarray(inputs["L0_attn_Wqg"], np.float32).reshape(DM, 16, 192)
    kv = np.asarray(inputs["L0_attn_Wkv"], np.float32).reshape(DM, 16, 192)
    p["wq"] = np.ascontiguousarray(qg[:, :, :64].reshape(DM, 1024))
    p["wgt"] = np.ascontiguousarray(qg[:, :, 64:].reshape(DM, DM))
    p["wk"] = np.ascontiguousarray(kv[:, :, :64].reshape(DM, 1024))
    p["wv"] = np.ascontiguousarray(kv[:, :, 64:].reshape(DM, DM))
    p["wo"] = np.ascontiguousarray(np.asarray(inputs["L0_attn_Wo"], np.float32))
    p["ls_iaw"] = np.asarray(inputs["L0_attn_ls_iaw"], np.float32).reshape(1, 16)
    p["ls_hws"] = np.asarray(inputs["L0_attn_ls_hws"], np.float32).reshape(1, 16)
    wg = np.asarray(inputs["L0_ffn_Wg"], np.float32)
    p["wgp"] = np.ascontiguousarray(wg.reshape(2, 16, 128, 8).transpose(2, 1, 0, 3))
    p["bgr"] = np.ascontiguousarray(np.asarray(inputs["L0_ffn_bg"], np.float32).reshape(1, 16))
    p["wup"] = np.ascontiguousarray(np.asarray(inputs["L0_ffn_Wup"], np.float32))
    p["wsw"] = np.ascontiguousarray(np.asarray(inputs["L0_ffn_Wsw"], np.float32))
    p["wdn"] = np.ascontiguousarray(np.asarray(inputs["L0_ffn_Wdn"], np.float32))
    p["wsu"] = np.ascontiguousarray(np.asarray(inputs["L0_ffn_Wsu"], np.float32))
    p["wsa"] = np.ascontiguousarray(np.asarray(inputs["L0_ffn_Wsa"], np.float32))
    p["wsd"] = np.ascontiguousarray(np.asarray(inputs["L0_ffn_Wsd"], np.float32))
    p["bupp"] = np.ascontiguousarray(np.asarray(inputs["L0_ffn_bup"], np.float32).reshape(NEXP, 8, 128).transpose(2, 0, 1))
    p["bswp"] = np.ascontiguousarray(np.asarray(inputs["L0_ffn_bsw"], np.float32).reshape(NEXP, 8, 128).transpose(2, 0, 1))
    p["bdn2"] = np.ascontiguousarray(np.asarray(inputs["L0_ffn_bdn"], np.float32))
    p["bsup"] = np.ascontiguousarray(np.asarray(inputs["L0_ffn_bsu"], np.float32).reshape(16, 128).T)
    p["bsap"] = np.ascontiguousarray(np.asarray(inputs["L0_ffn_bsa"], np.float32).reshape(8, 128).T)
    p["bsdp"] = np.ascontiguousarray(np.asarray(inputs["L0_ffn_bsd"], np.float32).reshape(16, 128).T)
    sel = np.zeros((128, 2), np.float32)
    sel[:64, 0] = 1.0
    sel[64:, 1] = 1.0
    p["csel64"] = sel
    p["csel64T"] = np.ascontiguousarray(sel.T)
    p["cones1"] = np.ones((1, 128), np.float32)
    p["cones1f"] = np.ones((1, 128), np.float32)
    p["conescol"] = np.ones((128, 1), np.float32)
    p["cident"] = np.eye(128, dtype=np.float32)
    p["ceps"] = np.full((1, 1), 1e-5, np.float32)
    return x, p


def kernel(**inputs):
    x, p = _prep(inputs)
    nonce = np.array([[int(time.time_ns() // 1000) & 0x3FFFFFFF]], dtype=np.uint32)
    nc = build_nc(STAGES, DEBUG)
    in_maps = []
    for cid in range(8):
        m = dict(p)
        m["x0"] = np.ascontiguousarray(x[cid * TOK:(cid + 1) * TOK, :].T)
        m["nonce"] = nonce
        in_maps.append(m)
    res = run_bass_kernel_spmd(nc, in_maps, core_ids=list(range(8)))
    kernel._last_res = res
    outs = [np.asarray(res.results[cid]["out"]).T for cid in range(8)]
    return np.concatenate(outs, axis=0).reshape(4, 4096, DM).astype(np.float32)


# version 04
# speedup vs baseline: 1.9691x; 1.9691x over previous
import time
import numpy as np
import ml_dtypes
import concourse.bass as bass
import concourse.mybir as mybir
from concourse.bass_utils import run_bass_kernel_spmd

F32 = mybir.dt.float32
BF16 = mybir.dt.bfloat16
U32 = mybir.dt.uint32
AF = mybir.ActivationFunctionType
ALU = mybir.AluOpType
AX = mybir.AxisListType
SP = mybir.EngineType.SP

DM = 2048
TOK = 2048
SLAB = 1024
KT = 16
NEXP = 6
LN_EPS = 1e-5
L2_EPS = 1e-12
BIG = 1e30
DEBUG = False
STAGES = 3


class Ctx:
    ENG = ["tensor", "scalar", "vector", "gpsimd", "sync"]

    def __init__(self, nc):
        self.nc = nc
        self.sem = {}
        self.cnt = {}
        self.waited = {}
        for name in ["w0", "w1", "v0", "v1", "xb0", "xb1", "oh0", "oh1",
                     "sd", "xr", "xi", "kv0", "kv1", "kvw", "mm", "sc", "vv", "pl", "p1", "p2"]:
            self.sem[name] = nc.alloc_semaphore(name="s_" + name)
            self.cnt[name] = 0
        self.stream = {e: [] for e in self.ENG}

    def wait(self, ename, sname, target):
        if target is None or target <= 0:
            return
        key = (ename, sname)
        if self.waited.get(key, 0) >= target:
            return
        self.waited[key] = target
        sem = self.sem[sname]
        self.stream[ename].append(lambda eng, s=sem, t=target: eng.wait_ge(s, t))

    def wait_pair(self, ename, pair):
        if pair is not None:
            self.wait(ename, pair[0], pair[1])

    def op(self, ename, fn, inc=None):
        if inc is None:
            self.stream[ename].append(fn)
            return None
        sem = self.sem[inc]
        self.cnt[inc] += 1
        tgt = self.cnt[inc]
        def wrapped(eng, fn=fn, sem=sem):
            fn(eng).then_inc(sem, 1)
        self.stream[ename].append(wrapped)
        return tgt

    def dma(self, ename, sname, out, in_):
        sem = self.sem[sname]
        self.cnt[sname] += 16
        tgt = self.cnt[sname]
        def fn(eng, out=out, in_=in_, sem=sem):
            eng.dma_start(out=out, in_=in_).then_inc(sem, 16)
        self.stream[ename].append(fn)
        return tgt

    def barrier_waits(self, ename):
        for s in ["mm", "sc", "vv"]:
            self.wait(ename, s, self.cnt[s])

    def run(self, block):
        decos = {"tensor": block.tensor, "scalar": block.scalar,
                 "vector": block.vector, "gpsimd": block.gpsimd, "sync": block.sync}
        for ename in self.ENG:
            ops = self.stream[ename]
            if not ops:
                continue
            def body(eng, ops=ops):
                for f in ops:
                    f(eng)
            decos[ename](body)
        self.stream = {e: [] for e in self.ENG}


class PsumRing:
    def __init__(self, ctx, banks):
        self.ctx = ctx
        self.banks = list(banks)
        self.i = 0
        self.free_at = {b: None for b in self.banks}

    def take(self, engine="tensor"):
        b = self.banks[self.i % len(self.banks)]
        self.i += 1
        fa = self.free_at[b]
        if fa is not None:
            self.ctx.wait(engine, fa[0], fa[1])
        return b

    def done(self, b, sname, target):
        self.free_at[b] = (sname, target)


def build_nc(stages=STAGES, debug=DEBUG, self_pair=False):
    nc = bass.Bass("TRN2", target_bir_lowering=False)
    c = Ctx(nc)

    def I(name, shape, dt=F32):
        return nc.dram_tensor(name, shape, dt, kind="ExternalInput").ap()

    x0 = I("x0", [DM, TOK])
    wq = I("wq", [DM, 1024]); wk = I("wk", [DM, 1024])
    wv = I("wv", [DM, DM]); wgt = I("wgt", [DM, DM]); wo = I("wo", [DM, DM])
    ls_iaw = I("ls_iaw", [1, 16]); ls_hws = I("ls_hws", [1, 16])
    wgp = I("wgp", [128, KT, 2, 8]); bgr = I("bgr", [1, 16])
    wup = I("wup", [NEXP, DM, 1024]); wsw = I("wsw", [NEXP, 1024, 1024])
    wdn = I("wdn", [NEXP, 1024, DM])
    wsu = I("wsu", [DM, DM]); wsa = I("wsa", [1024, 1024]); wsd = I("wsd", [DM, DM])
    bupp = I("bupp", [128, NEXP, 8]); bswp = I("bswp", [128, NEXP, 8])
    bdn2 = I("bdn2", [NEXP, DM]); bsup = I("bsup", [128, 16]); bsap = I("bsap", [128, 8])
    bsdp = I("bsdp", [128, 16])
    csel64 = I("csel64", [128, 2]); csel64T = I("csel64T", [2, 128])
    cones1 = I("cones1", [1, 128]); cones1f = I("cones1f", [1, 128])
    conescol = I("conescol", [128, 1]); cident = I("cident", [128, 128])
    nonce_in = I("nonce", [1, 1], U32)
    ceps = I("ceps", [1, 1])

    out_d = nc.dram_tensor("out", [DM, TOK], F32, kind="ExternalOutput").ap()
    dbg1 = dbg2 = None
    if debug:
        dbg1 = nc.dram_tensor("dbg1", [DM, TOK], F32, kind="ExternalOutput").ap()
        dbg2 = nc.dram_tensor("dbg2", [DM, TOK], F32, kind="ExternalOutput").ap()

    xres1 = nc.dram_tensor("xres1", [DM, TOK], F32, kind="Internal").ap()
    xres2 = nc.dram_tensor("xres2", [DM, TOK], F32, kind="Internal").ap()
    g2d = nc.dram_tensor("g2d", [16, 128, TOK], BF16, kind="Internal").ap()
    rdram = nc.dram_tensor("rdram", [1, 16], F32, kind="Internal").ap()
    statd = nc.dram_tensor("statd", [3, 2, 4, 512], F32, kind="Internal").ap()
    KSH = nc.dram_tensor("KSH", [2, 8, 128, TOK], BF16, kind="Internal", addr_space="Shared").ap()
    VSH = nc.dram_tensor("VSH", [2, 16, 128, DM], BF16, kind="Internal", addr_space="Shared").ap()
    RNSH = nc.dram_tensor("RNSH", [2, 16, 128, 32], F32, kind="Internal", addr_space="Shared").ap()
    KLOC = nc.dram_tensor("KLOC", [8, 128, TOK], BF16, kind="Internal").ap()
    VLOC = nc.dram_tensor("VLOC", [16, 128, DM], BF16, kind="Internal").ap()
    RLOC = nc.dram_tensor("RLOC", [16, 128, 32], F32, kind="Internal").ap()
    FKV = nc.dram_tensor("FKV", [2, 1, 8], U32, kind="Internal", addr_space="Shared").ap()
    FDN = nc.dram_tensor("FDN", [2, 1, 8], U32, kind="Internal", addr_space="Shared").ap()

    sb = lambda n, s, d: nc.alloc_sbuf_tensor(n, s, d)
    arena = sb("arena", [128, 128 * 1024], mybir.dt.uint8)
    abase = nc.lookup_mloc(arena).addr

    def ar(name, shape, dtype, off_kb):
        return nc.alloc_sbuf_tensor_at(name, shape, dtype, offset=int(abase + off_kb * 1024))

    a_b = ar("a_b", [128, KT, SLAB], BF16, 0)
    acc = ar("acc", [128, KT, SLAB], F32, 32)
    B0 = ar("B0", [128, 8, SLAB], BF16, 96)
    B1 = ar("B1", [128, 8, SLAB], BF16, 112)
    Bd = [B0, B1]
    sh_h = ar("sh_h", [128, KT, 512], BF16, 96)
    h2sh = ar("h2sh", [128, 8, 512], BF16, 112)
    h2sh2 = ar("h2sh2", [128, 8, 512], BF16, 120)
    a2 = ar("a2", [128, KT, TOK], BF16, 0)
    wvp = ar("wvp", [128, 2, KT, 512], BF16, 64)
    q_sb = ar("q_sb", [128, 8, TOK], BF16, 64)
    kbuf = ar("kbuf", [128, 2, 2, TOK], BF16, 0)
    vbuf = ar("vbuf", [128, 2, 2, 32, 128], BF16, 16)
    osc = ar("osc", [128, 16, 512], BF16, 48)
    rt = ar("rt", [128, 2, 2, 512], F32, 96)
    at = ar("at", [128, 2, 4, 512], BF16, 104)
    af = ar("af", [128, 2, 512], F32, 112)
    g2b = ar("g2b", [128, 2, 2, 512], BF16, 116)
    srn = ar("srn", [128, 32, 32], F32, 120)
    sqe = ar("sqe", [128, 512], BF16, 124)
    aft = ar("aft", [128, 512], F32, 125)
    vsacc = ar("vsacc", [128, 16, 16], F32, 105)
    vps = ar("vps", [128, 512], F32, 102)
    vnb = ar("vnb", [128, 512], BF16, 106)
    prA = ar("prA", [128, 512], F32, 96)
    prB = ar("prB", [128, 512], BF16, 98)
    prC = ar("prC", [128, 512], F32, 99)
    prD = ar("prD", [128, 512], BF16, 101)

    wpool = sb("wpool", [128, 2, KT * 256], BF16)
    xbuf = sb("xbuf", [128, 2, 512], F32)
    t1b = sb("t1b", [128, 512], F32)
    t1c = sb("t1c", [128, 512], F32)
    outh = sb("outh", [128, 2, 512], F32)
    xb16 = sb("xb16", [128, 2, 512], BF16)
    sqx = sb("sqx", [128, 2, 512], BF16)
    gate_f = sb("gate_f", [128, 64], F32)
    tk = {n: sb("tk_" + n, [128, 64], F32) for n in ["t8", "g1", "g2t", "k8", "d8", "e8", "em8"]}
    w_tm = sb("w_tm", [128, 64], F32)
    tkr = {n: sb("tkr_" + n, [128, 8], F32) for n in ["m1", "m2", "m3", "nmx", "gz", "gr"]}
    w_fm = sb("w_fm", [8, 8, 128], BF16)
    w_bc = sb("w_bc", [128, SLAB], BF16)
    wrow = sb("wrow", [1, SLAB], BF16)
    bgb = sb("bgb", [128, 16], F32)
    wg_sb = sb("wg_sb", [128, KT, 2, 8], BF16)
    bup_sb = sb("bup_sb", [128, NEXP, 8], F32)
    bsw_sb = sb("bsw_sb", [128, NEXP, 8], F32)
    bdn_sb = sb("bdn_sb", [NEXP, DM], BF16)
    bsu_sb = sb("bsu_sb", [128, 16], F32)
    bsa_sb = sb("bsa_sb", [128, 8], F32)
    bsd_sb = sb("bsd_sb", [128, 16], F32)
    sel64 = sb("sel64", [128, 2], BF16)
    sel64T = sb("sel64T", [2, 128], BF16)
    ones1 = sb("ones1", [1, 128], BF16)
    ones1f = sb("ones1f", [1, 128], F32)
    onescol = sb("onescol", [128, 1], BF16)
    ident_f = sb("ident_f", [128, 128], F32)
    rra = sb("rra", [1, 16], F32)
    rrb = sb("rrb", [1, 16], F32)
    rrc = sb("rrc", [1, 16], F32)
    r_cols = sb("r_cols", [2, 8], F32)
    ehws = sb("ehws", [1, 16], F32)
    b_all = sb("b_all", [128, 16], F32)
    rowS = sb("rowS", [1, 512], F32)
    rowS2 = sb("rowS2", [1, 512], F32)
    rowU = sb("rowU", [1, 512], F32)
    rowB = sb("rowB", [1, 512], F32)
    rowT = sb("rowT", [1, 512], F32)
    rowT2 = sb("rowT2", [1, 512], F32)
    ubc_sb = sb("ubc_sb", [128, 2, 512], F32)
    bbc_sb = sb("bbc_sb", [128, 2, 512], F32)
    arwA = sb("arwA", [1, 512], F32)
    arwB = sb("arwB", [1, 512], F32)
    arwC = sb("arwC", [1, 512], F32)
    rnk = sb("rnk", [2, 512], F32)
    rnk2 = sb("rnk2", [2, 512], F32)
    rnkb = sb("rnkb", [2, 512], BF16)
    epst = sb("epst", [1, 1], F32)
    nonb = sb("nonb", [1, 8], U32)
    fsb = sb("fsb", [1, 8], U32)

    ps = nc.alloc_psum_tensor("ps", [128, 8, 512], F32)

    wslot_free = {}
    wload_counter = {}

    def wload(src_ap, slot_group, dst_views, nkt=None, ncol=128):
        idx = wload_counter.setdefault(slot_group, 0)
        wload_counter[slot_group] = idx + 1
        slot = idx % 2
        key = (slot_group, slot)
        wm = wslot_free.get(key)
        if wm is not None:
            c.wait("gpsimd", wm[0], wm[1])
        view = dst_views[slot]
        if nkt is not None and view.shape[1] != nkt:
            view = view[:, 0:nkt, :]
        if view.shape[2] != ncol:
            view = view[:, :, 0:ncol]
        sname = ("w" if slot_group == "w" else "v") + str(slot)
        tgt = c.dma("gpsimd", sname, view, src_ap)
        return view, (sname, tgt), key

    def wfree(key, mm_target):
        wslot_free[key] = ("mm", mm_target)

    wpv = [wpool[:, s, :].rearrange("p (kt m) -> p kt m", m=256) for s in range(2)]
    wvpv = [wvp[:, s, :, :] for s in range(2)]

    def vop(fn, sync=True):
        t = c.op("vector", fn, inc="vv")
        if sync:
            c.wait("vector", "vv", t)
        return t

    def sop(fn, sync=True):
        t = c.op("scalar", fn, inc="sc")
        if sync:
            c.wait("scalar", "sc", t)
        return t

    def mmop(fn):
        return c.op("tensor", fn, inc="mm")

    def pop_(fn, sync=True):
        t = c.op("gpsimd", fn, inc="pl")
        if sync:
            c.wait("gpsimd", "pl", t)
        return t

    up_ring = PsumRing(c, [0, 1])
    sw_ring = PsumRing(c, [2, 3])
    dn_ring = PsumRing(c, [4, 5])
    misc_ring = PsumRing(c, [6, 7])

    def group(ring, mms, extra_wait=(), out_cols=None, out_rows=128):
        b = ring.take()
        for sname, t in extra_wait:
            c.wait("tensor", sname, t)
        n = len(mms)
        last = None
        for i, (lh, rh) in enumerate(mms):
            o = ps[0:out_rows, b, :] if out_cols is None else ps[0:out_rows, b, out_cols[0]:out_cols[0] + out_cols[1]]
            def f(eng, lh=lh, rh=rh, i=i, n=n, o=o):
                return eng.matmul(o, lh, rh, start=(i == 0), stop=(i == n - 1))
            if i == n - 1:
                last = mmop(f)
            else:
                c.op("tensor", f)
        return b, last

    def seg8(tns):
        return tns[:, :].rearrange("p (a b) -> p a b", b=8)

    def colb8(tns):
        v = tns[:, :].rearrange("p (a b) -> p a b", b=1)
        return bass.AP(v.tensor, v.offset, [list(v.ap[0]), list(v.ap[1]), [0, 8]])

    nc.const_aps.register(F32, float(LN_EPS), epst[0:1, 0:1])
    pid = nc.partition_id(engines=[SP])
    par = pid & 1
    opar = par if self_pair else ((pid + 1) & 1)

    war = {}

    def col_block(w2d, col0, ncol=128):
        return w2d[:, col0:col0 + ncol].rearrange("(kt p) m -> p kt m", p=128)

    def emit_init():
        for dst, src in [(bup_sb[:, :, :], bupp), (bsw_sb[:, :, :], bswp),
                         (bsu_sb[:, :], bsup), (bsa_sb[:, :], bsap), (bsd_sb[:, :], bsdp),
                         (rra[:, :], ls_iaw), (rrb[:, :], ls_hws),
                         (nonb[0:1, 0:1], nonce_in), (ones1f[:, :], cones1f),
                         (epst[:, :], ceps),
                         (ident_f[:, :], cident), (rowT[0:1, 0:16], bgr)]:
            c.dma("sync", "xi", dst, src)
        tload = c.cnt["xi"]
        c.wait("sync", "xi", tload)
        for dst, src in [(sel64[:, :], csel64), (sel64T[:, :], csel64T),
                         (ones1[:, :], cones1), (onescol[:, :], conescol),
                         (wg_sb[:, :, :, :], wgp), (bdn_sb[:, :], bdn2)]:
            c.dma("gpsimd", "w0", dst, src)
        tcast = c.cnt["w0"]
        c.wait("gpsimd", "w0", tcast)
        for e in ["scalar", "vector", "tensor"]:
            c.wait(e, "xi", tload)
        c.wait("tensor", "w0", tcast)
        t = sop(lambda e: e.activation(rrc[:, :], rra[:, :], AF.Exp))
        c.wait("vector", "sc", t)
        vop(lambda e: e.tensor_scalar(rrc[:, :], rrc[:, :], 1.0, None, ALU.add))
        vop(lambda e: e.reciprocal(rra[:, :], rrc[:, :]))
        tv = vop(lambda e: e.tensor_scalar(rrc[:, :], rra[:, :], -1.0, 1.0, ALU.mult, ALU.add))
        c.wait("scalar", "vv", tv)
        sop(lambda e: e.activation(ehws[:, :], rrb[:, :], AF.Exp))
        c.wait("sync", "vv", tv)
        c.dma("sync", "xi", rdram, rra[:, :])
        c.wait("sync", "xi", c.cnt["xi"])
        c.cnt["xi"] += 16; tr = c.cnt["xi"]
        def _rl(eng, sem=c.sem["xi"]):
            eng.dma_start(out=r_cols[:, :], in_=rdram.rearrange("a (t j) -> (a j) t", j=2),
                          allow_slow_non_contiguous=True).then_inc(sem, 16)
        c.stream["sync"].append(_rl)
        c.wait("sync", "xi", tr)
        c.wait("tensor", "xi", tr)
        b, last = group(misc_ring, [(ones1f[:, :], rrc[:, :])], extra_wait=[("vv", tv)], out_cols=(0, 16))
        c.wait("vector", "mm", last)
        t = vop(lambda e, b=b: e.tensor_copy(b_all[:, :], ps[:, b, 0:16]))
        misc_ring.done(b, "vv", t)
        b, last = group(misc_ring, [(ones1f[:, :], rowT[0:1, 0:16])], out_cols=(0, 16))
        c.wait("vector", "mm", last)
        t = vop(lambda e, b=b: e.tensor_copy(bgb[:, :], ps[:, b, 0:16]))
        misc_ring.done(b, "vv", t)

    def ln_rows(ln_idx, tcg):
        c.wait("sync", "vv", c.cnt["vv"])
        c.wait("sync", "mm", c.cnt["mm"])
        c.wait("sync", "sd", c.cnt["sd"])
        c.dma("sync", "xr", rowS[:, :], statd[ln_idx, 0, tcg, :][None, :])
        t2 = c.dma("sync", "xr", rowS2[:, :], statd[ln_idx, 1, tcg, :][None, :])
        c.wait("sync", "xr", t2)
        c.wait("vector", "xr", t2)
        vop(lambda e: e.tensor_scalar_mul(rowS[:, :], rowS[:, :], 1.0 / DM))
        vop(lambda e: e.tensor_scalar_mul(rowS2[:, :], rowS2[:, :], 1.0 / DM))
        vop(lambda e: e.tensor_mul(rowT2[:, :], rowS[:, :], rowS[:, :]))
        t = vop(lambda e: e.tensor_sub(rowS2[:, :], rowS2[:, :], rowT2[:, :]))
        c.wait("scalar", "vv", t)
        t = sop(lambda e: e.activation(rowT2[:, :], rowS2[:, :], AF.Sqrt, bias=float(LN_EPS)))
        c.wait("vector", "sc", t)
        vop(lambda e: e.reciprocal(rowU[:, :], rowT2[:, :]))
        t = vop(lambda e: e.scalar_tensor_tensor(rowB[:, :], rowS[:, :], -1.0, rowU[:, :], ALU.mult, ALU.mult))
        bu, _ = group(misc_ring, [(ones1f[:, :], rowU[:, :])], extra_wait=[("vv", t)])
        bb, lastb = group(misc_ring, [(ones1f[:, :], rowB[:, :])])
        return bu, bb, lastb

    def ln_rows_sbuf(ln_idx, tcg, tc2):
        bu, bb, lastb = ln_rows(ln_idx, tcg)
        c.wait("vector", "mm", lastb)
        c.wait_pair("vector", war.get(("ubc_sb", tc2)))
        c.wait_pair("vector", war.get(("ubc2_sb", tc2)))
        vop(lambda e, bu=bu, tc2=tc2: e.tensor_copy(ubc_sb[:, tc2, :], ps[:, bu, :]))
        t = vop(lambda e, bb=bb, tc2=tc2: e.tensor_copy(bbc_sb[:, tc2, :], ps[:, bb, :]))
        misc_ring.done(bu, "vv", t)
        misc_ring.done(bb, "vv", t)
        return t

    def xload(src_ap, slot):
        c.wait_pair("sync", war.get(("xbuf", slot)))
        return ("xb%d" % slot, c.dma("sync", "xb%d" % slot, xbuf[:, slot, :], src_ap))

    def cast_pair(slot, src_tile, mm_guard_key):
        c.wait_pair("scalar", war.get((mm_guard_key, slot)))
        sop(lambda e, slot=slot, src_tile=src_tile: e.copy(xb16[:, slot, :], src_tile))
        return sop(lambda e, slot=slot, src_tile=src_tile: e.activation(sqx[:, slot, :], src_tile, AF.Square))

    def emit_stats(src_d, ln_idx):
        for tcg in range(4):
            bs = dn_ring.take()
            bs2 = dn_ring.take()
            l2 = None
            for mt in range(KT):
                slot = mt % 2
                tl = xload(src_d[mt * 128:(mt + 1) * 128, tcg * 512:(tcg + 1) * 512], slot)
                c.wait_pair("scalar", tl)
                ts2 = cast_pair(slot, xbuf[:, slot, :], "statmm")
                war[("xbuf", slot)] = ("sc", ts2)
                c.wait("tensor", "sc", ts2)
                def f1(eng, slot=slot, mt=mt, bs=bs):
                    return eng.matmul(ps[0:1, bs, :], onescol[:, :], xb16[:, slot, :],
                                      start=(mt == 0), stop=(mt == KT - 1))
                def f2(eng, slot=slot, mt=mt, bs2=bs2):
                    return eng.matmul(ps[0:1, bs2, :], onescol[:, :], sqx[:, slot, :],
                                      start=(mt == 0), stop=(mt == KT - 1))
                mmop(f1)
                tf2 = mmop(f2)
                if mt == KT - 1:
                    l2 = tf2
                war[("statmm", slot)] = ("mm", tf2)
            c.wait("vector", "mm", l2)
            c.wait_pair("vector", war.get(("rowS",)))
            v1 = vop(lambda e, bs=bs: e.tensor_copy(rowS[:, :], ps[0:1, bs, :]))
            v2 = vop(lambda e, bs2=bs2: e.tensor_copy(rowS2[:, :], ps[0:1, bs2, :]))
            dn_ring.done(bs, "vv", v1)
            dn_ring.done(bs2, "vv", v2)
            c.wait("sync", "vv", v2)
            c.dma("sync", "sd", statd[ln_idx, 0, tcg, :][None, :], rowS[:, :])
            war[("rowS",)] = ("sd", c.dma("sync", "sd", statd[ln_idx, 1, tcg, :][None, :], rowS2[:, :]))

    def emit_ln_apply(src_d, ln_idx, base_tcg, ntc, dest, dest_guard=True):
        if dest_guard:
            c.barrier_waits("vector")
            c.barrier_waits("gpsimd")
        for tc in range(ntc):
            tcg = base_tcg + tc
            tub = ln_rows_sbuf(ln_idx, tcg, tc % 2)
            c.wait("gpsimd", "vv", tub)
            free_v = free_p = None
            for mt in range(KT):
                slot = mt % 2
                tl = xload(src_d[mt * 128:(mt + 1) * 128, tcg * 512:(tcg + 1) * 512], slot)
                if mt % 2 == 0:
                    c.wait_pair("vector", tl)
                    vop(lambda e, slot=slot, tc=tc: e.tensor_mul(t1b[:, :], xbuf[:, slot, :], ubc_sb[:, tc % 2, :]))
                    free_v = vop(lambda e, mt=mt, tc=tc, dest=dest: e.tensor_add(
                        dest[:, mt, tc * 512:(tc + 1) * 512], t1b[:, :], bbc_sb[:, tc % 2, :]))
                    war[("xbuf", slot)] = ("vv", free_v)
                else:
                    c.wait_pair("gpsimd", tl)
                    pop_(lambda e, slot=slot, tc=tc: e.tensor_mul(t1c[:, :], xbuf[:, slot, :], ubc_sb[:, tc % 2, :]))
                    free_p = pop_(lambda e, mt=mt, tc=tc, dest=dest: e.tensor_add(
                        dest[:, mt, tc * 512:(tc + 1) * 512], t1c[:, :], bbc_sb[:, tc % 2, :]))
                    war[("xbuf", slot)] = ("pl", free_p)
            war[("ubc_sb", tc % 2)] = ("pl", free_p)
            war[("a_ready",)] = ("vv", free_v)
            war[("a_ready2",)] = ("pl", free_p)

    def emit_gate(j):
        c.wait("vector", "mm", c.cnt["mm"])
        c.wait_pair("tensor", war.get(("a_ready",)))
        c.wait_pair("tensor", war.get(("a_ready2",)))
        bg_ = misc_ring.take()
        last = None
        for tm in range(8):
            for k in range(KT):
                def f(eng, tm=tm, k=k, j=j, bg_=bg_):
                    return eng.matmul(ps[:, bg_, tm * 8:(tm + 1) * 8],
                                      a_b[:, k, tm * 128:(tm + 1) * 128],
                                      wg_sb[:, k, j, :], start=(k == 0), stop=(k == KT - 1))
                if tm == 7 and k == KT - 1:
                    last = mmop(f)
                else:
                    c.op("tensor", f)
        c.wait("vector", "mm", last)
        ev = vop(lambda e, bg_=bg_: e.tensor_copy(gate_f[:, :], ps[:, bg_, 0:64]))
        misc_ring.done(bg_, "vv", ev)
        bgv0 = bgb[:, j * 8:(j + 1) * 8].rearrange("p (a b) -> p a b", a=1)
        bgv = bass.AP(bgv0.tensor, bgv0.offset, [list(bgv0.ap[0]), [0, 8], list(bgv0.ap[2])])
        vop(lambda e: e.tensor_tensor(seg8(gate_f), seg8(gate_f), bgv, ALU.add))
        vop(lambda e: e.tensor_reduce(tkr["m1"][:, :].rearrange("p (a b) -> p a b", b=1), seg8(gate_f), AX.X, ALU.min))
        vop(lambda e: e.tensor_tensor(seg8(tk["t8"]), seg8(gate_f), colb8(tkr["m1"]), ALU.is_equal))
        vop(lambda e: e.scalar_tensor_tensor(tk["g1"][:, :], tk["t8"][:, :], BIG, gate_f[:, :], ALU.mult, ALU.add))
        vop(lambda e: e.tensor_reduce(tkr["m2"][:, :].rearrange("p (a b) -> p a b", b=1), seg8(tk["g1"]), AX.X, ALU.min))
        vop(lambda e: e.tensor_tensor(seg8(tk["t8"]), seg8(tk["g1"]), colb8(tkr["m2"]), ALU.is_equal))
        vop(lambda e: e.scalar_tensor_tensor(tk["g2t"][:, :], tk["t8"][:, :], BIG, tk["g1"][:, :], ALU.mult, ALU.add))
        vop(lambda e: e.tensor_reduce(tkr["m3"][:, :].rearrange("p (a b) -> p a b", b=1), seg8(tk["g2t"]), AX.X, ALU.min))
        vop(lambda e: e.tensor_tensor(seg8(tk["k8"]), seg8(gate_f), colb8(tkr["m3"]), ALU.is_ge))
        vop(lambda e: e.tensor_reduce(tkr["nmx"][:, :].rearrange("p (a b) -> p a b", b=1), seg8(gate_f), AX.X, ALU.max, negate=True))
        tdv = vop(lambda e: e.tensor_tensor(seg8(tk["d8"]), seg8(gate_f), colb8(tkr["nmx"]), ALU.add))
        c.wait("scalar", "vv", tdv)
        te = sop(lambda e: e.activation(tk["e8"][:, :], tk["d8"][:, :], AF.Exp))
        c.wait("vector", "sc", te)
        vop(lambda e: e.tensor_mul(tk["em8"][:, :], tk["e8"][:, :], tk["k8"][:, :]))
        vop(lambda e: e.tensor_reduce(tkr["gz"][:, :].rearrange("p (a b) -> p a b", b=1), seg8(tk["em8"]), AX.X, ALU.add))
        vop(lambda e: e.reciprocal(tkr["gr"][:, :], tkr["gz"][:, :]))
        wt = vop(lambda e: e.tensor_tensor(seg8(w_tm), seg8(tk["em8"]), colb8(tkr["gr"]), ALU.mult))
        c.wait("tensor", "vv", wt)
        ev = None
        for tm in range(8):
            bt = misc_ring.take()
            def ft(eng, tm=tm, bt=bt):
                return eng.transpose(ps[0:8, bt, 0:128], w_tm[:, tm * 8:(tm + 1) * 8], ident_f[:, :])
            lastt = mmop(ft)
            c.wait("vector", "mm", lastt)
            ev = vop(lambda e, tm=tm, bt=bt: e.tensor_copy(w_fm[:, tm, :], ps[0:8, bt, 0:128]))
            misc_ring.done(bt, "vv", ev)
        war[("w_fm",)] = ("vv", ev)

    down_par = {0: None, 1: None}

    def emit_shared(slab):
        for half in range(2):
            tof = half * 512
            sh_ready = None
            for ms2 in range(8):
                v, wt, key = wload(col_block(wsu, ms2 * 256, 256), "w", wpv, ncol=256)
                c.wait_pair("tensor", wt)
                for sub in range(2):
                    ms = ms2 * 2 + sub
                    vv = v[:, :, sub * 128:(sub + 1) * 128]
                    b, last = group(up_ring, [(vv[:, k, :], a_b[:, k, tof:tof + 512]) for k in range(KT)])
                    if sub == 1:
                        wfree(key, last)
                    c.wait("scalar", "mm", last)
                    c.wait_pair("scalar", ("mm", down_par[0][1]) if down_par[0] else None)
                    t = sop(lambda e, ms=ms, b=b: e.activation(
                        sh_h[:, ms, :], ps[:, b, :], AF.Identity, bias=bsu_sb[:, ms:ms + 1]))
                    up_ring.done(b, "sc", t)
                    sh_ready = t
            sa_last = None
            sd_last = None
            if True:
                h2_ready = {}
                h2b = [h2sh, h2sh2]
                for s2 in range(2):
                    for ma2 in range(4):
                        v, wt, key = wload(col_block(wsa, ma2 * 256, 256), "w", wpv, nkt=8, ncol=256)
                        c.wait_pair("tensor", wt)
                        for sub in range(2):
                            ma = ma2 * 2 + sub
                            vv = v[:, :, sub * 128:(sub + 1) * 128]
                            b, last = group(sw_ring, [(vv[:, k, :], sh_h[:, s2 * 8 + k, :]) for k in range(8)],
                                            extra_wait=[("sc", sh_ready)])
                            if sub == 1:
                                wfree(key, last)
                            sa_last = last
                            bsv = misc_ring.take(engine="scalar")
                            c.wait("scalar", "mm", last)
                            t = sop(lambda e, b=b, bsv=bsv, ma=ma: e.activation(
                                ps[:, bsv, :], ps[:, b, :], AF.Silu, bias=bsa_sb[:, ma:ma + 1]))
                            sw_ring.done(b, "sc", t)
                            c.wait("vector", "sc", t)
                            c.wait_pair("vector", ("mm", down_par[1][1]) if down_par[1] else None)
                            c.wait_pair("vector", war.get(("h2b", s2)))
                            t = vop(lambda e, bsv=bsv, ma=ma, s2=s2, h2b=h2b: e.tensor_mul(
                                h2b[s2][:, ma, :], ps[:, bsv, :], sh_h[:, s2 * 8 + ma, :]))
                            misc_ring.done(bsv, "vv", t)
                            h2_ready[s2] = t
                for s2 in range(2):
                    for md2 in range(8):
                        v, wt, key = wload(col_block(wsd[s2 * 1024:(s2 + 1) * 1024, :], md2 * 256, 256), "w", wpv, nkt=8, ncol=256)
                        c.wait_pair("tensor", wt)
                        for sub in range(2):
                            md = md2 * 2 + sub
                            vv = v[:, :, sub * 128:(sub + 1) * 128]
                            b, last = group(dn_ring, [(vv[:, k, :], h2b[s2][:, k, :]) for k in range(8)],
                                            extra_wait=[("vv", h2_ready[s2])])
                            if sub == 1:
                                wfree(key, last)
                            sd_last = last
                            war[("h2b", s2)] = ("mm", last)
                            c.wait("vector", "mm", last)
                            if s2 == 0:
                                t = vop(lambda e, md=md, b=b, tof=tof: e.tensor_scalar_add(
                                    acc[:, md, tof:tof + 512], ps[:, b, :], bsd_sb[:, md:md + 1]))
                            else:
                                t = vop(lambda e, md=md, b=b, tof=tof: e.tensor_add(
                                    acc[:, md, tof:tof + 512], acc[:, md, tof:tof + 512], ps[:, b, :]))
                            dn_ring.done(b, "vv", t)
            down_par[0] = ("mm", sa_last)
            down_par[1] = ("mm", sd_last)

    def emit_expert(n, tail_cb=None):
        src = Bd[n % 2]
        dst = Bd[(n + 1) % 2]
        c.wait_pair("sync", war.get(("w_fm",)))
        c.wait_pair("sync", war.get(("wrow",)))
        trw = c.dma("sync", "xr", wrow[0:1, :],
                    w_fm[n:n + 1, :, :].rearrange("a tm m -> a (tm m)"))
        c.wait("sync", "xr", trw)
        for half in range(2):
            bwc = misc_ring.take()
            c.wait("tensor", "xr", trw)
            def fbc(eng, bwc=bwc, half=half):
                return eng.matmul(ps[:, bwc, :], ones1[:, :],
                                  wrow[0:1, half * 512:(half + 1) * 512], start=True, stop=True)
            last = mmop(fbc)
            war[("wrow",)] = ("mm", last)
            c.wait("vector", "mm", last)
            c.wait_pair("vector", war.get(("w_bc", half)))
            t = vop(lambda e, bwc=bwc, half=half: e.tensor_copy(
                w_bc[:, half * 512:(half + 1) * 512], ps[:, bwc, :]))
            misc_ring.done(bwc, "vv", t)
        h_up_ready = None
        for mu2 in range(4):
            v, wt, key = wload(col_block(wup[n, :, :], mu2 * 256, 256), "w", wpv, ncol=256)
            c.wait_pair("tensor", wt)
            for sub in range(2):
                mu = mu2 * 2 + sub
                vv = v[:, :, sub * 128:(sub + 1) * 128]
                for tc2 in range(2):
                    tof = tc2 * 512
                    b, last = group(up_ring, [(vv[:, k, :], a_b[:, k, tof:tof + 512]) for k in range(KT)])
                    if sub == 1 and tc2 == 1:
                        wfree(key, last)
                    c.wait("scalar", "mm", last)
                    c.wait_pair("scalar", ("mm", down_par[n % 2][1]) if down_par[n % 2] else None)
                    t = sop(lambda e, mu=mu, b=b, n=n, tof=tof, src=src: e.activation(
                        src[:, mu, tof:tof + 512], ps[:, b, :], AF.Identity, bias=bup_sb[:, n, mu:mu + 1]))
                    up_ring.done(b, "sc", t)
                    h_up_ready = t
        h2_ready = {0: None, 1: None}
        for mw2 in range(4):
            v, wt, key = wload(col_block(wsw[n, :, :], mw2 * 256, 256), "w", wpv, nkt=8, ncol=256)
            c.wait_pair("tensor", wt)
            for sub in range(2):
              mw = mw2 * 2 + sub
              vv = v[:, :, sub * 128:(sub + 1) * 128]
              for tc2 in range(2):
                tof = tc2 * 512
                b, last = group(sw_ring, [(vv[:, k, :], src[:, k, tof:tof + 512]) for k in range(8)],
                                extra_wait=[("sc", h_up_ready)])
                if sub == 1 and tc2 == 1:
                    wfree(key, last)
                bsv = misc_ring.take(engine="scalar")
                c.wait("scalar", "mm", last)
                t = sop(lambda e, b=b, bsv=bsv, n=n, mw=mw: e.activation(
                    ps[:, bsv, :], ps[:, b, :], AF.Silu, bias=bsw_sb[:, n, mw:mw + 1]))
                sw_ring.done(b, "sc", t)
                c.wait("vector", "sc", t)
                c.wait_pair("vector", ("mm", down_par[(n + 1) % 2][1]) if down_par[(n + 1) % 2] else None)
                t = vop(lambda e, bsv=bsv, mw=mw, tof=tof, src=src, dst=dst: e.tensor_mul(
                    dst[:, mw, tof:tof + 512], ps[:, bsv, :], src[:, mw, tof:tof + 512]))
                misc_ring.done(bsv, "vv", t)
                h2_ready[tc2] = vop(lambda e, mw=mw, tof=tof, dst=dst: e.tensor_mul(
                    dst[:, mw, tof:tof + 512], dst[:, mw, tof:tof + 512], w_bc[:, tof:tof + 512]))
        last_all = None
        for md2 in range(8):
            v, wt, key = wload(col_block(wdn[n, :, :], md2 * 256, 256), "w", wpv, nkt=8, ncol=256)
            c.wait_pair("tensor", wt)
            for sub in range(2):
              md = md2 * 2 + sub
              vv = v[:, :, sub * 128:(sub + 1) * 128]
              for tc2 in range(2):
                tof = tc2 * 512
                mms = [(vv[:, k, :], dst[:, k, tof:tof + 512]) for k in range(8)]
                ew = [("vv", h2_ready[tc2])]
                if n == 0:
                    wfmv = w_fm[0:6, :, :].rearrange("n tm m -> n (tm m)")[:, tof:tof + 512]
                    mms.append((bdn_sb[0:6, md * 128:(md + 1) * 128], wfmv))
                    ew.append(war.get(("w_fm",)))
                b, last = group(dn_ring, mms, extra_wait=[w for w in ew if w])
                if sub == 1 and tc2 == 1:
                    wfree(key, last)
                last_all = last
                c.wait("vector", "mm", last)
                t = vop(lambda e, md=md, b=b, tof=tof: e.tensor_add(
                    acc[:, md, tof:tof + 512], acc[:, md, tof:tof + 512], ps[:, b, :]))
                dn_ring.done(b, "vv", t)
                if tail_cb is not None:
                    tail_cb(md, tc2)
        down_par[(n + 1) % 2] = ("mm", last_all)
        war[("w_bc", 0)] = war[("w_bc", 1)] = ("mm", last_all)

    def make_xt1_tail(src_d, slab):
        def cb(mt, tc2):
            tcg = slab * 2 + tc2
            slot = mt % 2
            tl = xload(src_d[mt * 128:(mt + 1) * 128, tcg * 512:(tcg + 1) * 512], slot)
            c.wait_pair("vector", tl)
            vop(lambda e, slot=slot, tc2=tc2: e.tensor_mul(t1b[:, :], xbuf[:, slot, :], ubc_sb[:, tc2, :]))
            vop(lambda e, tc2=tc2: e.tensor_add(t1b[:, :], t1b[:, :], bbc_sb[:, tc2, :]))
            free_tgt = vop(lambda e, mt=mt, tc2=tc2: e.tensor_add(
                a_b[:, mt, tc2 * 512:(tc2 + 1) * 512], t1b[:, :], acc[:, mt, tc2 * 512:(tc2 + 1) * 512]))
            war[("xbuf", slot)] = ("vv", free_tgt)
            war[("a_ready",)] = ("vv", free_tgt)
        return cb

    def make_asm_tail(src_d, slab, dest_d, next_ln_idx, stat_banks):
        def cb(mt, tc2):
            tcg = slab * 2 + tc2
            slot = mt % 2
            tl = xload(src_d[mt * 128:(mt + 1) * 128, tcg * 512:(tcg + 1) * 512], slot)
            c.wait_pair("vector", tl)
            vop(lambda e, slot=slot, tc2=tc2: e.tensor_mul(t1b[:, :], xbuf[:, slot, :], ubc_sb[:, tc2, :]))
            vop(lambda e, tc2=tc2: e.tensor_add(t1b[:, :], t1b[:, :], bbc_sb[:, tc2, :]))
            vop(lambda e, mt=mt, tc2=tc2: e.tensor_add(
                t1b[:, :], t1b[:, :], acc[:, mt, tc2 * 512:(tc2 + 1) * 512]))
            c.wait_pair("vector", war.get(("outh", slot)))
            c.wait_pair("vector", war.get(("outhc", slot)))
            free_tgt = vop(lambda e, slot=slot: e.scalar_tensor_tensor(
                outh[:, slot, :], t1b[:, :], 0.5, xbuf[:, slot, :], ALU.mult, ALU.add))
            war[("xbuf", slot)] = ("vv", free_tgt)
            if next_ln_idx is not None:
                if tc2 not in stat_banks:
                    ring = sw_ring if tc2 == 0 else misc_ring
                    stat_banks[tc2] = (ring.take(), ring.take())
                    stat_banks[("ring", tc2)] = ring
                bs, bs2 = stat_banks[tc2]
                c.wait("scalar", "vv", free_tgt)
                ts2 = cast_pair(slot, outh[:, slot, :], "statmm")
                war[("outhc", slot)] = ("sc", ts2)
                c.wait("tensor", "sc", ts2)
                def f1(eng, slot=slot, mt=mt, bs=bs):
                    return eng.matmul(ps[0:1, bs, :], onescol[:, :], xb16[:, slot, :],
                                      start=(mt == 0), stop=(mt == KT - 1))
                def f2(eng, slot=slot, mt=mt, bs2=bs2):
                    return eng.matmul(ps[0:1, bs2, :], onescol[:, :], sqx[:, slot, :],
                                      start=(mt == 0), stop=(mt == KT - 1))
                mmop(f1)
                tf2 = mmop(f2)
                war[("statmm", slot)] = ("mm", tf2)
                stat_banks[("last", tc2)] = tf2
            c.wait("sync", "vv", free_tgt)
            st = c.dma("sync", "oh%d" % slot, dest_d[mt * 128:(mt + 1) * 128, tcg * 512:(tcg + 1) * 512],
                       outh[:, slot, :])
            war[("outh", slot)] = ("oh%d" % slot, st)
        return cb

    def finish_asm_stats(slab, next_ln_idx, stat_banks):
        if next_ln_idx is None:
            return
        for tc2 in range(2):
            tcg = slab * 2 + tc2
            bs, bs2 = stat_banks[tc2]
            ring = stat_banks[("ring", tc2)]
            c.wait("vector", "mm", stat_banks[("last", tc2)])
            c.wait_pair("vector", war.get(("rowS",)))
            v1 = vop(lambda e, bs=bs: e.tensor_copy(rowS[:, :], ps[0:1, bs, :]))
            v2 = vop(lambda e, bs2=bs2: e.tensor_copy(rowS2[:, :], ps[0:1, bs2, :]))
            ring.done(bs, "vv", v1)
            ring.done(bs2, "vv", v2)
            c.wait("sync", "vv", v2)
            c.dma("sync", "sd", statd[next_ln_idx, 0, tcg, :][None, :], rowS[:, :])
            war[("rowS",)] = ("sd", c.dma("sync", "sd", statd[next_ln_idx, 1, tcg, :][None, :], rowS2[:, :]))

    def emit_xt1(src_d, ln_idx, slab):
        c.wait("vector", "mm", c.cnt["mm"])
        for tc in range(2):
            tcg = slab * 2 + tc
            bu, bb, lastb = ln_rows(ln_idx, tcg)
            free_tgt = None
            for mt in range(KT):
                slot = mt % 2
                tl = xload(src_d[mt * 128:(mt + 1) * 128, tcg * 512:(tcg + 1) * 512], slot)
                c.wait_pair("vector", tl)
                c.wait("vector", "mm", lastb)
                vop(lambda e, slot=slot, bu=bu: e.tensor_mul(t1b[:, :], xbuf[:, slot, :], ps[:, bu, :]))
                vop(lambda e, bb=bb: e.tensor_add(t1b[:, :], t1b[:, :], ps[:, bb, :]))
                free_tgt = vop(lambda e, mt=mt, tc=tc: e.tensor_add(
                    a_b[:, mt, tc * 512:(tc + 1) * 512], t1b[:, :], acc[:, mt, tc * 512:(tc + 1) * 512]))
                war[("xbuf", slot)] = ("vv", free_tgt)
            misc_ring.done(bu, "vv", free_tgt)
            misc_ring.done(bb, "vv", free_tgt)
            war[("a_ready",)] = ("vv", free_tgt)

    def emit_assembly(src_d, ln_idx, slab, dest_d, next_ln_idx):
        for tc in range(2):
            tcg = slab * 2 + tc
            bu, bb, lastb = ln_rows(ln_idx, tcg)
            bs = bs2 = None
            if next_ln_idx is not None:
                bs = dn_ring.take()
                bs2 = dn_ring.take()
            l2 = None
            free_tgt = None
            for mt in range(KT):
                slot = mt % 2
                tl = xload(src_d[mt * 128:(mt + 1) * 128, tcg * 512:(tcg + 1) * 512], slot)
                c.wait_pair("vector", tl)
                c.wait("vector", "mm", lastb)
                vop(lambda e, slot=slot, bu=bu: e.tensor_mul(t1b[:, :], xbuf[:, slot, :], ps[:, bu, :]))
                vop(lambda e, bb=bb: e.tensor_add(t1b[:, :], t1b[:, :], ps[:, bb, :]))
                vop(lambda e, mt=mt, tc=tc: e.tensor_add(
                    t1b[:, :], t1b[:, :], acc[:, mt, tc * 512:(tc + 1) * 512]))
                c.wait_pair("vector", war.get(("outh", slot)))
                c.wait_pair("vector", war.get(("outhc", slot)))
                free_tgt = vop(lambda e, slot=slot: e.scalar_tensor_tensor(
                    outh[:, slot, :], t1b[:, :], 0.5, xbuf[:, slot, :], ALU.mult, ALU.add))
                war[("xbuf", slot)] = ("vv", free_tgt)
                if next_ln_idx is not None:
                    c.wait("scalar", "vv", free_tgt)
                    ts2 = cast_pair(slot, outh[:, slot, :], "statmm")
                    war[("outhc", slot)] = ("sc", ts2)
                    c.wait("tensor", "sc", ts2)
                    def f1(eng, slot=slot, mt=mt, bs=bs):
                        return eng.matmul(ps[0:1, bs, :], onescol[:, :], xb16[:, slot, :],
                                          start=(mt == 0), stop=(mt == KT - 1))
                    def f2(eng, slot=slot, mt=mt, bs2=bs2):
                        return eng.matmul(ps[0:1, bs2, :], onescol[:, :], sqx[:, slot, :],
                                          start=(mt == 0), stop=(mt == KT - 1))
                    mmop(f1)
                    tf2 = mmop(f2)
                    if mt == KT - 1:
                        l2 = tf2
                    war[("statmm", slot)] = ("mm", tf2)
                c.wait("sync", "vv", free_tgt)
                st = c.dma("sync", "oh%d" % slot, dest_d[mt * 128:(mt + 1) * 128, tcg * 512:(tcg + 1) * 512],
                           outh[:, slot, :])
                war[("outh", slot)] = ("oh%d" % slot, st)
            misc_ring.done(bu, "vv", free_tgt)
            misc_ring.done(bb, "vv", free_tgt)
            if next_ln_idx is not None:
                c.wait("vector", "mm", l2)
                v1 = vop(lambda e, bs=bs: e.tensor_copy(rowS[:, :], ps[0:1, bs, :]))
                v2 = vop(lambda e, bs2=bs2: e.tensor_copy(rowS2[:, :], ps[0:1, bs2, :]))
                dn_ring.done(bs, "vv", v1)
                dn_ring.done(bs2, "vv", v2)
                c.wait("sync", "vv", v2)
                c.dma("sync", "sd", statd[next_ln_idx, 0, tcg, :][None, :], rowS[:, :])
                war[("rowS",)] = ("sd", c.dma("sync", "sd", statd[next_ln_idx, 1, tcg, :][None, :], rowS2[:, :]))

    def emit_ffn_call(src_d, ln_idx, dest_d, next_ln_idx):
        for e in ["vector", "scalar", "gpsimd", "sync"]:
            c.wait(e, "mm", c.cnt["mm"])
        c.wait("tensor", "vv", c.cnt["vv"])
        c.wait("tensor", "sc", c.cnt["sc"])
        c.wait("sync", "oh0", c.cnt["oh0"])
        c.wait("sync", "oh1", c.cnt["oh1"])
        for slab in range(2):
            emit_ln_apply(src_d, ln_idx, slab * 2, 2, a_b)
            for j in range(2):
                emit_gate(j)
                emit_shared(slab)
                for n in range(NEXP - 1):
                    emit_expert(n)
                if j == 0:
                    for tc2 in range(2):
                        ln_rows_sbuf(ln_idx, slab * 2 + tc2, tc2)
                        war[("ubc_sb", tc2)] = None
                    emit_expert(NEXP - 1, tail_cb=make_xt1_tail(src_d, slab))
                else:
                    for tc2 in range(2):
                        ln_rows_sbuf(ln_idx, slab * 2 + tc2, tc2)
                        war[("ubc_sb", tc2)] = None
                    stat_banks = {}
                    emit_expert(NEXP - 1, tail_cb=make_asm_tail(src_d, slab, dest_d, next_ln_idx, stat_banks))
                    finish_asm_stats(slab, next_ln_idx, stat_banks)

    def emit_attn_proj():
        for e in ["vector", "scalar", "gpsimd", "sync"]:
            c.wait(e, "mm", c.cnt["mm"])
        c.wait("tensor", "vv", c.cnt["vv"])
        c.wait("tensor", "sc", c.cnt["sc"])
        c.wait("sync", "oh0", c.cnt["oh0"])
        c.wait("sync", "oh1", c.cnt["oh1"])
        emit_ln_apply(xres1, 1, 0, 4, a2, dest_guard=False)
        c.wait_pair("tensor", war.get(("a_ready",)))
        c.wait_pair("tensor", war.get(("a_ready2",)))
        for mq2 in range(4):
            v, wt, key = wload(col_block(wk, mq2 * 256, 256), "w", wpv, ncol=256)
            c.wait_pair("tensor", wt)
            sub_src = [(mq2 * 2 + s, v[:, :, s * 128:(s + 1) * 128]) for s in range(2)]
            for mq, vsub in sub_src:
              for tc in range(4):
                tof = tc * 512
                b, last = group(up_ring, [(vsub[:, k, :], a2[:, k, tof:tof + 512]) for k in range(KT)])
                if mq % 2 == 1 and tc == 3:
                    wfree(key, last)
                c.wait("scalar", "mm", last)
                c.wait_pair("scalar", war.get(("prB", 0)))
                tsq = sop(lambda e, b=b: e.activation(prB[:, :], ps[:, b, :], AF.Square))
                bsq, lasts = group(misc_ring, [(sel64[:, :], prB[:, :])], out_rows=2,
                                   extra_wait=[("sc", tsq)])
                war[("prB", 0)] = ("mm", lasts)
                c.wait_pair("scalar", war.get(("prA", 0)))
                t = sop(lambda e, b=b: e.copy(prA[:, :], ps[:, b, :]))
                up_ring.done(b, "sc", t)
                c.wait("scalar", "mm", lasts)
                t = sop(lambda e, bsq=bsq: e.activation(rnk[:, :], ps[0:2, bsq, :], AF.Sqrt))
                misc_ring.done(bsq, "sc", t)
                c.wait("vector", "sc", t)
                vop(lambda e: e.tensor_scalar_max(rnk[:, :], rnk[:, :], float(L2_EPS)))
                vop(lambda e: e.reciprocal(rnk2[:, :], rnk[:, :]))
                vop(lambda e, mq=mq: e.tensor_scalar_mul(rnk2[:, :], rnk2[:, :], r_cols[:, mq:mq + 1]))
                c.wait_pair("vector", war.get(("rnkb",)))
                tv = vop(lambda e: e.tensor_copy(rnkb[:, :], rnk2[:, :]))
                bb2, lastb2 = group(misc_ring, [(sel64T[:, :], rnkb[:, :])], extra_wait=[("vv", tv)])
                war[("rnkb",)] = ("mm", lastb2)
                c.wait("vector", "mm", lastb2)
                c.wait_pair("vector", war.get(("vnb", 0)))
                tv = vop(lambda e, bb2=bb2: e.tensor_mul(vnb[:, :], prA[:, :], ps[:, bb2, :]))
                war[("prA", 0)] = ("vv", tv)
                misc_ring.done(bb2, "vv", tv)
                c.wait("sync", "vv", tv)
                st = c.dma("sync", "kvw", KLOC[mq, :, tof:tof + 512], vnb[:, :])
                war[("vnb", 0)] = ("kvw", st)
        for dvc in range(4):
            v, wt, key = wload(wv[:, dvc * 512:(dvc + 1) * 512].rearrange("(kt p) m -> p kt m", p=128),
                               "v", wvpv, ncol=512)
            c.wait_pair("tensor", wt)
            for tm in range(KT):
                b, last = group(up_ring, [(a2[:, k, tm * 128:(tm + 1) * 128], v[:, k, :]) for k in range(KT)])
                if tm == KT - 1:
                    wfree(key, last)
                c.wait("scalar", "mm", last)
                c.wait_pair("scalar", war.get(("vps", 0)))
                ts = sop(lambda e, b=b: e.activation(vps[:, :], ps[:, b, :], AF.Square))
                c.wait("vector", "sc", ts)
                tv = vop(lambda e, tm=tm, dvc=dvc: e.tensor_reduce(
                    vsacc[:, tm:tm + 1, dvc * 4:(dvc + 1) * 4].rearrange("p a (x b) -> p (a x) b", b=1),
                    vps[:, :].rearrange("p (a b) -> p a b", b=128), AX.X, ALU.add))
                war[("vps", 0)] = ("vv", tv)
                c.wait_pair("scalar", war.get(("vnb", 0)))
                t = sop(lambda e, b=b: e.copy(vnb[:, :], ps[:, b, :]))
                up_ring.done(b, "sc", t)
                c.wait("sync", "sc", t)
                st = c.dma("sync", "kvw", VLOC[tm, :, dvc * 512:(dvc + 1) * 512], vnb[:, :])
                war[("vnb", 0)] = ("kvw", st)
        c.wait("scalar", "vv", c.cnt["vv"])
        for tm in range(KT):
            c.wait_pair("scalar", war.get(("vps", 0)))
            c.wait_pair("scalar", war.get(("vpsrn", 0)))
            t = sop(lambda e, tm=tm: e.activation(vps[:, 0:16], vsacc[:, tm, :], AF.Sqrt))
            c.wait("vector", "sc", t)
            vop(lambda e: e.tensor_scalar_max(vps[:, 0:16], vps[:, 0:16], float(L2_EPS)))
            tv = vop(lambda e: e.reciprocal(vps[:, 48:64], vps[:, 0:16]))
            c.wait("scalar", "vv", tv)
            t = sop(lambda e: e.activation(vps[:, 32:48], vps[:, 48:64], AF.Sqrt))
            c.wait("sync", "sc", t)
            st = c.dma("sync", "kvw", RLOC[tm, :, :], vps[:, 32:64])
            war[("vpsrn", 0)] = ("kvw", st)
            war[("vps", 0)] = ("kvw", st)
        c.wait("sync", "kvw", c.cnt["kvw"])
        c.dma("sync", "kvw", KSH[bass.ds(par, 1), :, :, :], KLOC[None, :, :, :])
        c.dma("sync", "kvw", VSH[bass.ds(par, 1), :, :, :], VLOC[None, :, :, :])
        c.dma("sync", "kvw", RNSH[bass.ds(par, 1), :, :, :], RLOC[None, :, :, :])
        c.wait("sync", "kvw", c.cnt["kvw"])
        c.dma("sync", "kvw", FKV[bass.ds(par, 1), 0:1, 0:1], nonb[0:1, 0:1][None, :, :])
        c.wait("sync", "kvw", c.cnt["kvw"])
        for mq2 in range(4):
            v, wt, key = wload(col_block(wq, mq2 * 256, 256), "w", wpv, ncol=256)
            c.wait_pair("tensor", wt)
            for mq, vsub in [(mq2 * 2 + s, v[:, :, s * 128:(s + 1) * 128]) for s in range(2)]:
              for tc in range(4):
                tof = tc * 512
                b, last = group(up_ring, [(vsub[:, k, :], a2[:, k, tof:tof + 512]) for k in range(KT)])
                if mq % 2 == 1 and tc == 3:
                    wfree(key, last)
                c.wait("scalar", "mm", last)
                c.wait_pair("scalar", war.get(("prB", 0)))
                tsq = sop(lambda e, b=b: e.activation(prB[:, :], ps[:, b, :], AF.Square))
                bsq, lasts = group(misc_ring, [(sel64[:, :], prB[:, :])], out_rows=2,
                                   extra_wait=[("sc", tsq)])
                war[("prB", 0)] = ("mm", lasts)
                c.wait_pair("scalar", war.get(("prA", 0)))
                t = sop(lambda e, b=b: e.copy(prA[:, :], ps[:, b, :]))
                up_ring.done(b, "sc", t)
                c.wait("scalar", "mm", lasts)
                t = sop(lambda e, bsq=bsq: e.activation(rnk[:, :], ps[0:2, bsq, :], AF.Sqrt))
                misc_ring.done(bsq, "sc", t)
                c.wait("vector", "sc", t)
                vop(lambda e: e.tensor_scalar_max(rnk[:, :], rnk[:, :], float(L2_EPS)))
                vop(lambda e: e.reciprocal(rnk2[:, :], rnk[:, :]))
                c.wait_pair("vector", war.get(("rnkb",)))
                tv = vop(lambda e: e.tensor_copy(rnkb[:, :], rnk2[:, :]))
                bb2, lastb2 = group(misc_ring, [(sel64T[:, :], rnkb[:, :])], extra_wait=[("vv", tv)])
                war[("rnkb",)] = ("mm", lastb2)
                c.wait("vector", "mm", lastb2)
                tv = vop(lambda e, bb2=bb2, mq=mq, tof=tof: e.tensor_mul(
                    q_sb[:, mq, tof:tof + 512], prA[:, :], ps[:, bb2, :]))
                war[("prA", 0)] = ("vv", tv)
                misc_ring.done(bb2, "vv", tv)
        for gm2 in range(8):
            v, wt, key = wload(col_block(wgt, gm2 * 256, 256), "w", wpv, ncol=256)
            c.wait_pair("tensor", wt)
            for gm, vsub in [(gm2 * 2 + s, v[:, :, s * 128:(s + 1) * 128]) for s in range(2)]:
              for tc in range(4):
                tof = tc * 512
                b, last = group(up_ring, [(vsub[:, k, :], a2[:, k, tof:tof + 512]) for k in range(KT)])
                if gm % 2 == 1 and tc == 3:
                    wfree(key, last)
                c.wait("scalar", "mm", last)
                c.wait_pair("scalar", war.get(("prD", 0)))
                t = sop(lambda e, b=b: e.activation(prC[:, :], ps[:, b, :], AF.Silu))
                up_ring.done(b, "sc", t)
                t = sop(lambda e: e.activation(prD[:, :], prC[:, :], AF.Tanh))
                c.wait("sync", "sc", t)
                st = c.dma("sync", "kvw", g2d[gm, :, tof:tof + 512], prD[:, :])
                war[("prD", 0)] = ("kvw", st)

    sim_free = {}
    attn_ring = PsumRing(c, [0, 3])

    def emit_attn_core():
        for e in ["vector", "scalar", "gpsimd", "sync"]:
            c.wait(e, "mm", c.cnt["mm"])
        c.wait("tensor", "vv", c.cnt["vv"])
        c.wait("tensor", "sc", c.cnt["sc"])
        c.wait("sync", "oh0", c.cnt["oh0"])
        c.wait("sync", "oh1", c.cnt["oh1"])
        for p2 in range(2):
            rn_src = (RLOC[:, :, :] if self_pair else RNSH[p2, :, :, :])
            c.dma("sync", "xr", srn[:, p2 * 16:(p2 + 1) * 16, :],
                  rn_src.rearrange("tm p h -> p tm h"))
        tsrn = c.cnt["xr"]
        c.wait("sync", "xr", tsrn)
        c.wait("scalar", "xr", tsrn)
        c.wait("vector", "xr", tsrn)
        kv_free = {}
        wo_last = None
        for qb in range(4):
            qof = qb * 512
            if wo_last is not None:
                c.wait("vector", "mm", wo_last)
            for hp in range(8):
                slot = hp % 2
                c.wait_pair("sync", kv_free.get(slot))
                ksv = "kv%d" % slot
                for p2 in range(2):
                    k_src = (KLOC[hp, :, :] if self_pair else KSH[p2, hp, :, :])
                    c.dma("sync", ksv, kbuf[:, slot, p2, :], k_src)
                    for j in range(2):
                        h = 2 * hp + j
                        v_src = (VLOC[:, :, h * 128:(h + 1) * 128]
                                 if self_pair else VSH[p2, :, :, h * 128:(h + 1) * 128])
                        c.dma("sync", ksv, vbuf[:, slot, j, p2 * 16:(p2 + 1) * 16, :],
                              v_src.rearrange("tm p dv -> p tm dv"))
                for j in range(2):
                    c.dma("sync", ksv, g2b[:, slot, j, :], g2d[2 * hp + j, :, qof:qof + 512])
                tkv = c.cnt[ksv]
                c.wait("tensor", ksv, tkv)
                c.wait("vector", ksv, tkv)
                c.wait("scalar", ksv, tkv)
                agg_last = [None, None]
                sq_t = {}
                PIPE = 2
                for it in range(32 + PIPE):
                    if it < 32:
                        kb = it
                        for j in range(2):
                            h = 2 * hp + j
                            bsim = j * 3 + kb % 3
                            c.wait_pair("tensor", sim_free.get(bsim))
                            if bsim in (0, 3):
                                fa_ = attn_ring.free_at.get(bsim)
                                if fa_ is not None:
                                    c.wait_pair("tensor", fa_)
                            def fsim(eng, j=j, kb=kb, slot=slot, hp=hp, qof=qof, bsim=bsim):
                                return eng.matmul(
                                    ps[:, bsim, :],
                                    kbuf[j * 64:(j + 1) * 64, slot, kb // 16, (kb % 16) * 128:(kb % 16) * 128 + 128],
                                    q_sb[j * 64:(j + 1) * 64, hp, qof:qof + 512],
                                    start=True, stop=True, tile_position=(j * 64, 0))
                            lsim = mmop(fsim)
                            use_sc_sq = ((kb + j) % 2 == 0)
                            if j == 0:
                                c.wait("scalar", "mm", lsim)
                                c.wait_pair("scalar", war.get(("rt", j, kb % 2)))
                                trl = sop(lambda e, j=j, kb=kb, h=h, bsim=bsim: e.activation(
                                    rt[:, j, kb % 2, :].bitcast(BF16)[:, 0:512], ps[:, bsim, :],
                                    AF.Relu, bias=b_all[:, h:h + 1]))
                                sim_free[bsim] = ("sc", trl)
                                if use_sc_sq:
                                    c.wait_pair("scalar", war.get(("at", j, kb % 4)))
                                    tq = sop(lambda e, j=j, kb=kb, h=h: e.activation(
                                        at[:, j, kb % 4, :], rt[:, j, kb % 2, :].bitcast(BF16)[:, 0:512],
                                        AF.Square, scale=srn[:, kb, h:h + 1]), sync=False)
                                    war[("rt", j, kb % 2)] = ("sc", tq)
                                    sq_t[(kb, j)] = ("sc", tq)
                                else:
                                    c.wait("vector", "sc", trl)
                                    c.wait_pair("vector", war.get(("at", j, kb % 4)))
                                    tq = vop(lambda e, j=j, kb=kb, h=h: e.scalar_tensor_tensor(
                                        at[:, j, kb % 4, :], rt[:, j, kb % 2, :].bitcast(BF16)[:, 0:512],
                                        srn[:, kb, 16 + h:16 + h + 1],
                                        rt[:, j, kb % 2, :].bitcast(BF16)[:, 0:512], ALU.mult, ALU.mult), sync=False)
                                    war[("rt", j, kb % 2)] = ("vv", tq)
                                    sq_t[(kb, j)] = ("vv", tq)
                            else:
                                c.wait("vector", "mm", lsim)
                                c.wait_pair("vector", war.get(("rt", j, kb % 2)))
                                trl = vop(lambda e, j=j, kb=kb, h=h, bsim=bsim: e.tensor_scalar(
                                    rt[:, j, kb % 2, :].bitcast(BF16)[:, 0:512], ps[:, bsim, :],
                                    b_all[:, h:h + 1], 0.0, ALU.add, ALU.max), sync=False)
                                sim_free[bsim] = ("vv", trl)
                                if use_sc_sq:
                                    c.wait("scalar", "vv", trl)
                                    c.wait_pair("scalar", war.get(("at", j, kb % 4)))
                                    tq = sop(lambda e, j=j, kb=kb, h=h: e.activation(
                                        at[:, j, kb % 4, :], rt[:, j, kb % 2, :].bitcast(BF16)[:, 0:512],
                                        AF.Square, scale=srn[:, kb, h:h + 1]), sync=False)
                                    war[("rt", j, kb % 2)] = ("sc", tq)
                                    sq_t[(kb, j)] = ("sc", tq)
                                else:
                                    c.wait("vector", "vv", trl)
                                    c.wait_pair("vector", war.get(("at", j, kb % 4)))
                                    tq = vop(lambda e, j=j, kb=kb, h=h: e.scalar_tensor_tensor(
                                        at[:, j, kb % 4, :], rt[:, j, kb % 2, :].bitcast(BF16)[:, 0:512],
                                        srn[:, kb, 16 + h:16 + h + 1],
                                        rt[:, j, kb % 2, :].bitcast(BF16)[:, 0:512], ALU.mult, ALU.mult), sync=False)
                                    war[("rt", j, kb % 2)] = ("vv", tq)
                                    sq_t[(kb, j)] = ("vv", tq)
                    if it >= PIPE:
                        kb = it - PIPE
                        for j in range(2):
                            c.wait_pair("tensor", sq_t[(kb, j)])
                            if kb == 0:
                                c.wait_pair("tensor", sim_free.get(100 + j))
                            def fagg(eng, j=j, kb=kb, slot=slot):
                                return eng.matmul(ps[:, 6 + j, :], vbuf[:, slot, j, kb, :],
                                                  at[:, j, kb % 4, :], start=(kb == 0), stop=(kb == 31))
                            tagg = mmop(fagg)
                            if kb == 31:
                                agg_last[j] = tagg
                            war[("at", j, kb % 4)] = ("mm", tagg)
                ep_last = None
                c.wait_pair("tensor", sim_free.get(0))
                c.wait_pair("tensor", sim_free.get(3))
                for j in range(2):
                    h = 2 * hp + j
                    c.wait("scalar", "mm", agg_last[j])
                    c.wait("scalar", "vv", c.cnt["vv"])
                    c.wait_pair("scalar", war.get(("sqe", 0)))
                    c.wait_pair("scalar", war.get(("arwC",)))
                    tsq = sop(lambda e, j=j: e.activation(sqe[:, :], ps[:, 6 + j, :], AF.Square))
                    bsq, lasts = group(attn_ring, [(onescol[:, :], sqe[:, :])], out_rows=1,
                                       extra_wait=[("sc", tsq)])
                    war[("sqe", 0)] = ("mm", lasts)
                    c.wait_pair("scalar", war.get(("af", j)))
                    t = sop(lambda e, j=j: e.copy(af[:, j, :], ps[:, 6 + j, :]))
                    sim_free[100 + j] = ("sc", t)
                    c.wait("scalar", "mm", lasts)
                    t = sop(lambda e, bsq=bsq: e.activation(arwA[:, :], ps[0:1, bsq, :], AF.Sqrt))
                    attn_ring.done(bsq, "sc", t)
                    c.wait("vector", "sc", t)
                    vop(lambda e: e.tensor_scalar_max(arwB[:, :], arwA[:, :], float(L2_EPS)))
                    c.wait_pair("vector", war.get(("arwC",)))
                    tv = vop(lambda e: e.reciprocal(arwC[:, :], arwB[:, :]))
                    c.wait("scalar", "vv", tv)
                    t = sop(lambda e: e.activation(arwB[:, :], arwA[:, :], AF.Tanh))
                    c.wait("vector", "sc", t)
                    vop(lambda e: e.tensor_mul(arwC[:, :], arwC[:, :], arwB[:, :]))
                    tv = vop(lambda e, h=h: e.tensor_scalar_mul(arwC[:, :], arwC[:, :], ehws[0:1, h:h + 1]))
                    bb2, lastb2 = group(attn_ring, [(ones1f[:, :], arwC[:, :])], extra_wait=[("vv", tv)])
                    war[("arwC",)] = ("mm", lastb2)
                    c.wait("vector", "mm", lastb2)
                    vop(lambda e, j=j, bb2=bb2: e.tensor_mul(aft[:, :], af[:, j, :], ps[:, bb2, :]))
                    war[("af", j)] = ("vv", c.cnt["vv"])
                    ep_last = vop(lambda e, j=j, h=h, slot=slot: e.tensor_mul(
                        osc[:, h, :], aft[:, :], g2b[:, slot, j, :]))
                    kv_free[slot] = ("vv", ep_last)
                    attn_ring.done(bb2, "vv", ep_last)
                kv_free[slot] = ("vv", ep_last)
                war[("osc",)] = ("vv", ep_last)
            for bb_ in range(6):
                c.wait_pair("tensor", sim_free.get(bb_))
            c.wait_pair("tensor", sim_free.get(100))
            c.wait_pair("tensor", sim_free.get(101))
            bs = dn_ring.take()
            bs2 = dn_ring.take()
            l2 = None
            for mt2 in range(8):
                v, wt, key = wload(col_block(wo, mt2 * 256, 256), "w", wpv, ncol=256)
                c.wait_pair("tensor", wt)
                for mt, vsub in [(mt2 * 2 + s, v[:, :, s * 128:(s + 1) * 128]) for s in range(2)]:
                  b, last = group(up_ring, [(vsub[:, k, :], osc[:, k, :]) for k in range(KT)],
                                  extra_wait=[war.get(("osc",))] if war.get(("osc",)) else [])
                  if mt % 2 == 1:
                      wfree(key, last)
                  wo_last = last
                  slot = mt % 2
                  tl = xload(xres1[mt * 128:(mt + 1) * 128, qof:qof + 512], slot)
                  c.wait_pair("vector", tl)
                  c.wait("vector", "mm", last)
                  c.wait_pair("vector", war.get(("outh", slot)))
                  c.wait_pair("vector", war.get(("outhc", slot)))
                  free_tgt = vop(lambda e, b=b, slot=slot: e.tensor_add(
                      outh[:, slot, :], xbuf[:, slot, :], ps[:, b, :]))
                  war[("xbuf", slot)] = ("vv", free_tgt)
                  up_ring.done(b, "vv", free_tgt)
                  c.wait("scalar", "vv", free_tgt)
                  ts2 = cast_pair(slot, outh[:, slot, :], "statmm")
                  war[("outhc", slot)] = ("sc", ts2)
                  c.wait("tensor", "sc", ts2)
                  def f1(eng, slot=slot, mt=mt, bs=bs):
                      return eng.matmul(ps[0:1, bs, :], onescol[:, :], xb16[:, slot, :],
                                        start=(mt == 0), stop=(mt == KT - 1))
                  def f2(eng, slot=slot, mt=mt, bs2=bs2):
                      return eng.matmul(ps[0:1, bs2, :], onescol[:, :], sqx[:, slot, :],
                                        start=(mt == 0), stop=(mt == KT - 1))
                  mmop(f1)
                  tf2 = mmop(f2)
                  if mt == KT - 1:
                      l2 = tf2
                  war[("statmm", slot)] = ("mm", tf2)
                  c.wait("sync", "vv", free_tgt)
                  st = c.dma("sync", "oh%d" % slot, xres2[mt * 128:(mt + 1) * 128, qof:qof + 512], outh[:, slot, :])
                  war[("outh", slot)] = ("oh%d" % slot, st)
            c.wait("vector", "mm", l2)
            c.wait_pair("vector", war.get(("rowS",)))
            v1 = vop(lambda e, bs=bs: e.tensor_copy(rowS[:, :], ps[0:1, bs, :]))
            v2 = vop(lambda e, bs2=bs2: e.tensor_copy(rowS2[:, :], ps[0:1, bs2, :]))
            dn_ring.done(bs, "vv", v1)
            dn_ring.done(bs2, "vv", v2)
            c.wait("sync", "vv", v2)
            c.dma("sync", "sd", statd[2, 0, qb, :][None, :], rowS[:, :])
            war[("rowS",)] = ("sd", c.dma("sync", "sd", statd[2, 1, qb, :][None, :], rowS2[:, :]))

    def emit_poll(flag_t, psem_name, tag):
        loop = f"poll_{tag}_loop"
        done = f"poll_{tag}_done"
        sy = nc.sync
        (f_r,) = list(nc.alloc_registers(f"pf_{tag}", engines=[SP]))
        (w_r,) = list(nc.alloc_registers(f"pw_{tag}", engines=[SP]))
        (n_r,) = list(nc.alloc_registers(f"pn_{tag}", engines=[SP]))
        sy.reg_mov(w_r, 0)
        sy.reg_load(n_r, nonb[0:1, 0:1])
        nc.br(loop, engines=[SP])
        with nc.body(loop, valid_engines=[SP]):
            sy.dma_start(out=fsb[0:1, 0:1], in_=flag_t[bass.ds(opar, 1), 0:1, 0:1]).then_inc(c.sem[psem_name], 16)
            sy.reg_alu(w_r, w_r, 16, ALU.add)
            sy.wait_ge(c.sem[psem_name], w_r)
            sy.reg_load(f_r, fsb[0:1, 0:1])
            sy.reg_alu(f_r, f_r, n_r, ALU.bitwise_xor)
            sy.reg_alu(f_r, f_r, 0x7FFFFFFF, ALU.bitwise_and)
            nc.br_lt(f_r, 1, on_true=done, on_false=loop, engines=[SP])
        nc.switch_bb(done)

    with nc.Block() as block:
        emit_init()
        emit_stats(x0, 0)
        emit_ffn_call(x0, 0, xres1, 1)
        if stages >= 2:
            emit_attn_proj()
        else:
            for s_ in ["xb0", "xb1", "oh0", "oh1", "sd"]:
                c.wait("sync", s_, c.cnt[s_])
            c.dma("sync", "kvw", out_d, xres1)
            c.wait("sync", "kvw", c.cnt["kvw"])
        for s_ in ["xb0", "xb1", "oh0", "oh1", "sd", "kvw", "xr"]:
            c.wait("sync", s_, c.cnt[s_])
        c.run(block)

    if stages >= 2:
        emit_poll(FKV, "p1", "kv")
        with nc.Block() as block:
            emit_attn_core()
            if stages >= 3:
                for s_ in ["kv0", "kv1", "xr"]:
                    c.wait("sync", s_, c.cnt[s_])
                c.dma("sync", "kvw", FDN[bass.ds(par, 1), 0:1, 0:1], nonb[0:1, 0:1][None, :, :])
                emit_ffn_call(xres2, 2, out_d, None)
            else:
                c.wait("sync", "vv", c.cnt["vv"])
                c.dma("sync", "kvw", out_d, xres2)
            if debug:
                c.dma("sync", "kvw", dbg1, xres1)
                c.dma("sync", "kvw", dbg2, xres2)
            if stages < 3:
                c.dma("sync", "kvw", FDN[bass.ds(par, 1), 0:1, 0:1], nonb[0:1, 0:1][None, :, :])
            for s_ in ["xb0", "xb1", "oh0", "oh1", "sd", "kvw", "xr", "kv0", "kv1"]:
                c.wait("sync", s_, c.cnt[s_])
            c.run(block)
        emit_poll(FDN, "p2", "dn")
    return nc



def _prep(inputs):
    p = {}
    x = np.asarray(inputs["L0_attn_tokens"], np.float32).reshape(-1, DM)
    qg = np.asarray(inputs["L0_attn_Wqg"], np.float32).reshape(DM, 16, 192)
    kv = np.asarray(inputs["L0_attn_Wkv"], np.float32).reshape(DM, 16, 192)
    p["wq"] = np.ascontiguousarray(qg[:, :, :64].reshape(DM, 1024))
    p["wgt"] = np.ascontiguousarray(qg[:, :, 64:].reshape(DM, DM))
    p["wk"] = np.ascontiguousarray(kv[:, :, :64].reshape(DM, 1024))
    p["wv"] = np.ascontiguousarray(kv[:, :, 64:].reshape(DM, DM))
    p["wo"] = np.ascontiguousarray(np.asarray(inputs["L0_attn_Wo"], np.float32))
    p["ls_iaw"] = np.asarray(inputs["L0_attn_ls_iaw"], np.float32).reshape(1, 16)
    p["ls_hws"] = np.asarray(inputs["L0_attn_ls_hws"], np.float32).reshape(1, 16)
    wg = np.asarray(inputs["L0_ffn_Wg"], np.float32)
    p["wgp"] = np.ascontiguousarray(wg.reshape(2, 16, 128, 8).transpose(2, 1, 0, 3))
    p["bgr"] = np.ascontiguousarray(np.asarray(inputs["L0_ffn_bg"], np.float32).reshape(1, 16))
    p["wup"] = np.ascontiguousarray(np.asarray(inputs["L0_ffn_Wup"], np.float32))
    p["wsw"] = np.ascontiguousarray(np.asarray(inputs["L0_ffn_Wsw"], np.float32))
    p["wdn"] = np.ascontiguousarray(np.asarray(inputs["L0_ffn_Wdn"], np.float32))
    p["wsu"] = np.ascontiguousarray(np.asarray(inputs["L0_ffn_Wsu"], np.float32))
    p["wsa"] = np.ascontiguousarray(np.asarray(inputs["L0_ffn_Wsa"], np.float32))
    p["wsd"] = np.ascontiguousarray(np.asarray(inputs["L0_ffn_Wsd"], np.float32))
    p["bupp"] = np.ascontiguousarray(np.asarray(inputs["L0_ffn_bup"], np.float32).reshape(NEXP, 8, 128).transpose(2, 0, 1))
    p["bswp"] = np.ascontiguousarray(np.asarray(inputs["L0_ffn_bsw"], np.float32).reshape(NEXP, 8, 128).transpose(2, 0, 1))
    p["bdn2"] = np.ascontiguousarray(np.asarray(inputs["L0_ffn_bdn"], np.float32))
    p["bsup"] = np.ascontiguousarray(np.asarray(inputs["L0_ffn_bsu"], np.float32).reshape(16, 128).T)
    p["bsap"] = np.ascontiguousarray(np.asarray(inputs["L0_ffn_bsa"], np.float32).reshape(8, 128).T)
    p["bsdp"] = np.ascontiguousarray(np.asarray(inputs["L0_ffn_bsd"], np.float32).reshape(16, 128).T)
    sel = np.zeros((128, 2), np.float32)
    sel[:64, 0] = 1.0
    sel[64:, 1] = 1.0
    p["csel64"] = sel
    p["csel64T"] = np.ascontiguousarray(sel.T)
    p["cones1"] = np.ones((1, 128), np.float32)
    p["cones1f"] = np.ones((1, 128), np.float32)
    p["conescol"] = np.ones((128, 1), np.float32)
    p["cident"] = np.eye(128, dtype=np.float32)
    p["ceps"] = np.full((1, 1), 1e-5, np.float32)
    return x, p


def kernel(**inputs):
    x, p = _prep(inputs)
    nonce = np.array([[int(time.time_ns() // 1000) & 0x3FFFFFFF]], dtype=np.uint32)
    nc = build_nc(STAGES, DEBUG)
    in_maps = []
    for cid in range(8):
        m = dict(p)
        m["x0"] = np.ascontiguousarray(x[cid * TOK:(cid + 1) * TOK, :].T)
        m["nonce"] = nonce
        in_maps.append(m)
    res = run_bass_kernel_spmd(nc, in_maps, core_ids=list(range(8)))
    kernel._last_res = res
    outs = [np.asarray(res.results[cid]["out"]).T for cid in range(8)]
    return np.concatenate(outs, axis=0).reshape(4, 4096, DM).astype(np.float32)


# version 06 (final)
# speedup vs baseline: 2.0569x; 1.0446x over previous
import time
import numpy as np
import ml_dtypes
import concourse.bass as bass
import concourse.mybir as mybir
from concourse.bass_utils import run_bass_kernel_spmd

F32 = mybir.dt.float32
BF16 = mybir.dt.bfloat16
U32 = mybir.dt.uint32
AF = mybir.ActivationFunctionType
ALU = mybir.AluOpType
AX = mybir.AxisListType
SP = mybir.EngineType.SP

DM = 2048
TOK = 2048
SLAB = 1024
KT = 16
NEXP = 6
LN_EPS = 1e-5
L2_EPS = 1e-12
BIG = 1e30
DEBUG = False
STAGES = 3


class Ctx:
    ENG = ["tensor", "scalar", "vector", "gpsimd", "sync"]

    def __init__(self, nc):
        self.nc = nc
        self.sem = {}
        self.cnt = {}
        self.waited = {}
        for name in ["w0", "w1", "v0", "v1", "xb0", "xb1", "oh0", "oh1",
                     "sd", "xr", "xi", "kv0", "kv1", "kvw", "mm", "sc", "vv", "pl", "p1", "p2"]:
            self.sem[name] = nc.alloc_semaphore(name="s_" + name)
            self.cnt[name] = 0
        self.stream = {e: [] for e in self.ENG}

    def wait(self, ename, sname, target):
        if target is None or target <= 0:
            return
        key = (ename, sname)
        if self.waited.get(key, 0) >= target:
            return
        self.waited[key] = target
        sem = self.sem[sname]
        self.stream[ename].append(lambda eng, s=sem, t=target: eng.wait_ge(s, t))

    def wait_pair(self, ename, pair):
        if pair is not None:
            self.wait(ename, pair[0], pair[1])

    def op(self, ename, fn, inc=None):
        if inc is None:
            self.stream[ename].append(fn)
            return None
        sem = self.sem[inc]
        self.cnt[inc] += 1
        tgt = self.cnt[inc]
        def wrapped(eng, fn=fn, sem=sem):
            fn(eng).then_inc(sem, 1)
        self.stream[ename].append(wrapped)
        return tgt

    def dma(self, ename, sname, out, in_):
        sem = self.sem[sname]
        self.cnt[sname] += 16
        tgt = self.cnt[sname]
        def fn(eng, out=out, in_=in_, sem=sem):
            eng.dma_start(out=out, in_=in_).then_inc(sem, 16)
        self.stream[ename].append(fn)
        return tgt

    def barrier_waits(self, ename):
        for s in ["mm", "sc", "vv"]:
            self.wait(ename, s, self.cnt[s])

    def run(self, block):
        decos = {"tensor": block.tensor, "scalar": block.scalar,
                 "vector": block.vector, "gpsimd": block.gpsimd, "sync": block.sync}
        for ename in self.ENG:
            ops = self.stream[ename]
            if not ops:
                continue
            def body(eng, ops=ops):
                for f in ops:
                    f(eng)
            decos[ename](body)
        self.stream = {e: [] for e in self.ENG}


class PsumRing:
    def __init__(self, ctx, banks):
        self.ctx = ctx
        self.banks = list(banks)
        self.i = 0
        self.free_at = {b: None for b in self.banks}

    def take(self, engine="tensor"):
        b = self.banks[self.i % len(self.banks)]
        self.i += 1
        fa = self.free_at[b]
        if fa is not None:
            self.ctx.wait(engine, fa[0], fa[1])
        return b

    def done(self, b, sname, target):
        self.free_at[b] = (sname, target)


def build_nc(stages=STAGES, debug=DEBUG, self_pair=False):
    nc = bass.Bass("TRN2", target_bir_lowering=False)
    c = Ctx(nc)

    def I(name, shape, dt=F32):
        return nc.dram_tensor(name, shape, dt, kind="ExternalInput").ap()

    x0 = I("x0", [DM, TOK])
    wq = I("wq", [DM, 1024]); wk = I("wk", [DM, 1024])
    wv = I("wv", [DM, DM]); wgt = I("wgt", [DM, DM]); wo = I("wo", [DM, DM])
    ls_iaw = I("ls_iaw", [1, 16]); ls_hws = I("ls_hws", [1, 16])
    wgp = I("wgp", [128, KT, 2, 8]); bgr = I("bgr", [1, 16])
    wup = I("wup", [NEXP, DM, 1024]); wsw = I("wsw", [NEXP, 1024, 1024])
    wdn = I("wdn", [NEXP, 1024, DM])
    wsu = I("wsu", [DM, DM]); wsa = I("wsa", [1024, 1024]); wsd = I("wsd", [DM, DM])
    bupp = I("bupp", [128, NEXP, 8]); bswp = I("bswp", [128, NEXP, 8])
    bdn2 = I("bdn2", [NEXP, DM]); bsup = I("bsup", [128, 16]); bsap = I("bsap", [128, 8])
    bsdp = I("bsdp", [128, 16])
    csel64 = I("csel64", [128, 2]); csel64T = I("csel64T", [2, 128])
    cones1 = I("cones1", [1, 128]); cones1f = I("cones1f", [1, 128])
    conescol = I("conescol", [128, 1]); cident = I("cident", [128, 128])
    nonce_in = I("nonce", [1, 1], U32)
    ceps = I("ceps", [1, 1])

    out_d = nc.dram_tensor("out", [DM, TOK], F32, kind="ExternalOutput").ap()
    dbg1 = dbg2 = None
    if debug:
        dbg1 = nc.dram_tensor("dbg1", [DM, TOK], F32, kind="ExternalOutput").ap()
        dbg2 = nc.dram_tensor("dbg2", [DM, TOK], F32, kind="ExternalOutput").ap()

    xres1 = nc.dram_tensor("xres1", [DM, TOK], F32, kind="Internal").ap()
    xres2 = nc.dram_tensor("xres2", [DM, TOK], F32, kind="Internal").ap()
    g2d = nc.dram_tensor("g2d", [16, 128, TOK], BF16, kind="Internal").ap()
    rdram = nc.dram_tensor("rdram", [1, 16], F32, kind="Internal").ap()
    statd = nc.dram_tensor("statd", [3, 2, 4, 512], F32, kind="Internal").ap()
    KSH = nc.dram_tensor("KSH", [2, 8, 128, TOK], BF16, kind="Internal", addr_space="Shared").ap()
    VSH = nc.dram_tensor("VSH", [2, 16, 128, DM], BF16, kind="Internal", addr_space="Shared").ap()
    RNSH = nc.dram_tensor("RNSH", [2, 16, 128, 32], F32, kind="Internal", addr_space="Shared").ap()
    KLOC = nc.dram_tensor("KLOC", [8, 128, TOK], BF16, kind="Internal").ap()
    VLOC = nc.dram_tensor("VLOC", [16, 128, DM], BF16, kind="Internal").ap()
    RLOC = nc.dram_tensor("RLOC", [16, 128, 32], F32, kind="Internal").ap()
    FKV = nc.dram_tensor("FKV", [2, 1, 8], U32, kind="Internal", addr_space="Shared").ap()
    FDN = nc.dram_tensor("FDN", [2, 1, 8], U32, kind="Internal", addr_space="Shared").ap()

    sb = lambda n, s, d: nc.alloc_sbuf_tensor(n, s, d)
    arena = sb("arena", [128, 128 * 1024], mybir.dt.uint8)
    abase = nc.lookup_mloc(arena).addr

    def ar(name, shape, dtype, off_kb):
        return nc.alloc_sbuf_tensor_at(name, shape, dtype, offset=int(abase + off_kb * 1024))

    a_b = ar("a_b", [128, KT, SLAB], BF16, 0)
    acc = ar("acc", [128, KT, SLAB], F32, 32)
    B0 = ar("B0", [128, 8, SLAB], BF16, 96)
    B1 = ar("B1", [128, 8, SLAB], BF16, 112)
    Bd = [B0, B1]
    sh_h = ar("sh_h", [128, KT, 512], BF16, 96)
    h2sh = ar("h2sh", [128, 8, 512], BF16, 112)
    h2sh2 = ar("h2sh2", [128, 8, 512], BF16, 120)
    a2 = ar("a2", [128, KT, TOK], BF16, 0)
    wvp = ar("wvp", [128, 2, KT, 512], BF16, 64)
    q_sb = ar("q_sb", [128, 8, TOK], BF16, 64)
    kbuf = ar("kbuf", [128, 2, 2, TOK], BF16, 0)
    vbuf = ar("vbuf", [128, 2, 2, 32, 128], BF16, 16)
    osc = ar("osc", [128, 16, 512], BF16, 48)
    rt = ar("rt", [128, 2, 2, 512], F32, 96)
    at = ar("at", [128, 2, 4, 512], BF16, 104)
    af = ar("af", [128, 2, 512], F32, 112)
    g2b = ar("g2b", [128, 2, 2, 512], BF16, 116)
    srn = ar("srn", [128, 32, 32], F32, 120)
    sqe = ar("sqe", [128, 512], BF16, 124)
    aft = ar("aft", [128, 512], F32, 125)
    vsacc = ar("vsacc", [128, 16, 16], F32, 105)
    vps = ar("vps", [128, 512], F32, 102)
    vnb = ar("vnb", [128, 512], BF16, 106)
    prA = ar("prA", [128, 512], F32, 96)
    prB = ar("prB", [128, 512], BF16, 98)
    prC = ar("prC", [128, 512], F32, 99)
    prD = ar("prD", [128, 512], BF16, 101)

    wpool = sb("wpool", [128, 2, KT * 256], BF16)
    xbuf = sb("xbuf", [128, 2, 512], F32)
    t1b = sb("t1b", [128, 512], F32)
    t1c = sb("t1c", [128, 512], F32)
    outh = sb("outh", [128, 2, 512], F32)
    xb16 = sb("xb16", [128, 2, 512], BF16)
    sqx = sb("sqx", [128, 2, 512], BF16)
    gate_f = sb("gate_f", [128, 64], F32)
    tk = {n: sb("tk_" + n, [128, 64], F32) for n in ["t8", "g1", "g2t", "k8", "d8", "e8", "em8"]}
    w_tm = sb("w_tm", [128, 64], F32)
    tkr = {n: sb("tkr_" + n, [128, 8], F32) for n in ["m1", "m2", "m3", "nmx", "gz", "gr"]}
    w_fm = sb("w_fm", [8, 8, 128], BF16)
    w_bc = sb("w_bc", [128, SLAB], BF16)
    wrow = sb("wrow", [1, SLAB], BF16)
    bgb = sb("bgb", [128, 16], F32)
    wg_sb = sb("wg_sb", [128, KT, 2, 8], BF16)
    bup_sb = sb("bup_sb", [128, NEXP, 8], F32)
    bsw_sb = sb("bsw_sb", [128, NEXP, 8], F32)
    bdn_sb = sb("bdn_sb", [NEXP, DM], BF16)
    bsu_sb = sb("bsu_sb", [128, 16], F32)
    bsa_sb = sb("bsa_sb", [128, 8], F32)
    bsd_sb = sb("bsd_sb", [128, 16], F32)
    sel64 = sb("sel64", [128, 2], BF16)
    sel64T = sb("sel64T", [2, 128], BF16)
    ones1 = sb("ones1", [1, 128], BF16)
    ones1f = sb("ones1f", [1, 128], F32)
    onescol = sb("onescol", [128, 1], BF16)
    ident_f = sb("ident_f", [128, 128], F32)
    rra = sb("rra", [1, 16], F32)
    rrb = sb("rrb", [1, 16], F32)
    rrc = sb("rrc", [1, 16], F32)
    r_cols = sb("r_cols", [2, 8], F32)
    ehws = sb("ehws", [1, 16], F32)
    b_all = sb("b_all", [128, 16], F32)
    rowS = sb("rowS", [1, 512], F32)
    rowS2 = sb("rowS2", [1, 512], F32)
    rowU = sb("rowU", [1, 512], F32)
    rowB = sb("rowB", [1, 512], F32)
    rowT = sb("rowT", [1, 512], F32)
    rowT2 = sb("rowT2", [1, 512], F32)
    ubc_sb = sb("ubc_sb", [128, 2, 512], F32)
    bbc_sb = sb("bbc_sb", [128, 2, 512], F32)
    arwA = sb("arwA", [1, 512], F32)
    arwB = sb("arwB", [1, 512], F32)
    arwC = sb("arwC", [1, 512], F32)
    rnk = sb("rnk", [2, 512], F32)
    rnk2 = sb("rnk2", [2, 512], F32)
    rnkb = sb("rnkb", [2, 512], BF16)
    epst = sb("epst", [1, 1], F32)
    nonb = sb("nonb", [1, 8], U32)
    fsb = sb("fsb", [1, 8], U32)

    ps = nc.alloc_psum_tensor("ps", [128, 8, 512], F32)

    wslot_free = {}
    wload_counter = {}

    def wload(src_ap, slot_group, dst_views, nkt=None, ncol=128):
        idx = wload_counter.setdefault(slot_group, 0)
        wload_counter[slot_group] = idx + 1
        slot = idx % 2
        key = (slot_group, slot)
        wm = wslot_free.get(key)
        if wm is not None:
            c.wait("gpsimd", wm[0], wm[1])
        view = dst_views[slot]
        if nkt is not None and view.shape[1] != nkt:
            view = view[:, 0:nkt, :]
        if view.shape[2] != ncol:
            view = view[:, :, 0:ncol]
        sname = ("w" if slot_group == "w" else "v") + str(slot)
        tgt = c.dma("gpsimd", sname, view, src_ap)
        return view, (sname, tgt), key

    def wfree(key, mm_target):
        wslot_free[key] = ("mm", mm_target)

    wpv = [wpool[:, s, :].rearrange("p (kt m) -> p kt m", m=256) for s in range(2)]
    wvpv = [wvp[:, s, :, :] for s in range(2)]

    def vop(fn, sync=True):
        t = c.op("vector", fn, inc="vv")
        if sync:
            c.wait("vector", "vv", t)
        return t

    def sop(fn, sync=True):
        t = c.op("scalar", fn, inc="sc")
        if sync:
            c.wait("scalar", "sc", t)
        return t

    def mmop(fn):
        return c.op("tensor", fn, inc="mm")

    def pop_(fn, sync=True):
        t = c.op("gpsimd", fn, inc="pl")
        if sync:
            c.wait("gpsimd", "pl", t)
        return t

    up_ring = PsumRing(c, [0, 1])
    sw_ring = PsumRing(c, [2, 3])
    dn_ring = PsumRing(c, [4, 5])
    misc_ring = PsumRing(c, [6, 7])

    def group(ring, mms, extra_wait=(), out_cols=None, out_rows=128):
        b = ring.take()
        for sname, t in extra_wait:
            c.wait("tensor", sname, t)
        n = len(mms)
        last = None
        for i, (lh, rh) in enumerate(mms):
            o = ps[0:out_rows, b, :] if out_cols is None else ps[0:out_rows, b, out_cols[0]:out_cols[0] + out_cols[1]]
            def f(eng, lh=lh, rh=rh, i=i, n=n, o=o):
                return eng.matmul(o, lh, rh, start=(i == 0), stop=(i == n - 1))
            if i == n - 1:
                last = mmop(f)
            else:
                c.op("tensor", f)
        return b, last

    def seg8(tns):
        return tns[:, :].rearrange("p (a b) -> p a b", b=8)

    def colb8(tns):
        v = tns[:, :].rearrange("p (a b) -> p a b", b=1)
        return bass.AP(v.tensor, v.offset, [list(v.ap[0]), list(v.ap[1]), [0, 8]])

    nc.const_aps.register(F32, float(LN_EPS), epst[0:1, 0:1])
    pid = nc.partition_id(engines=[SP])
    par = pid & 1
    opar = par if self_pair else ((pid + 1) & 1)

    war = {}

    def col_block(w2d, col0, ncol=128):
        return w2d[:, col0:col0 + ncol].rearrange("(kt p) m -> p kt m", p=128)

    def emit_init():
        for dst, src in [(bup_sb[:, :, :], bupp), (bsw_sb[:, :, :], bswp),
                         (bsu_sb[:, :], bsup), (bsa_sb[:, :], bsap), (bsd_sb[:, :], bsdp),
                         (rra[:, :], ls_iaw), (rrb[:, :], ls_hws),
                         (nonb[0:1, 0:1], nonce_in), (ones1f[:, :], cones1f),
                         (epst[:, :], ceps),
                         (ident_f[:, :], cident), (rowT[0:1, 0:16], bgr)]:
            c.dma("sync", "xi", dst, src)
        tload = c.cnt["xi"]
        c.wait("sync", "xi", tload)
        for dst, src in [(sel64[:, :], csel64), (sel64T[:, :], csel64T),
                         (ones1[:, :], cones1), (onescol[:, :], conescol),
                         (wg_sb[:, :, :, :], wgp), (bdn_sb[:, :], bdn2)]:
            c.dma("gpsimd", "w0", dst, src)
        tcast = c.cnt["w0"]
        c.wait("gpsimd", "w0", tcast)
        for e in ["scalar", "vector", "tensor"]:
            c.wait(e, "xi", tload)
        c.wait("tensor", "w0", tcast)
        t = sop(lambda e: e.activation(rrc[:, :], rra[:, :], AF.Exp))
        c.wait("vector", "sc", t)
        vop(lambda e: e.tensor_scalar(rrc[:, :], rrc[:, :], 1.0, None, ALU.add))
        vop(lambda e: e.reciprocal(rra[:, :], rrc[:, :]))
        tv = vop(lambda e: e.tensor_scalar(rrc[:, :], rra[:, :], -1.0, 1.0, ALU.mult, ALU.add))
        c.wait("scalar", "vv", tv)
        sop(lambda e: e.activation(ehws[:, :], rrb[:, :], AF.Exp))
        c.wait("sync", "vv", tv)
        c.dma("sync", "xi", rdram, rra[:, :])
        c.wait("sync", "xi", c.cnt["xi"])
        c.cnt["xi"] += 16; tr = c.cnt["xi"]
        def _rl(eng, sem=c.sem["xi"]):
            eng.dma_start(out=r_cols[:, :], in_=rdram.rearrange("a (t j) -> (a j) t", j=2),
                          allow_slow_non_contiguous=True).then_inc(sem, 16)
        c.stream["sync"].append(_rl)
        c.wait("sync", "xi", tr)
        c.wait("tensor", "xi", tr)
        b, last = group(misc_ring, [(ones1f[:, :], rrc[:, :])], extra_wait=[("vv", tv)], out_cols=(0, 16))
        c.wait("vector", "mm", last)
        t = vop(lambda e, b=b: e.tensor_copy(b_all[:, :], ps[:, b, 0:16]))
        misc_ring.done(b, "vv", t)
        b, last = group(misc_ring, [(ones1f[:, :], rowT[0:1, 0:16])], out_cols=(0, 16))
        c.wait("vector", "mm", last)
        t = vop(lambda e, b=b: e.tensor_copy(bgb[:, :], ps[:, b, 0:16]))
        misc_ring.done(b, "vv", t)

    def ln_rows(ln_idx, tcg):
        c.wait("sync", "vv", c.cnt["vv"])
        c.wait("sync", "mm", c.cnt["mm"])
        c.wait("sync", "sd", c.cnt["sd"])
        c.dma("sync", "xr", rowS[:, :], statd[ln_idx, 0, tcg, :][None, :])
        t2 = c.dma("sync", "xr", rowS2[:, :], statd[ln_idx, 1, tcg, :][None, :])
        c.wait("sync", "xr", t2)
        c.wait("vector", "xr", t2)
        vop(lambda e: e.tensor_scalar_mul(rowS[:, :], rowS[:, :], 1.0 / DM))
        vop(lambda e: e.tensor_scalar_mul(rowS2[:, :], rowS2[:, :], 1.0 / DM))
        vop(lambda e: e.tensor_mul(rowT2[:, :], rowS[:, :], rowS[:, :]))
        t = vop(lambda e: e.tensor_sub(rowS2[:, :], rowS2[:, :], rowT2[:, :]))
        c.wait("scalar", "vv", t)
        t = sop(lambda e: e.activation(rowT2[:, :], rowS2[:, :], AF.Sqrt, bias=float(LN_EPS)))
        c.wait("vector", "sc", t)
        vop(lambda e: e.reciprocal(rowU[:, :], rowT2[:, :]))
        t = vop(lambda e: e.scalar_tensor_tensor(rowB[:, :], rowS[:, :], -1.0, rowU[:, :], ALU.mult, ALU.mult))
        bu, _ = group(misc_ring, [(ones1f[:, :], rowU[:, :])], extra_wait=[("vv", t)])
        bb, lastb = group(misc_ring, [(ones1f[:, :], rowB[:, :])])
        return bu, bb, lastb

    def ln_rows_sbuf(ln_idx, tcg, tc2):
        bu, bb, lastb = ln_rows(ln_idx, tcg)
        c.wait("vector", "mm", lastb)
        c.wait_pair("vector", war.get(("ubc_sb", tc2)))
        c.wait_pair("vector", war.get(("ubc2_sb", tc2)))
        vop(lambda e, bu=bu, tc2=tc2: e.tensor_copy(ubc_sb[:, tc2, :], ps[:, bu, :]))
        t = vop(lambda e, bb=bb, tc2=tc2: e.tensor_copy(bbc_sb[:, tc2, :], ps[:, bb, :]))
        misc_ring.done(bu, "vv", t)
        misc_ring.done(bb, "vv", t)
        return t

    def xload(src_ap, slot):
        c.wait_pair("sync", war.get(("xbuf", slot)))
        return ("xb%d" % slot, c.dma("sync", "xb%d" % slot, xbuf[:, slot, :], src_ap))

    def cast_pair(slot, src_tile, mm_guard_key):
        c.wait_pair("scalar", war.get((mm_guard_key, slot)))
        sop(lambda e, slot=slot, src_tile=src_tile: e.copy(xb16[:, slot, :], src_tile))
        return sop(lambda e, slot=slot, src_tile=src_tile: e.activation(sqx[:, slot, :], src_tile, AF.Square))

    def emit_stats(src_d, ln_idx):
        for tcg in range(4):
            bs = dn_ring.take()
            bs2 = dn_ring.take()
            l2 = None
            for mt in range(KT):
                slot = mt % 2
                tl = xload(src_d[mt * 128:(mt + 1) * 128, tcg * 512:(tcg + 1) * 512], slot)
                c.wait_pair("scalar", tl)
                ts2 = cast_pair(slot, xbuf[:, slot, :], "statmm")
                war[("xbuf", slot)] = ("sc", ts2)
                c.wait("tensor", "sc", ts2)
                def f1(eng, slot=slot, mt=mt, bs=bs):
                    return eng.matmul(ps[0:1, bs, :], onescol[:, :], xb16[:, slot, :],
                                      start=(mt == 0), stop=(mt == KT - 1))
                def f2(eng, slot=slot, mt=mt, bs2=bs2):
                    return eng.matmul(ps[0:1, bs2, :], onescol[:, :], sqx[:, slot, :],
                                      start=(mt == 0), stop=(mt == KT - 1))
                mmop(f1)
                tf2 = mmop(f2)
                if mt == KT - 1:
                    l2 = tf2
                war[("statmm", slot)] = ("mm", tf2)
            c.wait("vector", "mm", l2)
            c.wait_pair("vector", war.get(("rowS",)))
            v1 = vop(lambda e, bs=bs: e.tensor_copy(rowS[:, :], ps[0:1, bs, :]))
            v2 = vop(lambda e, bs2=bs2: e.tensor_copy(rowS2[:, :], ps[0:1, bs2, :]))
            dn_ring.done(bs, "vv", v1)
            dn_ring.done(bs2, "vv", v2)
            c.wait("sync", "vv", v2)
            c.dma("sync", "sd", statd[ln_idx, 0, tcg, :][None, :], rowS[:, :])
            war[("rowS",)] = ("sd", c.dma("sync", "sd", statd[ln_idx, 1, tcg, :][None, :], rowS2[:, :]))

    def emit_ln_apply(src_d, ln_idx, base_tcg, ntc, dest, dest_guard=True):
        if dest_guard:
            c.barrier_waits("vector")
            c.barrier_waits("gpsimd")
        for tc in range(ntc):
            tcg = base_tcg + tc
            tub = ln_rows_sbuf(ln_idx, tcg, tc % 2)
            c.wait("gpsimd", "vv", tub)
            free_v = free_p = None
            for mt in range(KT):
                slot = mt % 2
                tl = xload(src_d[mt * 128:(mt + 1) * 128, tcg * 512:(tcg + 1) * 512], slot)
                if mt % 2 == 0:
                    c.wait_pair("vector", tl)
                    vop(lambda e, slot=slot, tc=tc: e.tensor_mul(t1b[:, :], xbuf[:, slot, :], ubc_sb[:, tc % 2, :]))
                    free_v = vop(lambda e, mt=mt, tc=tc, dest=dest: e.tensor_add(
                        dest[:, mt, tc * 512:(tc + 1) * 512], t1b[:, :], bbc_sb[:, tc % 2, :]))
                    war[("xbuf", slot)] = ("vv", free_v)
                else:
                    c.wait_pair("gpsimd", tl)
                    pop_(lambda e, slot=slot, tc=tc: e.tensor_mul(t1c[:, :], xbuf[:, slot, :], ubc_sb[:, tc % 2, :]))
                    free_p = pop_(lambda e, mt=mt, tc=tc, dest=dest: e.tensor_add(
                        dest[:, mt, tc * 512:(tc + 1) * 512], t1c[:, :], bbc_sb[:, tc % 2, :]))
                    war[("xbuf", slot)] = ("pl", free_p)
            war[("ubc_sb", tc % 2)] = ("pl", free_p)
            war[("a_ready",)] = ("vv", free_v)
            war[("a_ready2",)] = ("pl", free_p)

    def emit_gate(j):
        c.wait("vector", "mm", c.cnt["mm"])
        c.wait_pair("tensor", war.get(("a_ready",)))
        c.wait_pair("tensor", war.get(("a_ready2",)))
        bg_ = misc_ring.take()
        last = None
        for tm in range(8):
            for k in range(KT):
                def f(eng, tm=tm, k=k, j=j, bg_=bg_):
                    return eng.matmul(ps[:, bg_, tm * 8:(tm + 1) * 8],
                                      a_b[:, k, tm * 128:(tm + 1) * 128],
                                      wg_sb[:, k, j, :], start=(k == 0), stop=(k == KT - 1))
                if tm == 7 and k == KT - 1:
                    last = mmop(f)
                else:
                    c.op("tensor", f)
        c.wait("vector", "mm", last)
        ev = vop(lambda e, bg_=bg_: e.tensor_copy(gate_f[:, :], ps[:, bg_, 0:64]))
        misc_ring.done(bg_, "vv", ev)
        bgv0 = bgb[:, j * 8:(j + 1) * 8].rearrange("p (a b) -> p a b", a=1)
        bgv = bass.AP(bgv0.tensor, bgv0.offset, [list(bgv0.ap[0]), [0, 8], list(bgv0.ap[2])])
        vop(lambda e: e.tensor_tensor(seg8(gate_f), seg8(gate_f), bgv, ALU.add))
        vop(lambda e: e.tensor_reduce(tkr["m1"][:, :].rearrange("p (a b) -> p a b", b=1), seg8(gate_f), AX.X, ALU.min))
        vop(lambda e: e.tensor_tensor(seg8(tk["t8"]), seg8(gate_f), colb8(tkr["m1"]), ALU.is_equal))
        vop(lambda e: e.scalar_tensor_tensor(tk["g1"][:, :], tk["t8"][:, :], BIG, gate_f[:, :], ALU.mult, ALU.add))
        vop(lambda e: e.tensor_reduce(tkr["m2"][:, :].rearrange("p (a b) -> p a b", b=1), seg8(tk["g1"]), AX.X, ALU.min))
        vop(lambda e: e.tensor_tensor(seg8(tk["t8"]), seg8(tk["g1"]), colb8(tkr["m2"]), ALU.is_equal))
        vop(lambda e: e.scalar_tensor_tensor(tk["g2t"][:, :], tk["t8"][:, :], BIG, tk["g1"][:, :], ALU.mult, ALU.add))
        vop(lambda e: e.tensor_reduce(tkr["m3"][:, :].rearrange("p (a b) -> p a b", b=1), seg8(tk["g2t"]), AX.X, ALU.min))
        vop(lambda e: e.tensor_tensor(seg8(tk["k8"]), seg8(gate_f), colb8(tkr["m3"]), ALU.is_ge))
        vop(lambda e: e.tensor_reduce(tkr["nmx"][:, :].rearrange("p (a b) -> p a b", b=1), seg8(gate_f), AX.X, ALU.max, negate=True))
        tdv = vop(lambda e: e.tensor_tensor(seg8(tk["d8"]), seg8(gate_f), colb8(tkr["nmx"]), ALU.add))
        c.wait("scalar", "vv", tdv)
        te = sop(lambda e: e.activation(tk["e8"][:, :], tk["d8"][:, :], AF.Exp))
        c.wait("vector", "sc", te)
        vop(lambda e: e.tensor_mul(tk["em8"][:, :], tk["e8"][:, :], tk["k8"][:, :]))
        vop(lambda e: e.tensor_reduce(tkr["gz"][:, :].rearrange("p (a b) -> p a b", b=1), seg8(tk["em8"]), AX.X, ALU.add))
        vop(lambda e: e.reciprocal(tkr["gr"][:, :], tkr["gz"][:, :]))
        wt = vop(lambda e: e.tensor_tensor(seg8(w_tm), seg8(tk["em8"]), colb8(tkr["gr"]), ALU.mult))
        c.wait("tensor", "vv", wt)
        ev = None
        for tm in range(8):
            bt = misc_ring.take()
            def ft(eng, tm=tm, bt=bt):
                return eng.transpose(ps[0:8, bt, 0:128], w_tm[:, tm * 8:(tm + 1) * 8], ident_f[:, :])
            lastt = mmop(ft)
            c.wait("vector", "mm", lastt)
            ev = vop(lambda e, tm=tm, bt=bt: e.tensor_copy(w_fm[:, tm, :], ps[0:8, bt, 0:128]))
            misc_ring.done(bt, "vv", ev)
        war[("w_fm",)] = ("vv", ev)

    down_par = {0: None, 1: None}

    def emit_shared(slab):
        for half in range(2):
            tof = half * 512
            sh_ready = None
            for ms2 in range(8):
                v, wt, key = wload(col_block(wsu, ms2 * 256, 256), "w", wpv, ncol=256)
                c.wait_pair("tensor", wt)
                for sub in range(2):
                    ms = ms2 * 2 + sub
                    vv = v[:, :, sub * 128:(sub + 1) * 128]
                    b, last = group(up_ring, [(vv[:, k, :], a_b[:, k, tof:tof + 512]) for k in range(KT)])
                    if sub == 1:
                        wfree(key, last)
                    c.wait("scalar", "mm", last)
                    c.wait_pair("scalar", ("mm", down_par[0][1]) if down_par[0] else None)
                    t = sop(lambda e, ms=ms, b=b: e.activation(
                        sh_h[:, ms, :], ps[:, b, :], AF.Identity, bias=bsu_sb[:, ms:ms + 1]))
                    up_ring.done(b, "sc", t)
                    sh_ready = t
            sa_last = None
            sd_last = None
            if True:
                h2_ready = {}
                h2b = [h2sh, h2sh2]
                for s2 in range(2):
                    for ma2 in range(4):
                        v, wt, key = wload(col_block(wsa, ma2 * 256, 256), "w", wpv, nkt=8, ncol=256)
                        c.wait_pair("tensor", wt)
                        for sub in range(2):
                            ma = ma2 * 2 + sub
                            vv = v[:, :, sub * 128:(sub + 1) * 128]
                            b, last = group(sw_ring, [(vv[:, k, :], sh_h[:, s2 * 8 + k, :]) for k in range(8)],
                                            extra_wait=[("sc", sh_ready)])
                            if sub == 1:
                                wfree(key, last)
                            sa_last = last
                            bsv = misc_ring.take(engine="scalar")
                            c.wait("scalar", "mm", last)
                            t = sop(lambda e, b=b, bsv=bsv, ma=ma: e.activation(
                                ps[:, bsv, :], ps[:, b, :], AF.Silu, bias=bsa_sb[:, ma:ma + 1]))
                            sw_ring.done(b, "sc", t)
                            c.wait("vector", "sc", t)
                            c.wait_pair("vector", ("mm", down_par[1][1]) if down_par[1] else None)
                            c.wait_pair("vector", war.get(("h2b", s2)))
                            t = vop(lambda e, bsv=bsv, ma=ma, s2=s2, h2b=h2b: e.tensor_mul(
                                h2b[s2][:, ma, :], ps[:, bsv, :], sh_h[:, s2 * 8 + ma, :]))
                            misc_ring.done(bsv, "vv", t)
                            h2_ready[s2] = t
                for s2 in range(2):
                    for md2 in range(8):
                        v, wt, key = wload(col_block(wsd[s2 * 1024:(s2 + 1) * 1024, :], md2 * 256, 256), "w", wpv, nkt=8, ncol=256)
                        c.wait_pair("tensor", wt)
                        for sub in range(2):
                            md = md2 * 2 + sub
                            vv = v[:, :, sub * 128:(sub + 1) * 128]
                            b, last = group(dn_ring, [(vv[:, k, :], h2b[s2][:, k, :]) for k in range(8)],
                                            extra_wait=[("vv", h2_ready[s2])])
                            if sub == 1:
                                wfree(key, last)
                            sd_last = last
                            war[("h2b", s2)] = ("mm", last)
                            c.wait("vector", "mm", last)
                            if s2 == 0:
                                t = vop(lambda e, md=md, b=b, tof=tof: e.tensor_scalar_add(
                                    acc[:, md, tof:tof + 512], ps[:, b, :], bsd_sb[:, md:md + 1]))
                            else:
                                t = vop(lambda e, md=md, b=b, tof=tof: e.tensor_add(
                                    acc[:, md, tof:tof + 512], acc[:, md, tof:tof + 512], ps[:, b, :]))
                            dn_ring.done(b, "vv", t)
            down_par[0] = ("mm", sa_last)
            down_par[1] = ("mm", sd_last)

    def emit_expert(n, tail_cb=None):
        src = Bd[n % 2]
        dst = Bd[(n + 1) % 2]
        c.wait_pair("sync", war.get(("w_fm",)))
        c.wait_pair("sync", war.get(("wrow",)))
        trw = c.dma("sync", "xr", wrow[0:1, :],
                    w_fm[n:n + 1, :, :].rearrange("a tm m -> a (tm m)"))
        c.wait("sync", "xr", trw)
        for half in range(2):
            bwc = misc_ring.take()
            c.wait("tensor", "xr", trw)
            def fbc(eng, bwc=bwc, half=half):
                return eng.matmul(ps[:, bwc, :], ones1[:, :],
                                  wrow[0:1, half * 512:(half + 1) * 512], start=True, stop=True)
            last = mmop(fbc)
            war[("wrow",)] = ("mm", last)
            c.wait("vector", "mm", last)
            c.wait_pair("vector", war.get(("w_bc", half)))
            t = vop(lambda e, bwc=bwc, half=half: e.tensor_copy(
                w_bc[:, half * 512:(half + 1) * 512], ps[:, bwc, :]))
            misc_ring.done(bwc, "vv", t)
        h_up_ready = None
        for mu2 in range(4):
            v, wt, key = wload(col_block(wup[n, :, :], mu2 * 256, 256), "w", wpv, ncol=256)
            c.wait_pair("tensor", wt)
            for sub in range(2):
                mu = mu2 * 2 + sub
                vv = v[:, :, sub * 128:(sub + 1) * 128]
                for tc2 in range(2):
                    tof = tc2 * 512
                    b, last = group(up_ring, [(vv[:, k, :], a_b[:, k, tof:tof + 512]) for k in range(KT)])
                    if sub == 1 and tc2 == 1:
                        wfree(key, last)
                    c.wait("scalar", "mm", last)
                    c.wait_pair("scalar", ("mm", down_par[n % 2][1]) if down_par[n % 2] else None)
                    t = sop(lambda e, mu=mu, b=b, n=n, tof=tof, src=src: e.activation(
                        src[:, mu, tof:tof + 512], ps[:, b, :], AF.Identity, bias=bup_sb[:, n, mu:mu + 1]))
                    up_ring.done(b, "sc", t)
                    h_up_ready = t
        h2_ready = {0: None, 1: None}
        for mw2 in range(4):
            v, wt, key = wload(col_block(wsw[n, :, :], mw2 * 256, 256), "w", wpv, nkt=8, ncol=256)
            c.wait_pair("tensor", wt)
            for sub in range(2):
              mw = mw2 * 2 + sub
              vv = v[:, :, sub * 128:(sub + 1) * 128]
              for tc2 in range(2):
                tof = tc2 * 512
                b, last = group(sw_ring, [(vv[:, k, :], src[:, k, tof:tof + 512]) for k in range(8)],
                                extra_wait=[("sc", h_up_ready)])
                if sub == 1 and tc2 == 1:
                    wfree(key, last)
                bsv = misc_ring.take(engine="scalar")
                c.wait("scalar", "mm", last)
                t = sop(lambda e, b=b, bsv=bsv, n=n, mw=mw: e.activation(
                    ps[:, bsv, :], ps[:, b, :], AF.Silu, bias=bsw_sb[:, n, mw:mw + 1]))
                sw_ring.done(b, "sc", t)
                c.wait("vector", "sc", t)
                c.wait_pair("vector", ("mm", down_par[(n + 1) % 2][1]) if down_par[(n + 1) % 2] else None)
                t = vop(lambda e, bsv=bsv, mw=mw, tof=tof, src=src, dst=dst: e.tensor_mul(
                    dst[:, mw, tof:tof + 512], ps[:, bsv, :], src[:, mw, tof:tof + 512]))
                misc_ring.done(bsv, "vv", t)
                h2_ready[tc2] = vop(lambda e, mw=mw, tof=tof, dst=dst: e.tensor_mul(
                    dst[:, mw, tof:tof + 512], dst[:, mw, tof:tof + 512], w_bc[:, tof:tof + 512]))
        last_all = None
        for md2 in range(8):
            v, wt, key = wload(col_block(wdn[n, :, :], md2 * 256, 256), "w", wpv, nkt=8, ncol=256)
            c.wait_pair("tensor", wt)
            for sub in range(2):
              md = md2 * 2 + sub
              vv = v[:, :, sub * 128:(sub + 1) * 128]
              for tc2 in range(2):
                tof = tc2 * 512
                mms = [(vv[:, k, :], dst[:, k, tof:tof + 512]) for k in range(8)]
                ew = [("vv", h2_ready[tc2])]
                if n == 0:
                    wfmv = w_fm[0:6, :, :].rearrange("n tm m -> n (tm m)")[:, tof:tof + 512]
                    mms.append((bdn_sb[0:6, md * 128:(md + 1) * 128], wfmv))
                    ew.append(war.get(("w_fm",)))
                b, last = group(dn_ring, mms, extra_wait=[w for w in ew if w])
                if sub == 1 and tc2 == 1:
                    wfree(key, last)
                last_all = last
                c.wait("vector", "mm", last)
                t = vop(lambda e, md=md, b=b, tof=tof: e.tensor_add(
                    acc[:, md, tof:tof + 512], acc[:, md, tof:tof + 512], ps[:, b, :]))
                dn_ring.done(b, "vv", t)
                if tail_cb is not None:
                    tail_cb(md, tc2)
        down_par[(n + 1) % 2] = ("mm", last_all)
        war[("w_bc", 0)] = war[("w_bc", 1)] = ("mm", last_all)

    def make_xt1_tail(src_d, slab):
        def cb(mt, tc2):
            tcg = slab * 2 + tc2
            slot = mt % 2
            tl = xload(src_d[mt * 128:(mt + 1) * 128, tcg * 512:(tcg + 1) * 512], slot)
            c.wait_pair("vector", tl)
            vop(lambda e, slot=slot, tc2=tc2: e.tensor_mul(t1b[:, :], xbuf[:, slot, :], ubc_sb[:, tc2, :]))
            vop(lambda e, tc2=tc2: e.tensor_add(t1b[:, :], t1b[:, :], bbc_sb[:, tc2, :]))
            free_tgt = vop(lambda e, mt=mt, tc2=tc2: e.tensor_add(
                a_b[:, mt, tc2 * 512:(tc2 + 1) * 512], t1b[:, :], acc[:, mt, tc2 * 512:(tc2 + 1) * 512]))
            war[("xbuf", slot)] = ("vv", free_tgt)
            war[("a_ready",)] = ("vv", free_tgt)
        return cb

    def make_asm_tail(src_d, slab, dest_d, next_ln_idx, stat_banks):
        def cb(mt, tc2):
            tcg = slab * 2 + tc2
            slot = mt % 2
            tl = xload(src_d[mt * 128:(mt + 1) * 128, tcg * 512:(tcg + 1) * 512], slot)
            c.wait_pair("vector", tl)
            vop(lambda e, slot=slot, tc2=tc2: e.tensor_mul(t1b[:, :], xbuf[:, slot, :], ubc_sb[:, tc2, :]))
            vop(lambda e, tc2=tc2: e.tensor_add(t1b[:, :], t1b[:, :], bbc_sb[:, tc2, :]))
            vop(lambda e, mt=mt, tc2=tc2: e.tensor_add(
                t1b[:, :], t1b[:, :], acc[:, mt, tc2 * 512:(tc2 + 1) * 512]))
            c.wait_pair("vector", war.get(("outh", slot)))
            c.wait_pair("vector", war.get(("outhc", slot)))
            free_tgt = vop(lambda e, slot=slot: e.scalar_tensor_tensor(
                outh[:, slot, :], t1b[:, :], 0.5, xbuf[:, slot, :], ALU.mult, ALU.add))
            war[("xbuf", slot)] = ("vv", free_tgt)
            if next_ln_idx is not None:
                if tc2 not in stat_banks:
                    ring = sw_ring if tc2 == 0 else misc_ring
                    stat_banks[tc2] = (ring.take(), ring.take())
                    stat_banks[("ring", tc2)] = ring
                bs, bs2 = stat_banks[tc2]
                c.wait("scalar", "vv", free_tgt)
                ts2 = cast_pair(slot, outh[:, slot, :], "statmm")
                war[("outhc", slot)] = ("sc", ts2)
                c.wait("tensor", "sc", ts2)
                def f1(eng, slot=slot, mt=mt, bs=bs):
                    return eng.matmul(ps[0:1, bs, :], onescol[:, :], xb16[:, slot, :],
                                      start=(mt == 0), stop=(mt == KT - 1))
                def f2(eng, slot=slot, mt=mt, bs2=bs2):
                    return eng.matmul(ps[0:1, bs2, :], onescol[:, :], sqx[:, slot, :],
                                      start=(mt == 0), stop=(mt == KT - 1))
                mmop(f1)
                tf2 = mmop(f2)
                war[("statmm", slot)] = ("mm", tf2)
                stat_banks[("last", tc2)] = tf2
            c.wait("sync", "vv", free_tgt)
            st = c.dma("sync", "oh%d" % slot, dest_d[mt * 128:(mt + 1) * 128, tcg * 512:(tcg + 1) * 512],
                       outh[:, slot, :])
            war[("outh", slot)] = ("oh%d" % slot, st)
        return cb

    def finish_asm_stats(slab, next_ln_idx, stat_banks):
        if next_ln_idx is None:
            return
        for tc2 in range(2):
            tcg = slab * 2 + tc2
            bs, bs2 = stat_banks[tc2]
            ring = stat_banks[("ring", tc2)]
            c.wait("vector", "mm", stat_banks[("last", tc2)])
            c.wait_pair("vector", war.get(("rowS",)))
            v1 = vop(lambda e, bs=bs: e.tensor_copy(rowS[:, :], ps[0:1, bs, :]))
            v2 = vop(lambda e, bs2=bs2: e.tensor_copy(rowS2[:, :], ps[0:1, bs2, :]))
            ring.done(bs, "vv", v1)
            ring.done(bs2, "vv", v2)
            c.wait("sync", "vv", v2)
            c.dma("sync", "sd", statd[next_ln_idx, 0, tcg, :][None, :], rowS[:, :])
            war[("rowS",)] = ("sd", c.dma("sync", "sd", statd[next_ln_idx, 1, tcg, :][None, :], rowS2[:, :]))

    def emit_xt1(src_d, ln_idx, slab):
        c.wait("vector", "mm", c.cnt["mm"])
        for tc in range(2):
            tcg = slab * 2 + tc
            bu, bb, lastb = ln_rows(ln_idx, tcg)
            free_tgt = None
            for mt in range(KT):
                slot = mt % 2
                tl = xload(src_d[mt * 128:(mt + 1) * 128, tcg * 512:(tcg + 1) * 512], slot)
                c.wait_pair("vector", tl)
                c.wait("vector", "mm", lastb)
                vop(lambda e, slot=slot, bu=bu: e.tensor_mul(t1b[:, :], xbuf[:, slot, :], ps[:, bu, :]))
                vop(lambda e, bb=bb: e.tensor_add(t1b[:, :], t1b[:, :], ps[:, bb, :]))
                free_tgt = vop(lambda e, mt=mt, tc=tc: e.tensor_add(
                    a_b[:, mt, tc * 512:(tc + 1) * 512], t1b[:, :], acc[:, mt, tc * 512:(tc + 1) * 512]))
                war[("xbuf", slot)] = ("vv", free_tgt)
            misc_ring.done(bu, "vv", free_tgt)
            misc_ring.done(bb, "vv", free_tgt)
            war[("a_ready",)] = ("vv", free_tgt)

    def emit_assembly(src_d, ln_idx, slab, dest_d, next_ln_idx):
        for tc in range(2):
            tcg = slab * 2 + tc
            bu, bb, lastb = ln_rows(ln_idx, tcg)
            bs = bs2 = None
            if next_ln_idx is not None:
                bs = dn_ring.take()
                bs2 = dn_ring.take()
            l2 = None
            free_tgt = None
            for mt in range(KT):
                slot = mt % 2
                tl = xload(src_d[mt * 128:(mt + 1) * 128, tcg * 512:(tcg + 1) * 512], slot)
                c.wait_pair("vector", tl)
                c.wait("vector", "mm", lastb)
                vop(lambda e, slot=slot, bu=bu: e.tensor_mul(t1b[:, :], xbuf[:, slot, :], ps[:, bu, :]))
                vop(lambda e, bb=bb: e.tensor_add(t1b[:, :], t1b[:, :], ps[:, bb, :]))
                vop(lambda e, mt=mt, tc=tc: e.tensor_add(
                    t1b[:, :], t1b[:, :], acc[:, mt, tc * 512:(tc + 1) * 512]))
                c.wait_pair("vector", war.get(("outh", slot)))
                c.wait_pair("vector", war.get(("outhc", slot)))
                free_tgt = vop(lambda e, slot=slot: e.scalar_tensor_tensor(
                    outh[:, slot, :], t1b[:, :], 0.5, xbuf[:, slot, :], ALU.mult, ALU.add))
                war[("xbuf", slot)] = ("vv", free_tgt)
                if next_ln_idx is not None:
                    c.wait("scalar", "vv", free_tgt)
                    ts2 = cast_pair(slot, outh[:, slot, :], "statmm")
                    war[("outhc", slot)] = ("sc", ts2)
                    c.wait("tensor", "sc", ts2)
                    def f1(eng, slot=slot, mt=mt, bs=bs):
                        return eng.matmul(ps[0:1, bs, :], onescol[:, :], xb16[:, slot, :],
                                          start=(mt == 0), stop=(mt == KT - 1))
                    def f2(eng, slot=slot, mt=mt, bs2=bs2):
                        return eng.matmul(ps[0:1, bs2, :], onescol[:, :], sqx[:, slot, :],
                                          start=(mt == 0), stop=(mt == KT - 1))
                    mmop(f1)
                    tf2 = mmop(f2)
                    if mt == KT - 1:
                        l2 = tf2
                    war[("statmm", slot)] = ("mm", tf2)
                c.wait("sync", "vv", free_tgt)
                st = c.dma("sync", "oh%d" % slot, dest_d[mt * 128:(mt + 1) * 128, tcg * 512:(tcg + 1) * 512],
                           outh[:, slot, :])
                war[("outh", slot)] = ("oh%d" % slot, st)
            misc_ring.done(bu, "vv", free_tgt)
            misc_ring.done(bb, "vv", free_tgt)
            if next_ln_idx is not None:
                c.wait("vector", "mm", l2)
                v1 = vop(lambda e, bs=bs: e.tensor_copy(rowS[:, :], ps[0:1, bs, :]))
                v2 = vop(lambda e, bs2=bs2: e.tensor_copy(rowS2[:, :], ps[0:1, bs2, :]))
                dn_ring.done(bs, "vv", v1)
                dn_ring.done(bs2, "vv", v2)
                c.wait("sync", "vv", v2)
                c.dma("sync", "sd", statd[next_ln_idx, 0, tcg, :][None, :], rowS[:, :])
                war[("rowS",)] = ("sd", c.dma("sync", "sd", statd[next_ln_idx, 1, tcg, :][None, :], rowS2[:, :]))

    def emit_ffn_call(src_d, ln_idx, dest_d, next_ln_idx):
        for e in ["vector", "scalar", "gpsimd", "sync"]:
            c.wait(e, "mm", c.cnt["mm"])
        c.wait("tensor", "vv", c.cnt["vv"])
        c.wait("tensor", "sc", c.cnt["sc"])
        c.wait("sync", "oh0", c.cnt["oh0"])
        c.wait("sync", "oh1", c.cnt["oh1"])
        for slab in range(2):
            emit_ln_apply(src_d, ln_idx, slab * 2, 2, a_b)
            for j in range(2):
                emit_gate(j)
                emit_shared(slab)
                for n in range(NEXP - 1):
                    emit_expert(n)
                if j == 0:
                    for tc2 in range(2):
                        ln_rows_sbuf(ln_idx, slab * 2 + tc2, tc2)
                        war[("ubc_sb", tc2)] = None
                    emit_expert(NEXP - 1, tail_cb=make_xt1_tail(src_d, slab))
                else:
                    for tc2 in range(2):
                        ln_rows_sbuf(ln_idx, slab * 2 + tc2, tc2)
                        war[("ubc_sb", tc2)] = None
                    stat_banks = {}
                    emit_expert(NEXP - 1, tail_cb=make_asm_tail(src_d, slab, dest_d, next_ln_idx, stat_banks))
                    finish_asm_stats(slab, next_ln_idx, stat_banks)

    def emit_attn_proj():
        for e in ["vector", "scalar", "gpsimd", "sync"]:
            c.wait(e, "mm", c.cnt["mm"])
        c.wait("tensor", "vv", c.cnt["vv"])
        c.wait("tensor", "sc", c.cnt["sc"])
        c.wait("sync", "oh0", c.cnt["oh0"])
        c.wait("sync", "oh1", c.cnt["oh1"])
        emit_ln_apply(xres1, 1, 0, 4, a2, dest_guard=False)
        c.wait_pair("tensor", war.get(("a_ready",)))
        c.wait_pair("tensor", war.get(("a_ready2",)))
        prAs = [prA, prC]
        prBs = [prB, prD]
        pend = [None]

        def kq_part1(idx, vsub, tof):
            sl = idx % 2
            b, last = group(up_ring, [(vsub[:, k, :], a2[:, k, tof:tof + 512]) for k in range(KT)])
            c.wait("scalar", "mm", last)
            c.wait_pair("scalar", war.get(("prB", sl)))
            tsq = sop(lambda e, b=b, sl=sl: e.activation(prBs[sl][:, :], ps[:, b, :], AF.Square))
            bsq, lasts = group(proj_ring, [(sel64[:, :], prBs[sl][:, :])], out_rows=2,
                               extra_wait=[("sc", tsq)])
            war[("prB", sl)] = ("mm", lasts)
            c.wait_pair("scalar", war.get(("prA", sl)))
            t = sop(lambda e, b=b, sl=sl: e.copy(prAs[sl][:, :], ps[:, b, :]))
            up_ring.done(b, "sc", t)
            return (bsq, lasts, sl)

        def kq_part2(state, mq, tof, is_q):
            bsq, lasts, sl = state
            c.wait("scalar", "mm", lasts)
            c.wait_pair("scalar", war.get(("rnk",)))
            t = sop(lambda e, bsq=bsq: e.activation(rnk[:, :], ps[0:2, bsq, :], AF.Sqrt))
            proj_ring.done(bsq, "sc", t)
            c.wait("vector", "sc", t)
            vop(lambda e: e.tensor_scalar_max(rnk[:, :], rnk[:, :], float(L2_EPS)))
            vop(lambda e: e.reciprocal(rnk2[:, :], rnk[:, :]))
            if not is_q:
                vop(lambda e, mq=mq: e.tensor_scalar_mul(rnk2[:, :], rnk2[:, :], r_cols[:, mq:mq + 1]))
            c.wait_pair("vector", war.get(("rnkb",)))
            tv = vop(lambda e: e.tensor_copy(rnkb[:, :], rnk2[:, :]))
            war[("rnk",)] = ("vv", tv)
            bb2, lastb2 = group(proj_ring, [(sel64T[:, :], rnkb[:, :])], extra_wait=[("vv", tv)])
            war[("rnkb",)] = ("mm", lastb2)
            c.wait("vector", "mm", lastb2)
            if is_q:
                tv = vop(lambda e, bb2=bb2, mq=mq, tof=tof, sl=sl: e.tensor_mul(
                    q_sb[:, mq, tof:tof + 512], prAs[sl][:, :], ps[:, bb2, :]))
                war[("prA", sl)] = ("vv", tv)
                proj_ring.done(bb2, "vv", tv)
            else:
                c.wait_pair("vector", war.get(("vnb", 0)))
                tv = vop(lambda e, bb2=bb2, sl=sl: e.tensor_mul(vnb[:, :], prAs[sl][:, :], ps[:, bb2, :]))
                war[("prA", sl)] = ("vv", tv)
                proj_ring.done(bb2, "vv", tv)
                c.wait("sync", "vv", tv)
                st = c.dma("sync", "kvw", KLOC[mq, :, tof:tof + 512], vnb[:, :])
                war[("vnb", 0)] = ("kvw", st)

        kq_items = []
        for mq2 in range(4):
            for s in range(2):
                for tc in range(4):
                    kq_items.append((mq2, mq2 * 2 + s, s, tc))
        idx = 0
        loaded = {}
        for mq2, mq, s, tc in kq_items:
            if mq2 not in loaded:
                v, wt, key = wload(col_block(wk, mq2 * 256, 256), "w", wpv, ncol=256)
                loaded[mq2] = (v, wt, key)
            v, wt, key = loaded[mq2]
            c.wait_pair("tensor", wt)
            st1 = kq_part1(idx, v[:, :, s * 128:(s + 1) * 128], tc * 512)
            if s == 1 and tc == 3:
                wfree(key, st1[1])
            if pend[0] is not None:
                kq_part2(*pend[0])
            pend[0] = (st1, mq, tc * 512, False)
            idx += 1
        kq_part2(*pend[0])
        pend[0] = None
        for dvc in range(4):
            v, wt, key = wload(wv[:, dvc * 512:(dvc + 1) * 512].rearrange("(kt p) m -> p kt m", p=128),
                               "v", wvpv, ncol=512)
            c.wait_pair("tensor", wt)
            for tm in range(KT):
                b, last = group(up_ring, [(a2[:, k, tm * 128:(tm + 1) * 128], v[:, k, :]) for k in range(KT)])
                if tm == KT - 1:
                    wfree(key, last)
                c.wait("scalar", "mm", last)
                c.wait_pair("scalar", war.get(("vps", 0)))
                ts = sop(lambda e, b=b: e.activation(vps[:, :], ps[:, b, :], AF.Square))
                c.wait("vector", "sc", ts)
                tv = vop(lambda e, tm=tm, dvc=dvc: e.tensor_reduce(
                    vsacc[:, tm:tm + 1, dvc * 4:(dvc + 1) * 4].rearrange("p a (x b) -> p (a x) b", b=1),
                    vps[:, :].rearrange("p (a b) -> p a b", b=128), AX.X, ALU.add))
                war[("vps", 0)] = ("vv", tv)
                c.wait_pair("scalar", war.get(("vnb", 0)))
                t = sop(lambda e, b=b: e.copy(vnb[:, :], ps[:, b, :]))
                up_ring.done(b, "sc", t)
                c.wait("sync", "sc", t)
                st = c.dma("sync", "kvw", VLOC[tm, :, dvc * 512:(dvc + 1) * 512], vnb[:, :])
                war[("vnb", 0)] = ("kvw", st)
        c.wait("scalar", "vv", c.cnt["vv"])
        for tm in range(KT):
            c.wait_pair("scalar", war.get(("vps", 0)))
            c.wait_pair("scalar", war.get(("vpsrn", 0)))
            t = sop(lambda e, tm=tm: e.activation(vps[:, 0:16], vsacc[:, tm, :], AF.Sqrt))
            c.wait("vector", "sc", t)
            vop(lambda e: e.tensor_scalar_max(vps[:, 0:16], vps[:, 0:16], float(L2_EPS)))
            tv = vop(lambda e: e.reciprocal(vps[:, 48:64], vps[:, 0:16]))
            c.wait("scalar", "vv", tv)
            t = sop(lambda e: e.activation(vps[:, 32:48], vps[:, 48:64], AF.Sqrt))
            c.wait("sync", "sc", t)
            st = c.dma("sync", "kvw", RLOC[tm, :, :], vps[:, 32:64])
            war[("vpsrn", 0)] = ("kvw", st)
            war[("vps", 0)] = ("kvw", st)
        c.wait("sync", "kvw", c.cnt["kvw"])
        c.dma("sync", "kvw", KSH[bass.ds(par, 1), :, :, :], KLOC[None, :, :, :])
        c.dma("sync", "kvw", VSH[bass.ds(par, 1), :, :, :], VLOC[None, :, :, :])
        c.dma("sync", "kvw", RNSH[bass.ds(par, 1), :, :, :], RLOC[None, :, :, :])
        c.wait("sync", "kvw", c.cnt["kvw"])
        c.dma("sync", "kvw", FKV[bass.ds(par, 1), 0:1, 0:1], nonb[0:1, 0:1][None, :, :])
        c.wait("sync", "kvw", c.cnt["kvw"])
        idx = 0
        loaded = {}
        for mq2, mq, s, tc in kq_items:
            if mq2 not in loaded:
                v, wt, key = wload(col_block(wq, mq2 * 256, 256), "w", wpv, ncol=256)
                loaded[mq2] = (v, wt, key)
            v, wt, key = loaded[mq2]
            c.wait_pair("tensor", wt)
            st1 = kq_part1(idx, v[:, :, s * 128:(s + 1) * 128], tc * 512)
            if s == 1 and tc == 3:
                wfree(key, st1[1])
            if pend[0] is not None:
                kq_part2(*pend[0])
            pend[0] = (st1, mq, tc * 512, True)
            idx += 1
        kq_part2(*pend[0])
        pend[0] = None
        c.wait("scalar", "vv", c.cnt["vv"])
        c.wait("scalar", "mm", c.cnt["mm"])
        for gm2 in range(8):
            v, wt, key = wload(col_block(wgt, gm2 * 256, 256), "w", wpv, ncol=256)
            c.wait_pair("tensor", wt)
            for gm, vsub in [(gm2 * 2 + s, v[:, :, s * 128:(s + 1) * 128]) for s in range(2)]:
              for tc in range(4):
                tof = tc * 512
                b, last = group(up_ring, [(vsub[:, k, :], a2[:, k, tof:tof + 512]) for k in range(KT)])
                if gm % 2 == 1 and tc == 3:
                    wfree(key, last)
                c.wait("scalar", "mm", last)
                c.wait_pair("scalar", war.get(("prD", 0)))
                t = sop(lambda e, b=b: e.activation(prC[:, :], ps[:, b, :], AF.Silu))
                up_ring.done(b, "sc", t)
                t = sop(lambda e: e.activation(prD[:, :], prC[:, :], AF.Tanh))
                c.wait("sync", "sc", t)
                st = c.dma("sync", "kvw", g2d[gm, :, tof:tof + 512], prD[:, :])
                war[("prD", 0)] = ("kvw", st)

    sim_free = {}
    attn_ring = PsumRing(c, [0, 3])
    proj_ring = PsumRing(c, [4, 5, 6, 7])

    def emit_attn_core():
        for e in ["vector", "scalar", "gpsimd", "sync"]:
            c.wait(e, "mm", c.cnt["mm"])
        c.wait("tensor", "vv", c.cnt["vv"])
        c.wait("tensor", "sc", c.cnt["sc"])
        c.wait("sync", "oh0", c.cnt["oh0"])
        c.wait("sync", "oh1", c.cnt["oh1"])
        for p2 in range(2):
            rn_src = (RLOC[:, :, :] if self_pair else RNSH[p2, :, :, :])
            c.dma("sync", "xr", srn[:, p2 * 16:(p2 + 1) * 16, :],
                  rn_src.rearrange("tm p h -> p tm h"))
        tsrn = c.cnt["xr"]
        c.wait("sync", "xr", tsrn)
        c.wait("scalar", "xr", tsrn)
        c.wait("vector", "xr", tsrn)
        kv_free = {}
        wo_last = None
        for qb in range(4):
            qof = qb * 512
            if wo_last is not None:
                c.wait("vector", "mm", wo_last)
            for hp in range(8):
                slot = hp % 2
                c.wait_pair("sync", kv_free.get(slot))
                ksv = "kv%d" % slot
                for p2 in range(2):
                    k_src = (KLOC[hp, :, :] if self_pair else KSH[p2, hp, :, :])
                    c.dma("sync", ksv, kbuf[:, slot, p2, :], k_src)
                    for j in range(2):
                        h = 2 * hp + j
                        v_src = (VLOC[:, :, h * 128:(h + 1) * 128]
                                 if self_pair else VSH[p2, :, :, h * 128:(h + 1) * 128])
                        c.dma("sync", ksv, vbuf[:, slot, j, p2 * 16:(p2 + 1) * 16, :],
                              v_src.rearrange("tm p dv -> p tm dv"))
                for j in range(2):
                    c.dma("sync", ksv, g2b[:, slot, j, :], g2d[2 * hp + j, :, qof:qof + 512])
                tkv = c.cnt[ksv]
                c.wait("tensor", ksv, tkv)
                c.wait("vector", ksv, tkv)
                c.wait("scalar", ksv, tkv)
                agg_last = [None, None]
                sq_t = {}
                PIPE = 2
                for it in range(32 + PIPE):
                    if it < 32:
                        kb = it
                        for j in range(2):
                            h = 2 * hp + j
                            bsim = j * 3 + kb % 3
                            c.wait_pair("tensor", sim_free.get(bsim))
                            if bsim in (0, 3):
                                fa_ = attn_ring.free_at.get(bsim)
                                if fa_ is not None:
                                    c.wait_pair("tensor", fa_)
                            def fsim(eng, j=j, kb=kb, slot=slot, hp=hp, qof=qof, bsim=bsim):
                                return eng.matmul(
                                    ps[:, bsim, :],
                                    kbuf[j * 64:(j + 1) * 64, slot, kb // 16, (kb % 16) * 128:(kb % 16) * 128 + 128],
                                    q_sb[j * 64:(j + 1) * 64, hp, qof:qof + 512],
                                    start=True, stop=True, tile_position=(j * 64, 0))
                            lsim = mmop(fsim)
                            use_sc_sq = ((kb + j) % 2 == 0)
                            if j == 0:
                                c.wait("scalar", "mm", lsim)
                                c.wait_pair("scalar", war.get(("rt", j, kb % 2)))
                                trl = sop(lambda e, j=j, kb=kb, h=h, bsim=bsim: e.activation(
                                    rt[:, j, kb % 2, :].bitcast(BF16)[:, 0:512], ps[:, bsim, :],
                                    AF.Relu, bias=b_all[:, h:h + 1]))
                                sim_free[bsim] = ("sc", trl)
                                if use_sc_sq:
                                    c.wait_pair("scalar", war.get(("at", j, kb % 4)))
                                    tq = sop(lambda e, j=j, kb=kb, h=h: e.activation(
                                        at[:, j, kb % 4, :], rt[:, j, kb % 2, :].bitcast(BF16)[:, 0:512],
                                        AF.Square, scale=srn[:, kb, h:h + 1]), sync=False)
                                    war[("rt", j, kb % 2)] = ("sc", tq)
                                    sq_t[(kb, j)] = ("sc", tq)
                                else:
                                    c.wait("vector", "sc", trl)
                                    c.wait_pair("vector", war.get(("at", j, kb % 4)))
                                    tq = vop(lambda e, j=j, kb=kb, h=h: e.scalar_tensor_tensor(
                                        at[:, j, kb % 4, :], rt[:, j, kb % 2, :].bitcast(BF16)[:, 0:512],
                                        srn[:, kb, 16 + h:16 + h + 1],
                                        rt[:, j, kb % 2, :].bitcast(BF16)[:, 0:512], ALU.mult, ALU.mult), sync=False)
                                    war[("rt", j, kb % 2)] = ("vv", tq)
                                    sq_t[(kb, j)] = ("vv", tq)
                            else:
                                c.wait("vector", "mm", lsim)
                                c.wait_pair("vector", war.get(("rt", j, kb % 2)))
                                trl = vop(lambda e, j=j, kb=kb, h=h, bsim=bsim: e.tensor_scalar(
                                    rt[:, j, kb % 2, :].bitcast(BF16)[:, 0:512], ps[:, bsim, :],
                                    b_all[:, h:h + 1], 0.0, ALU.add, ALU.max), sync=False)
                                sim_free[bsim] = ("vv", trl)
                                if use_sc_sq:
                                    c.wait("scalar", "vv", trl)
                                    c.wait_pair("scalar", war.get(("at", j, kb % 4)))
                                    tq = sop(lambda e, j=j, kb=kb, h=h: e.activation(
                                        at[:, j, kb % 4, :], rt[:, j, kb % 2, :].bitcast(BF16)[:, 0:512],
                                        AF.Square, scale=srn[:, kb, h:h + 1]), sync=False)
                                    war[("rt", j, kb % 2)] = ("sc", tq)
                                    sq_t[(kb, j)] = ("sc", tq)
                                else:
                                    c.wait("vector", "vv", trl)
                                    c.wait_pair("vector", war.get(("at", j, kb % 4)))
                                    tq = vop(lambda e, j=j, kb=kb, h=h: e.scalar_tensor_tensor(
                                        at[:, j, kb % 4, :], rt[:, j, kb % 2, :].bitcast(BF16)[:, 0:512],
                                        srn[:, kb, 16 + h:16 + h + 1],
                                        rt[:, j, kb % 2, :].bitcast(BF16)[:, 0:512], ALU.mult, ALU.mult), sync=False)
                                    war[("rt", j, kb % 2)] = ("vv", tq)
                                    sq_t[(kb, j)] = ("vv", tq)
                    if it >= PIPE:
                        kb = it - PIPE
                        for j in range(2):
                            c.wait_pair("tensor", sq_t[(kb, j)])
                            if kb == 0:
                                c.wait_pair("tensor", sim_free.get(100 + j))
                            def fagg(eng, j=j, kb=kb, slot=slot):
                                return eng.matmul(ps[:, 6 + j, :], vbuf[:, slot, j, kb, :],
                                                  at[:, j, kb % 4, :], start=(kb == 0), stop=(kb == 31))
                            tagg = mmop(fagg)
                            if kb == 31:
                                agg_last[j] = tagg
                            war[("at", j, kb % 4)] = ("mm", tagg)
                ep_last = None
                c.wait_pair("tensor", sim_free.get(0))
                c.wait_pair("tensor", sim_free.get(3))
                for j in range(2):
                    h = 2 * hp + j
                    c.wait("scalar", "mm", agg_last[j])
                    c.wait("scalar", "vv", c.cnt["vv"])
                    c.wait_pair("scalar", war.get(("sqe", 0)))
                    c.wait_pair("scalar", war.get(("arwC",)))
                    tsq = sop(lambda e, j=j: e.activation(sqe[:, :], ps[:, 6 + j, :], AF.Square))
                    bsq, lasts = group(attn_ring, [(onescol[:, :], sqe[:, :])], out_rows=1,
                                       extra_wait=[("sc", tsq)])
                    war[("sqe", 0)] = ("mm", lasts)
                    c.wait_pair("scalar", war.get(("af", j)))
                    t = sop(lambda e, j=j: e.copy(af[:, j, :], ps[:, 6 + j, :]))
                    sim_free[100 + j] = ("sc", t)
                    c.wait("scalar", "mm", lasts)
                    t = sop(lambda e, bsq=bsq: e.activation(arwA[:, :], ps[0:1, bsq, :], AF.Sqrt))
                    attn_ring.done(bsq, "sc", t)
                    c.wait("vector", "sc", t)
                    vop(lambda e: e.tensor_scalar_max(arwB[:, :], arwA[:, :], float(L2_EPS)))
                    c.wait_pair("vector", war.get(("arwC",)))
                    tv = vop(lambda e: e.reciprocal(arwC[:, :], arwB[:, :]))
                    c.wait("scalar", "vv", tv)
                    t = sop(lambda e: e.activation(arwB[:, :], arwA[:, :], AF.Tanh))
                    c.wait("vector", "sc", t)
                    vop(lambda e: e.tensor_mul(arwC[:, :], arwC[:, :], arwB[:, :]))
                    tv = vop(lambda e, h=h: e.tensor_scalar_mul(arwC[:, :], arwC[:, :], ehws[0:1, h:h + 1]))
                    bb2, lastb2 = group(attn_ring, [(ones1f[:, :], arwC[:, :])], extra_wait=[("vv", tv)])
                    war[("arwC",)] = ("mm", lastb2)
                    c.wait("vector", "mm", lastb2)
                    vop(lambda e, j=j, bb2=bb2: e.tensor_mul(aft[:, :], af[:, j, :], ps[:, bb2, :]))
                    war[("af", j)] = ("vv", c.cnt["vv"])
                    ep_last = vop(lambda e, j=j, h=h, slot=slot: e.tensor_mul(
                        osc[:, h, :], aft[:, :], g2b[:, slot, j, :]))
                    kv_free[slot] = ("vv", ep_last)
                    attn_ring.done(bb2, "vv", ep_last)
                kv_free[slot] = ("vv", ep_last)
                war[("osc",)] = ("vv", ep_last)
            for bb_ in range(6):
                c.wait_pair("tensor", sim_free.get(bb_))
            c.wait_pair("tensor", sim_free.get(100))
            c.wait_pair("tensor", sim_free.get(101))
            bs = dn_ring.take()
            bs2 = dn_ring.take()
            l2 = None
            for mt2 in range(8):
                v, wt, key = wload(col_block(wo, mt2 * 256, 256), "w", wpv, ncol=256)
                c.wait_pair("tensor", wt)
                for mt, vsub in [(mt2 * 2 + s, v[:, :, s * 128:(s + 1) * 128]) for s in range(2)]:
                  b, last = group(up_ring, [(vsub[:, k, :], osc[:, k, :]) for k in range(KT)],
                                  extra_wait=[war.get(("osc",))] if war.get(("osc",)) else [])
                  if mt % 2 == 1:
                      wfree(key, last)
                  wo_last = last
                  slot = mt % 2
                  tl = xload(xres1[mt * 128:(mt + 1) * 128, qof:qof + 512], slot)
                  c.wait_pair("vector", tl)
                  c.wait("vector", "mm", last)
                  c.wait_pair("vector", war.get(("outh", slot)))
                  c.wait_pair("vector", war.get(("outhc", slot)))
                  free_tgt = vop(lambda e, b=b, slot=slot: e.tensor_add(
                      outh[:, slot, :], xbuf[:, slot, :], ps[:, b, :]))
                  war[("xbuf", slot)] = ("vv", free_tgt)
                  up_ring.done(b, "vv", free_tgt)
                  c.wait("scalar", "vv", free_tgt)
                  ts2 = cast_pair(slot, outh[:, slot, :], "statmm")
                  war[("outhc", slot)] = ("sc", ts2)
                  c.wait("tensor", "sc", ts2)
                  def f1(eng, slot=slot, mt=mt, bs=bs):
                      return eng.matmul(ps[0:1, bs, :], onescol[:, :], xb16[:, slot, :],
                                        start=(mt == 0), stop=(mt == KT - 1))
                  def f2(eng, slot=slot, mt=mt, bs2=bs2):
                      return eng.matmul(ps[0:1, bs2, :], onescol[:, :], sqx[:, slot, :],
                                        start=(mt == 0), stop=(mt == KT - 1))
                  mmop(f1)
                  tf2 = mmop(f2)
                  if mt == KT - 1:
                      l2 = tf2
                  war[("statmm", slot)] = ("mm", tf2)
                  c.wait("sync", "vv", free_tgt)
                  st = c.dma("sync", "oh%d" % slot, xres2[mt * 128:(mt + 1) * 128, qof:qof + 512], outh[:, slot, :])
                  war[("outh", slot)] = ("oh%d" % slot, st)
            c.wait("vector", "mm", l2)
            c.wait_pair("vector", war.get(("rowS",)))
            v1 = vop(lambda e, bs=bs: e.tensor_copy(rowS[:, :], ps[0:1, bs, :]))
            v2 = vop(lambda e, bs2=bs2: e.tensor_copy(rowS2[:, :], ps[0:1, bs2, :]))
            dn_ring.done(bs, "vv", v1)
            dn_ring.done(bs2, "vv", v2)
            c.wait("sync", "vv", v2)
            c.dma("sync", "sd", statd[2, 0, qb, :][None, :], rowS[:, :])
            war[("rowS",)] = ("sd", c.dma("sync", "sd", statd[2, 1, qb, :][None, :], rowS2[:, :]))

    def emit_poll(flag_t, psem_name, tag):
        loop = f"poll_{tag}_loop"
        done = f"poll_{tag}_done"
        sy = nc.sync
        (f_r,) = list(nc.alloc_registers(f"pf_{tag}", engines=[SP]))
        (w_r,) = list(nc.alloc_registers(f"pw_{tag}", engines=[SP]))
        (n_r,) = list(nc.alloc_registers(f"pn_{tag}", engines=[SP]))
        sy.reg_mov(w_r, 0)
        sy.reg_load(n_r, nonb[0:1, 0:1])
        nc.br(loop, engines=[SP])
        with nc.body(loop, valid_engines=[SP]):
            sy.dma_start(out=fsb[0:1, 0:1], in_=flag_t[bass.ds(opar, 1), 0:1, 0:1]).then_inc(c.sem[psem_name], 16)
            sy.reg_alu(w_r, w_r, 16, ALU.add)
            sy.wait_ge(c.sem[psem_name], w_r)
            sy.reg_load(f_r, fsb[0:1, 0:1])
            sy.reg_alu(f_r, f_r, n_r, ALU.bitwise_xor)
            sy.reg_alu(f_r, f_r, 0x7FFFFFFF, ALU.bitwise_and)
            nc.br_lt(f_r, 1, on_true=done, on_false=loop, engines=[SP])
        nc.switch_bb(done)

    with nc.Block() as block:
        emit_init()
        emit_stats(x0, 0)
        emit_ffn_call(x0, 0, xres1, 1)
        if stages >= 2:
            emit_attn_proj()
        else:
            for s_ in ["xb0", "xb1", "oh0", "oh1", "sd"]:
                c.wait("sync", s_, c.cnt[s_])
            c.dma("sync", "kvw", out_d, xres1)
            c.wait("sync", "kvw", c.cnt["kvw"])
        for s_ in ["xb0", "xb1", "oh0", "oh1", "sd", "kvw", "xr"]:
            c.wait("sync", s_, c.cnt[s_])
        c.run(block)

    if stages >= 2:
        emit_poll(FKV, "p1", "kv")
        with nc.Block() as block:
            emit_attn_core()
            if stages >= 3:
                for s_ in ["kv0", "kv1", "xr"]:
                    c.wait("sync", s_, c.cnt[s_])
                c.dma("sync", "kvw", FDN[bass.ds(par, 1), 0:1, 0:1], nonb[0:1, 0:1][None, :, :])
                emit_ffn_call(xres2, 2, out_d, None)
            else:
                c.wait("sync", "vv", c.cnt["vv"])
                c.dma("sync", "kvw", out_d, xres2)
            if debug:
                c.dma("sync", "kvw", dbg1, xres1)
                c.dma("sync", "kvw", dbg2, xres2)
            if stages < 3:
                c.dma("sync", "kvw", FDN[bass.ds(par, 1), 0:1, 0:1], nonb[0:1, 0:1][None, :, :])
            for s_ in ["xb0", "xb1", "oh0", "oh1", "sd", "kvw", "xr", "kv0", "kv1"]:
                c.wait("sync", s_, c.cnt[s_])
            c.run(block)
        emit_poll(FDN, "p2", "dn")
    return nc



def _prep(inputs):
    p = {}
    x = np.asarray(inputs["L0_attn_tokens"], np.float32).reshape(-1, DM)
    qg = np.asarray(inputs["L0_attn_Wqg"], np.float32).reshape(DM, 16, 192)
    kv = np.asarray(inputs["L0_attn_Wkv"], np.float32).reshape(DM, 16, 192)
    p["wq"] = np.ascontiguousarray(qg[:, :, :64].reshape(DM, 1024))
    p["wgt"] = np.ascontiguousarray(qg[:, :, 64:].reshape(DM, DM))
    p["wk"] = np.ascontiguousarray(kv[:, :, :64].reshape(DM, 1024))
    p["wv"] = np.ascontiguousarray(kv[:, :, 64:].reshape(DM, DM))
    p["wo"] = np.ascontiguousarray(np.asarray(inputs["L0_attn_Wo"], np.float32))
    p["ls_iaw"] = np.asarray(inputs["L0_attn_ls_iaw"], np.float32).reshape(1, 16)
    p["ls_hws"] = np.asarray(inputs["L0_attn_ls_hws"], np.float32).reshape(1, 16)
    wg = np.asarray(inputs["L0_ffn_Wg"], np.float32)
    p["wgp"] = np.ascontiguousarray(wg.reshape(2, 16, 128, 8).transpose(2, 1, 0, 3))
    p["bgr"] = np.ascontiguousarray(np.asarray(inputs["L0_ffn_bg"], np.float32).reshape(1, 16))
    p["wup"] = np.ascontiguousarray(np.asarray(inputs["L0_ffn_Wup"], np.float32))
    p["wsw"] = np.ascontiguousarray(np.asarray(inputs["L0_ffn_Wsw"], np.float32))
    p["wdn"] = np.ascontiguousarray(np.asarray(inputs["L0_ffn_Wdn"], np.float32))
    p["wsu"] = np.ascontiguousarray(np.asarray(inputs["L0_ffn_Wsu"], np.float32))
    p["wsa"] = np.ascontiguousarray(np.asarray(inputs["L0_ffn_Wsa"], np.float32))
    p["wsd"] = np.ascontiguousarray(np.asarray(inputs["L0_ffn_Wsd"], np.float32))
    p["bupp"] = np.ascontiguousarray(np.asarray(inputs["L0_ffn_bup"], np.float32).reshape(NEXP, 8, 128).transpose(2, 0, 1))
    p["bswp"] = np.ascontiguousarray(np.asarray(inputs["L0_ffn_bsw"], np.float32).reshape(NEXP, 8, 128).transpose(2, 0, 1))
    p["bdn2"] = np.ascontiguousarray(np.asarray(inputs["L0_ffn_bdn"], np.float32))
    p["bsup"] = np.ascontiguousarray(np.asarray(inputs["L0_ffn_bsu"], np.float32).reshape(16, 128).T)
    p["bsap"] = np.ascontiguousarray(np.asarray(inputs["L0_ffn_bsa"], np.float32).reshape(8, 128).T)
    p["bsdp"] = np.ascontiguousarray(np.asarray(inputs["L0_ffn_bsd"], np.float32).reshape(16, 128).T)
    sel = np.zeros((128, 2), np.float32)
    sel[:64, 0] = 1.0
    sel[64:, 1] = 1.0
    p["csel64"] = sel
    p["csel64T"] = np.ascontiguousarray(sel.T)
    p["cones1"] = np.ones((1, 128), np.float32)
    p["cones1f"] = np.ones((1, 128), np.float32)
    p["conescol"] = np.ones((128, 1), np.float32)
    p["cident"] = np.eye(128, dtype=np.float32)
    p["ceps"] = np.full((1, 1), 1e-5, np.float32)
    return x, p


def kernel(**inputs):
    x, p = _prep(inputs)
    nonce = np.array([[int(time.time_ns() // 1000) & 0x3FFFFFFF]], dtype=np.uint32)
    nc = build_nc(STAGES, DEBUG)
    in_maps = []
    for cid in range(8):
        m = dict(p)
        m["x0"] = np.ascontiguousarray(x[cid * TOK:(cid + 1) * TOK, :].T)
        m["nonce"] = nonce
        in_maps.append(m)
    res = run_bass_kernel_spmd(nc, in_maps, core_ids=list(range(8)))
    kernel._last_res = res
    outs = [np.asarray(res.results[cid]["out"]).T for cid in range(8)]
    return np.concatenate(outs, axis=0).reshape(4, 4096, DM).astype(np.float32)
